# version 01
# baseline (speedup 1.0000x reference)
import os
import numpy as np
from contextlib import ExitStack

import concourse.bacc as bacc
import concourse.bass as bass
import concourse.mybir as mybir
from concourse.bass_utils import run_bass_kernel_spmd

F32 = mybir.dt.float32
BF16 = mybir.dt.bfloat16
I16 = mybir.dt.int16
I32 = mybir.dt.int32
U32 = mybir.dt.uint32
AF = mybir.ActivationFunctionType
ALU = mybir.AluOpType
AX = mybir.AxisListType.X

NCORES = 8
TLOC = 1024
D = 2048
DT = D // 128
E = 16
CAP = 256
II = 1408
ITL = II // 128
SI = 2816
SIT = SI // 128
DST = 16
NSORT = E * CAP + 128
DUMPPOS = E * CAP
EPS = 1e-5

ENGINES = ("sync", "tensor", "vector", "scalar", "gpsimd")


class Rec:
    def __init__(self, nc, ctx, block):
        self.nc = nc
        self.block = block
        self.ops = {e: [] for e in ENGINES}
        self.semh = {}
        self.cnt = {}
        for name in ("IN", "WD", "GD", "PE", "VE", "SE", "GE", "CC", "ST"):
            self.semh[name] = ctx.enter_context(nc.semaphore(name))
            self.cnt[name] = 0
        self.barrier = dict(self.cnt)

    VSERIAL = True

    def op(self, eng, builder, sem=None, n=1):
        if sem is None and eng == "vector" and self.VSERIAL:
            sem = "VE"
        if sem is None:
            self.ops[eng].append(builder)
            return None
        h = self.semh[sem]
        self.cnt[sem] += n

        def f(EE, b=builder, h=h, n=n):
            b(EE).then_inc(h, n)
        self.ops[eng].append(f)
        tok = self.cnt[sem]
        if eng == "vector" and sem == "VE" and self.VSERIAL:
            self.ops[eng].append(lambda EE, h=h, t=tok: EE.wait_ge(h, t))
        return tok

    def wait(self, eng, sem, tok):
        if not tok:
            return
        h = self.semh[sem]
        self.ops[eng].append(lambda EE, h=h, t=tok: EE.wait_ge(h, t))

    def dma(self, eng, out, in_, sem=None):
        if sem is None:
            sem = "GD" if eng == "gpsimd" else ("IN" if eng == "sync" else "WD")
        return self.op(eng, lambda EE, o=out, i=in_: EE.dma_start(out=o, in_=i), sem, 16)

    def dma_nc(self, eng, out, in_, sem=None):
        if sem is None:
            sem = "GD" if eng == "gpsimd" else ("IN" if eng == "sync" else "WD")
        return self.op(eng, lambda EE, o=out, i=in_: EE.dma_start(
            out=o, in_=i, allow_slow_non_contiguous=True), sem, 16)

    def flush(self):
        for eng in ENGINES:
            ops = self.ops[eng]
            if not ops:
                continue

            def body(EE, ops=list(ops)):
                for f in ops:
                    f(EE)
            getattr(self.block, eng)(body)
            self.ops[eng] = []

    def barrier_now(self):
        toks = dict(self.cnt)
        for eng in ENGINES:
            for s, t in toks.items():
                if t > self.barrier.get(s, 0):
                    self.wait(eng, s, t)
        self.barrier = toks


def build_program(stage=6, debug=False):
    nc = bacc.Bacc("TRN2", target_bir_lowering=False, debug=False)

    dx = nc.dram_tensor("x", [TLOC, D], F32, kind="ExternalInput").ap()
    dcid = nc.dram_tensor("cid", [1, 1], F32, kind="ExternalInput").ap()
    dwsg = nc.dram_tensor("wsg", [2 * SIT, 128, DT, 128], F32, kind="ExternalInput").ap()
    dwsd = nc.dram_tensor("wsd", [SI, D], F32, kind="ExternalInput").ap()
    dwg = nc.dram_tensor("wg", [E, 2 * ITL, 128, DT, 128], F32, kind="ExternalInput").ap()
    dwd = nc.dram_tensor("wd", [E, II, D], F32, kind="ExternalInput").ap()
    dgwT = nc.dram_tensor("gwT", [D, E], F32, kind="ExternalInput").ap()
    dwdbcT = nc.dram_tensor("wdbcT", [D, 160], F32, kind="ExternalInput").ap()
    dwdtT = nc.dram_tensor("wdtT", [128, D], F32, kind="ExternalInput").ap()
    dbdlo = nc.dram_tensor("bdlo", [128, 1], F32, kind="ExternalInput").ap()
    dbbc = nc.dram_tensor("bbc", [32, 1], F32, kind="ExternalInput").ap()
    dbdt = nc.dram_tensor("bdt", [128, DT], F32, kind="ExternalInput").ap()
    dalog = nc.dram_tensor("alog", [128, DT, DST], F32, kind="ExternalInput").ap()
    ddvec = nc.dram_tensor("dvec", [128, DT], F32, kind="ExternalInput").ap()

    dout = nc.dram_tensor("out", [TLOC, D], F32, kind="ExternalOutput").ap()

    dzsp = nc.dram_tensor("zsp", [TLOC, D], F32).ap()
    dysort = nc.dram_tensor("ysort", [NSORT, D], F32).ap()
    didlin = nc.dram_tensor("idlin", [NSORT, 1], F32).ap()
    dglin = nc.dram_tensor("glin", [NSORT, 1], F32).ap()
    dposlin = nc.dram_tensor("poslin", [2 * TLOC, 1], F32).ap()
    dccin = nc.dram_tensor("ccin", [128, 512], F32).ap()
    dccout = nc.dram_tensor("ccout", [128 * NCORES, 512], F32).ap()

    dbg = {}
    if debug:
        dbg["z1"] = nc.dram_tensor("dbg_z1", [128, DT * TLOC], F32, kind="ExternalOutput").ap()
        dbg["route"] = nc.dram_tensor("dbg_route", [128, 64], F32, kind="ExternalOutput").ap()
        dbg["pos"] = nc.dram_tensor("dbg_pos", [128, 16], F32, kind="ExternalOutput").ap()
        dbg["idl"] = nc.dram_tensor("dbg_idl", [128, 33], F32, kind="ExternalOutput").ap()
        dbg["gl"] = nc.dram_tensor("dbg_gl", [128, 33], F32, kind="ExternalOutput").ap()
        dbg["snap"] = nc.dram_tensor("dbg_snap", [128, 80], F32, kind="ExternalOutput").ap()
        dbg["mam"] = nc.dram_tensor("dbg_mam", [128, 768], F32, kind="ExternalOutput").ap()
        dbg["bc"] = nc.dram_tensor("dbg_bc", [32, TLOC], F32, kind="ExternalOutput").ap()

    with ExitStack() as CTX:
        EN = CTX.enter_context
        block = EN(nc.Block())
        r = Rec(nc, CTX, block)

        rT = EN(nc.sbuf_tensor("rT", [128, DT, TLOC], F32))
        ident = EN(nc.sbuf_tensor("ident", [128, 128], F32))
        triUS = EN(nc.sbuf_tensor("triUS", [128, 128], F32))
        allone = EN(nc.sbuf_tensor("allone", [128, 128], F32))
        onecol = EN(nc.sbuf_tensor("onecol", [128, 1], F32))
        iota16f = EN(nc.sbuf_tensor("iota16f", [128, 16], F32))
        baseT = EN(nc.sbuf_tensor("baseT", [128, 16], F32))
        pid_f = EN(nc.sbuf_tensor("pid_f", [128, 1], F32))
        gwT_s = EN(nc.sbuf_tensor("gwT_s", [128, DT, E], F32))
        cid_s = EN(nc.sbuf_tensor("cid_s", [1, 1], F32))
        zero2k = EN(nc.sbuf_tensor("zero2k", [128, 64], F32))

        PS = [EN(nc.psum_tensor(f"ps{b}", [128, 512], F32)) for b in range(8)]
        ps_writer = [0] * 8
        ps_reader = [0] * 8

        toks = {}

        def v_ts(out, in0, s1, s2, op0, op1=None):
            if op1 is None:
                return r.op("vector", lambda EE, o=out, i=in0, a=s1, b=s2, p=op0:
                            EE.tensor_scalar(o, i, a, b, op0=p), "VE")
            return r.op("vector", lambda EE, o=out, i=in0, a=s1, b=s2, p=op0, q=op1:
                        EE.tensor_scalar(o, i, a, b, op0=p, op1=q), "VE")

        def v_tt(out, in0, in1, opk):
            return r.op("vector", lambda EE, o=out, a=in0, b=in1, p=opk:
                        EE.tensor_tensor(o, a, b, op=p), "VE")

        def v_stt(out, in0, scal, in1, op0, op1):
            return r.op("vector", lambda EE, o=out, a=in0, s=scal, b=in1, p=op0, q=op1:
                        EE.scalar_tensor_tensor(o, a, s, b, op0=p, op1=q), "VE")

        def v_copy(out, in_):
            return r.op("vector", lambda EE, o=out, i=in_: EE.tensor_copy(o, i), "VE")

        def v_red(out, in_, opk):
            r.op("vector", lambda EE: EE.nop())
            return r.op("vector", lambda EE, o=out, i=in_, p=opk:
                        EE.tensor_reduce(o, i, axis=AX, op=p), "VE")

        def v_memset(ap_, val):
            return r.op("vector", lambda EE, o=ap_, v=val: EE.memset(o, v), "VE")

        def v_recip(out, in_):
            r.op("vector", lambda EE: EE.nop())
            return r.op("vector", lambda EE, o=out, i=in_: EE.reciprocal(o, i), "VE")

        def v_scan(out, d0, d1, init, op0=ALU.mult, op1=ALU.add):
            r.op("vector", lambda EE: EE.nop())
            return r.op("vector", lambda EE, o=out, a=d0, b=d1, i=init, p=op0, q=op1:
                        EE.tensor_tensor_scan(o, a, b, i, op0=p, op1=q), "VE")

        def s_act(out, in_, func, bias=None, scale=None):
            def f(EE, o=out, i=in_, fn=func, b=bias, s=scale):
                kw = {}
                if b is not None:
                    kw["bias"] = b
                if s is not None:
                    kw["scale"] = s
                return EE.activation(o, i, fn, **kw)
            return r.op("scalar", f, "SE")

        def t_mm(out, lhsT, rhs, start, stop, sem=None):
            return r.op("tensor", lambda EE, o=out, l=lhsT, rr=rhs, s=start, t=stop:
                        EE.matmul(o, l, rr, start=s, stop=t), sem)

        def t_tp(out, in_, sem="PE"):
            return r.op("tensor", lambda EE, o=out, i=in_, d=ident[:]:
                        EE.transpose(o, i, d), sem)

        def mm_group(bank, width, lhsT_list, rhs_list):
            r.wait("tensor", "VE", ps_reader[bank])
            n = len(lhsT_list)
            pe = None
            for i, (lt, rh) in enumerate(zip(lhsT_list, rhs_list)):
                pe = t_mm(PS[bank][:, 0:width], lt, rh, i == 0, i == n - 1,
                          "PE" if i == n - 1 else None)
            ps_writer[bank] = pe
            return pe

        def evac(bank, dst, width, fn=None):
            r.wait("vector", "PE", ps_writer[bank])
            if fn is None:
                ve = v_copy(dst, PS[bank][:, 0:width])
            else:
                ve = fn(dst, PS[bank][:, 0:width])
            ps_reader[bank] = ve
            return ve

        scr_i = EN(nc.sbuf_tensor("scr_i", [128, 128], I32))
        p0ctx = ExitStack()
        xtm = p0ctx.enter_context(nc.sbuf_tensor("xtm", [128, 8, D], F32))

        idn_i = p0ctx.enter_context(nc.sbuf_tensor("idn_i", [128, 128], I32))
        r.op("gpsimd", lambda EE: EE.iota(scr_i[:, 0:16], pattern=[[1, 16]], base=0,
                                          channel_multiplier=0), "GE")
        r.op("gpsimd", lambda EE: EE.iota(scr_i[:, 16:17], pattern=[[0, 1]], base=0,
                                          channel_multiplier=1), "GE")
        r.op("gpsimd", lambda EE: EE.iota(idn_i[:], pattern=[[1, 128]], base=0,
                                          channel_multiplier=-1), "GE")
        ge3 = r.cnt["GE"]
        r.wait("vector", "GE", ge3)
        v_copy(iota16f[:], scr_i[:, 0:16])
        v_ts(baseT[:], iota16f[:], float(CAP), None, ALU.mult)
        v_copy(pid_f[:], scr_i[:, 16:17])
        v_memset(onecol[:], 1.0)
        v_memset(allone[:], 1.0)
        v_memset(zero2k[:], 0.0)
        v_ts(ident[:], idn_i[:], 0.0, None, ALU.is_equal)
        tv = v_ts(triUS[:], idn_i[:], 0.0, None, ALU.is_gt)

        in_x = r.dma("sync", xtm[:], dx.rearrange("(a p) d -> p a d", p=128))
        in_gw = r.dma("sync", gwT_s[:], dgwT.rearrange("(k p) e -> p k e", p=128))
        in_cid = r.dma("sync", cid_s[:], dcid)

        r.wait("tensor", "IN", in_x)
        r.wait("tensor", "VE", tv)
        for a in range(8):
            for k in range(DT):
                b = (a * DT + k) % 4
                r.wait("tensor", "VE", ps_reader[b])
                pe = t_tp(PS[b][:, 0:128], xtm[:, a, 128 * k:128 * (k + 1)])
                ps_writer[b] = pe
                evac(b, rT[:, k, 128 * a:128 * (a + 1)], 128)
        r.barrier_now()
        r.flush()
        p0ctx.close()

        def emit_ln(zT, meanB, rstdB, sc, make_z=True, sq=None):
            rows, rows2 = sc
            if sq is None:
                sq = zT
            for k in range(DT):
                v_tt(sq[:, k, :], rT[:, k, :], rT[:, k, :], ALU.mult)
            ve_sq = r.cnt["VE"]
            r.wait("tensor", "VE", ve_sq)
            for half in range(2):
                sl = slice(512 * half, 512 * (half + 1))
                m01, r01 = meanB[0:1, sl], rstdB[0:1, sl]
                for b, src, dstrow in ((0, rT, rows), (1, sq, rows2)):
                    r.wait("tensor", "VE", ps_reader[b])
                    for k in range(DT):
                        pe = t_mm(PS[b][0:1, 0:512], onecol[:], src[:, k, sl],
                                  k == 0, k == DT - 1, "PE" if k == DT - 1 else None)
                    ps_writer[b] = pe
                    r.wait("vector", "PE", ps_writer[b])
                    ve = v_copy(dstrow[:], PS[b][0:1, 0:512])
                    ps_reader[b] = ve
                v_ts(m01, rows[:], 1.0 / D, None, ALU.mult)
                v_ts(rows2[:], rows2[:], 1.0 / D, None, ALU.mult)
                v_tt(rows[:], m01, m01, ALU.mult)
                v_tt(rows2[:], rows2[:], rows[:], ALU.subtract)
                v_ts(rows2[:], rows2[:], EPS, None, ALU.add)
                ve = v_recip(rows[:], rows2[:])
                r.wait("scalar", "VE", ve)
                se = s_act(rows[:], rows[:], AF.Sqrt)
                r.wait("vector", "SE", se)
                v_tt(r01, rows[:], rows[:], ALU.mult)
                v_tt(r01, r01, rows2[:], ALU.mult)
                v_ts(r01, r01, -0.5, 1.5, ALU.mult, ALU.add)
                ve = v_tt(r01, rows[:], r01, ALU.mult)
            r.wait("gpsimd", "VE", ve)
            r.op("gpsimd", lambda EE, o=meanB: EE.partition_broadcast(o[:], o[0:1, :]), "GE")
            g2 = r.op("gpsimd", lambda EE, o=rstdB: EE.partition_broadcast(o[:], o[0:1, :]), "GE")
            r.wait("vector", "GE", g2)
            if make_z:
                for k in range(DT):
                    v_tt(zT[:, k, :], rT[:, k, :], meanB[:], ALU.subtract)
                    v_tt(zT[:, k, :], zT[:, k, :], rstdB[:], ALU.mult)

        def alloc_routing(EF, call):
            d = {}
            d["lg"] = EF(nc.sbuf_tensor(f"lg{call}", [128, 8, E], F32))
            d["mx8"] = EF(nc.sbuf_tensor(f"mx8{call}", [128, 8, 8], F32))
            d["mi8"] = EF(nc.sbuf_tensor(f"mi8{call}", [128, 8, 8], F32))
            d["wk"] = EF(nc.sbuf_tensor(f"wk{call}", [128, 8, 2], F32))
            d["colv3"] = EF(nc.sbuf_tensor(f"colv{call}", [128, 8, 12], F32))
            d["lgf"] = EF(nc.sbuf_tensor(f"lgf{call}", [128, 16], F32))
            d["mxf"] = EF(nc.sbuf_tensor(f"mxf{call}", [128, 8], F32))
            d["mif"] = EF(nc.sbuf_tensor(f"mif{call}", [128, 8], U32))
            d["ohA"] = EF(nc.sbuf_tensor(f"ohA{call}", [128, 16, E], F32))
            d["Rrk"] = EF(nc.sbuf_tensor(f"Rrk{call}", [128, 16, E], F32))
            d["posg"] = EF(nc.sbuf_tensor(f"posg{call}", [128, 16, 2], F32))
            d["posI"] = EF(nc.sbuf_tensor(f"posI{call}", [128, 1], U32))
            d["poscp"] = EF(nc.sbuf_tensor(f"poscp{call}", [128, 16], F32))
            d["valf"] = EF(nc.sbuf_tensor(f"valf{call}", [128, 1], F32))
            d["scr"] = EF(nc.sbuf_tensor(f"pscr{call}", [128, E], F32))
            d["sidx_all"] = EF(nc.sbuf_tensor(f"sidx{call}", [128, NSORT // 16], I16))
            d["idw8"] = EF(nc.sbuf_tensor(f"idw8{call}", [128, NSORT // 16], F32))
            d["gtile"] = EF(nc.sbuf_tensor(f"gtile{call}", [128, NSORT // 128], F32))
            d["posw"] = EF(nc.sbuf_tensor(f"posw{call}", [128, 128], I16))
            d["poswf"] = EF(nc.sbuf_tensor(f"poswf{call}", [128, 128], F32))
            if debug:
                d["snap"] = EF(nc.sbuf_tensor(f"snap{call}", [128, 80], F32))
            return d

        def emit_routing(zT, RA, call):
            lg = RA["lg"]
            mx8, mi8, wk = RA["mx8"], RA["mi8"], RA["wk"]
            colv3, lgf, mxf, mif = RA["colv3"], RA["lgf"], RA["mxf"], RA["mif"]
            ohA, Rrk, posg, posI = RA["ohA"], RA["Rrk"], RA["posg"], RA["posI"]
            poscp, valf, scr = RA["poscp"], RA["valf"], RA["scr"]
            sidx_all, idw8, gtile = RA["sidx_all"], RA["idw8"], RA["gtile"]
            posw, poswf = RA["posw"], RA["poswf"]
            snap = RA.get("snap")

            ve_z = r.cnt["VE"]
            r.wait("tensor", "VE", ve_z)
            for tt in range(8):
                b = tt % 2
                mm_group(b, E, [zT[:, k, 128 * tt:128 * (tt + 1)] for k in range(DT)],
                         [gwT_s[:, k, :] for k in range(DT)])
                evac(b, lg[:, tt, :], E)

            for tt in range(8):
                colv = colv3[:, tt, :]
                v_red(colv[:, 0:1], lg[:, tt, :], ALU.max)
                r.wait("vector", "VE", r.cnt["VE"])
                v_ts(lgf[:], lg[:, tt, :], colv[:, 0:1], None, ALU.is_equal)
                v_tt(scr[:], lgf[:], iota16f[:], ALU.mult)
                v_ts(lgf[:], lgf[:], -99.0, 99.0, ALU.mult, ALU.add)
                v_tt(scr[:], scr[:], lgf[:], ALU.add)
                if debug and tt == 0:
                    v_copy(snap[:, 32:48], scr[:])
                v_red(colv[:, 1:2], scr[:], ALU.min)
                r.wait("vector", "VE", r.cnt["VE"])
                v_ts(ohA[:, tt, :], iota16f[:], colv[:, 1:2], None, ALU.is_equal)
                v_stt(lgf[:], ohA[:, tt, :], -1e30, lg[:, tt, :], ALU.mult, ALU.add)
                if debug and tt == 0:
                    v_copy(snap[:, 0:16], ohA[:, 0, :])
                    v_copy(snap[:, 16:32], lgf[:])
                    v_copy(snap[:, 64:80], lg[:, 0, :])
                v_red(colv[:, 2:3], lgf[:], ALU.max)
                r.wait("vector", "VE", r.cnt["VE"])
                v_ts(scr[:], lgf[:], colv[:, 2:3], None, ALU.is_equal)
                v_tt(mx8[:, tt, 0:1], colv[:, 0:1], colv[:, 0:1], ALU.min)
                v_copy(mx8[:, tt, 1:2], colv[:, 2:3])
                v_copy(mi8[:, tt, 0:1], colv[:, 1:2])
                v_tt(scr[:], scr[:], iota16f[:], ALU.mult)
                r.wait("vector", "VE", r.cnt["VE"])
                v_ts(lgf[:], lgf[:], colv[:, 2:3], None, ALU.is_equal)
                v_ts(lgf[:], lgf[:], -99.0, 99.0, ALU.mult, ALU.add)
                v_tt(scr[:], scr[:], lgf[:], ALU.add)
                v_red(colv[:, 3:4], scr[:], ALU.min)
                r.wait("vector", "VE", r.cnt["VE"])
                v_ts(ohA[:, 8 + tt, :], iota16f[:], colv[:, 3:4], None, ALU.is_equal)
                v_copy(mi8[:, tt, 1:2], colv[:, 3:4])
                ve = v_ts(colv[:, 4:5], colv[:, 0:1], -1.0, None, ALU.mult)
                r.wait("scalar", "VE", ve)
                se = s_act(lg[:, tt, :], lg[:, tt, :], AF.Exp, bias=colv[:, 4:5])
                r.wait("vector", "SE", se)
                if debug and tt == 0:
                    v_copy(snap[:, 48:64], lg[:, 0, :])
                v_red(colv[:, 5:6], lg[:, tt, :], ALU.add)
                v_recip(colv[:, 6:7], colv[:, 5:6])
                v_copy(wk[:, tt, 0:1], colv[:, 6:7])
                ve = v_tt(colv[:, 7:8], colv[:, 2:3], colv[:, 4:5], ALU.add)
                r.wait("scalar", "VE", ve)
                se = s_act(colv[:, 8:9], colv[:, 7:8], AF.Exp)
                r.wait("vector", "SE", se)
                v_tt(wk[:, tt, 1:2], colv[:, 8:9], colv[:, 6:7], ALU.mult)
            ve_oh = r.cnt["VE"]

            r.wait("tensor", "VE", ve_oh)
            for c in range(16):
                b = 2 + (c % 2)
                lts = [allone[:]] * c + [triUS[:]]
                rhs = [ohA[:, cp, :] for cp in range(c)] + [ohA[:, c, :]]
                mm_group(b, E, lts, rhs)
                evac(b, Rrk[:, c, :], E)

            for c in range(16):
                tt, sl = c % 8, c // 8
                colv = colv3[:, tt, :]
                v_tt(scr[:], Rrk[:, c, :], ohA[:, c, :], ALU.mult)
                v_red(colv[:, 6:7], scr[:], ALU.add)
                v_tt(scr[:], baseT[:], ohA[:, c, :], ALU.mult)
                v_red(colv[:, 7:8], scr[:], ALU.add)
                v_ts(colv[:, 8:9], colv[:, 6:7], float(CAP), None, ALU.is_lt)
                v_tt(colv[:, 9:10], colv[:, 6:7], colv[:, 7:8], ALU.add)
                v_tt(colv[:, 9:10], colv[:, 9:10], colv[:, 8:9], ALU.mult)
                v_ts(colv[:, 10:11], colv[:, 8:9], -float(DUMPPOS), float(DUMPPOS),
                     ALU.mult, ALU.add)
                v_tt(posg[:, c, 0:1], colv[:, 9:10], colv[:, 10:11], ALU.add)
                v_tt(posg[:, c, 1:2], wk[:, tt, sl:sl + 1], colv[:, 8:9], ALU.mult)
            ve_pos = r.cnt["VE"]

            r.wait("gpsimd", "VE", ve_pos)
            r.dma("gpsimd", didlin.rearrange("(p a) o -> p (a o)", p=128),
                  zero2k[:, 0:NSORT // 128])
            r.dma("gpsimd", dglin.rearrange("(p a) o -> p (a o)", p=128),
                  zero2k[:, 0:NSORT // 128])
            v_copy(poscp[:], posg[:, :, 0:1].squeeze(2))
            r.wait("gpsimd", "VE", r.cnt["VE"])
            r.dma_nc("gpsimd", dposlin.rearrange("(c p) o -> p (c o)", p=128),
                  poscp[:])
            gd_pre = r.cnt["GD"]
            r.wait("gpsimd", "GD", gd_pre)
            for c in range(16):
                tt = c % 8
                v_ts(valf[:], pid_f[:], 1.0, float(128 * tt + 1), ALU.mult, ALU.add)
                ve = v_copy(posI[:], posg[:, c, 0:1])
                r.wait("gpsimd", "VE", ve)
                r.op("gpsimd", lambda EE: EE.indirect_dma_start(
                    out=didlin, out_offset=bass.IndirectOffsetOnAxis(ap=posI[:], axis=0),
                    in_=valf[:], in_offset=None), "GD", 16)
                r.op("gpsimd", lambda EE, c=c: EE.indirect_dma_start(
                    out=dglin, out_offset=bass.IndirectOffsetOnAxis(ap=posI[:], axis=0),
                    in_=posg[:, c, 1:2], in_offset=None), "GD", 16)
                r.wait("vector", "GD", r.cnt["GD"])
            gd_scat = r.cnt["GD"]

            r.wait("gpsimd", "GD", gd_scat)
            for g in range(8):
                r.dma_nc("gpsimd", idw8[16 * g:16 * (g + 1), :],
                      didlin.rearrange("(f p) o -> p (f o)", p=16))
            r.dma_nc("gpsimd", gtile[:], dglin.rearrange("(f p) o -> p (f o)", p=128))
            for g in range(8):
                r.dma_nc("gpsimd", poswf[16 * g:16 * (g + 1), :],
                      dposlin.rearrange("(f p) o -> p (f o)", p=16))
            gd_rb = r.cnt["GD"]
            r.wait("vector", "GD", gd_rb)
            v_ts(idw8[:], idw8[:], -1.0, 0.0, ALU.add, ALU.max)
            v_copy(sidx_all[:], idw8[:])
            ve = v_copy(posw[:], poswf[:])
            toks[("route", call)] = r.cnt["VE"]

            if debug and stage <= 2:
                rt = RA["mx8"]
                for tt in range(8):
                    v_copy(rt[:, tt, 2:3], mi8[:, tt, 0:1])
                    v_copy(rt[:, tt, 3:4], mi8[:, tt, 1:2])
                    v_copy(rt[:, tt, 4:5], wk[:, tt, 0:1])
                    v_copy(rt[:, tt, 5:6], wk[:, tt, 1:2])
                r.wait("gpsimd", "VE", r.cnt["VE"])
                r.dma("gpsimd", dbg["route"], rt.ap().rearrange("p a b -> p (a b)"), "ST")
                r.dma("gpsimd", dbg["pos"], poscp[:], "ST")
                r.dma("gpsimd", dbg["idl"], idw8[:, 0:33], "ST")
                r.dma("gpsimd", dbg["gl"], gtile[:], "ST")
                r.dma("gpsimd", dbg["snap"], snap[:], "ST")

            return

        def emit_spill_z(zT, EF, call):
            ztm = EF(nc.sbuf_tensor(f"ztm{call}", [128, D], F32))
            ve_z = r.cnt["VE"]
            r.wait("tensor", "VE", ve_z)
            for a in range(8):
                for k in range(DT):
                    b = 4 + (k % 2)
                    r.wait("tensor", "VE", ps_reader[b])
                    pe = t_tp(PS[b][:, 0:128], zT[:, k, 128 * a:128 * (a + 1)])
                    ps_writer[b] = pe
                    ve = evac(b, ztm[:, 128 * k:128 * (k + 1)], 128)
                r.wait("gpsimd", "VE", ve)
                gd = r.dma("gpsimd", dzsp[128 * a:128 * (a + 1), :], ztm[:])
                r.wait("vector", "GD", gd)
            toks[("zspill", call)] = r.cnt["GD"]

        def emit_shared(zT, EF, call):
            TCH = 256
            wbuf = [EF(nc.sbuf_tensor(f"wsgb{call}_0", [128, DT, 128], F32))]
            wdbuf = [EF(nc.sbuf_tensor(f"wsdb{call}_0", [128, D], F32))]
            hsb = EF(nc.sbuf_tensor(f"hs{call}", [128, SIT, TCH], F32))
            sg = EF(nc.sbuf_tensor(f"sgs{call}", [128, TCH], F32))
            wfree = [0, 0]
            wdfree = [0, 0]

            ve_z = r.cnt["VE"]
            r.wait("tensor", "VE", ve_z)
            for tc in range(TLOC // TCH):
                sl = slice(TCH * tc, TCH * (tc + 1))
                if tc > 0:
                    r.wait("vector", "PE", r.cnt["PE"])
                for it in range(SIT):
                    for gu in range(2):
                        jt = it + SIT * gu
                        buf = 0
                        r.wait("sync", "PE", wfree[buf])
                        wtok = r.dma("sync", wbuf[buf][:], dwsg[jt], "WD")
                        bank = gu
                        r.wait("tensor", "WD", wtok)
                        pe = mm_group(bank, TCH, [wbuf[buf][:, k, :] for k in range(DT)],
                                      [zT[:, k, sl] for k in range(DT)])
                        wfree[buf] = pe
                    r.wait("scalar", "PE", ps_writer[1])
                    se = s_act(sg[:], PS[0][:, 0:TCH], AF.Silu)
                    r.wait("vector", "SE", se)
                    ve = v_tt(hsb[:, it, :], sg[:], PS[1][:, 0:TCH], ALU.mult)
                    ps_reader[0] = ve
                    ps_reader[1] = ve
                ve_hs = r.cnt["VE"]
                r.wait("tensor", "VE", ve_hs)
                for dg in range(2):
                    for it in range(SIT):
                        buf = 0
                        r.wait("sync", "PE", wdfree[buf])
                        wtok = r.dma("sync", wdbuf[buf][:], dwsd[128 * it:128 * (it + 1), :], "WD")
                        r.wait("tensor", "WD", wtok)
                        for d8 in range(8):
                            k = 8 * dg + d8
                            bank = d8
                            if it == 0:
                                r.wait("tensor", "VE", ps_reader[bank])
                            pe = t_mm(PS[bank][:, 0:TCH], wdbuf[buf][:, 128 * k:128 * (k + 1)],
                                      hsb[:, it, :], it == 0, it == SIT - 1,
                                      "PE" if (d8 == 7 or it == SIT - 1) else None)
                            if it == SIT - 1:
                                ps_writer[bank] = pe
                        wdfree[buf] = pe
                    for d8 in range(8):
                        k = 8 * dg + d8
                        bank = d8
                        r.wait("vector", "PE", ps_writer[bank])
                        ve = v_stt(rT[:, k, sl], PS[bank][:, 0:TCH], 0.5, rT[:, k, sl],
                                   ALU.mult, ALU.add)
                        ps_reader[bank] = ve

        def emit_experts(RA, EF, call):
            sidx_all, gtile, posw = RA["sidx_all"], RA["gtile"], RA["posw"]
            zst = [EF(nc.sbuf_tensor(f"zst{call}_{i}", [128, 2, D], F32)) for i in range(2)]
            zeT = EF(nc.sbuf_tensor(f"zeT{call}", [128, DT, CAP], F32))
            wgb = [EF(nc.sbuf_tensor(f"wgb{call}_{i}", [128, DT, 128], F32)) for i in range(2)]
            hT = EF(nc.sbuf_tensor(f"hT{call}", [128, ITL, CAP], F32))
            wdb = [EF(nc.sbuf_tensor(f"wdb{call}_{i}", [128, D], F32)) for i in range(2)]
            ye = EF(nc.sbuf_tensor(f"ye{call}", [128, D], F32))
            sg = EF(nc.sbuf_tensor(f"sge{call}", [128, CAP], F32))
            zst_free = [0, 0]
            wg_free = [0, 0]
            wd_free = [0, 0]
            ye_free = 0

            r.wait("gpsimd", "VE", toks[("route", call)])
            r.wait("gpsimd", "GD", toks[("zspill", call)])
            for e in range(E):
                buf = e % 2
                if e > 0:
                    r.wait("vector", "PE", r.cnt["PE"])
                r.wait("gpsimd", "PE", zst_free[buf])
                gd = r.op("gpsimd", lambda EE, b=buf, e=e: EE.dma_gather(
                    zst[b][:], dzsp, sidx_all[:, 16 * e:16 * (e + 1)],
                    num_idxs=CAP, num_idxs_reg=CAP, elem_size=D, transpose=False),
                    "GD", 16)
                r.wait("tensor", "GD", gd)
                for c in range(2):
                    for k in range(DT):
                        b = 2 + (k % 2)
                        r.wait("tensor", "VE", ps_reader[b])
                        pe = t_tp(PS[b][:, 0:128], zst[buf][:, c, 128 * k:128 * (k + 1)])
                        ps_writer[b] = pe
                        evac(b, zeT[:, k, 128 * c:128 * (c + 1)], 128)
                zst_free[buf] = pe
                for it in range(ITL):
                    for gu in range(2):
                        jt = it + ITL * gu
                        wbuf_i = jt % 2
                        r.wait("sync", "PE", wg_free[wbuf_i])
                        wtok = r.dma("sync", wgb[wbuf_i][:], dwg[e, jt], "WD")
                        bank = 4 + gu
                        r.wait("tensor", "WD", wtok)
                        pe = mm_group(bank, CAP, [wgb[wbuf_i][:, k, :] for k in range(DT)],
                                      [zeT[:, k, :] for k in range(DT)])
                        wg_free[wbuf_i] = pe
                    r.wait("scalar", "PE", ps_writer[5])
                    se = s_act(sg[:], PS[4][:, 0:CAP], AF.Silu)
                    r.wait("vector", "SE", se)
                    ve = v_tt(hT[:, it, :], sg[:], PS[5][:, 0:CAP], ALU.mult)
                    ps_reader[4] = ve
                    ps_reader[5] = ve
                ve_h = r.cnt["VE"]
                r.wait("tensor", "VE", ve_h)
                for it in range(ITL):
                    buf_d = it % 2
                    r.wait("sync", "PE", wd_free[buf_d])
                    wtok = r.dma("sync", wdb[buf_d][:], dwd[e, 128 * it:128 * (it + 1), :], "WD")
                    r.wait("tensor", "WD", wtok)
                    for tokt in range(2):
                        for dc in range(4):
                            bank = 4 * tokt + dc
                            if it == 0:
                                r.wait("tensor", "VE", ps_reader[bank])
                            pe = t_mm(PS[bank][:, 0:512],
                                      hT[:, it, 128 * tokt:128 * (tokt + 1)],
                                      wdb[buf_d][:, 512 * dc:512 * (dc + 1)],
                                      it == 0, it == ITL - 1,
                                      "PE" if (tokt == 1 and dc == 3) or it == ITL - 1 else None)
                            if it == ITL - 1:
                                ps_writer[bank] = pe
                    wd_free[buf_d] = pe
                for tokt in range(2):
                    r.wait("vector", "GD", ye_free)
                    for dc in range(4):
                        bank = 4 * tokt + dc
                        r.wait("vector", "PE", ps_writer[bank])
                        ve = v_ts(ye[:, 512 * dc:512 * (dc + 1)], PS[bank][:, 0:512],
                                  gtile[:, 2 * e + tokt:2 * e + tokt + 1], None, ALU.mult)
                        ps_reader[bank] = ve
                    r.wait("gpsimd", "VE", ve)
                    rowbase = CAP * e + 128 * tokt
                    ye_free = r.dma("gpsimd", dysort[rowbase:rowbase + 128, :], ye[:])
            toks[("experts", call)] = r.cnt["GD"]

        def emit_return(RA, EF, call):
            posw = RA["posw"]
            Y0 = EF(nc.sbuf_tensor(f"Y0{call}", [128, 2, D], F32))
            Y1 = EF(nc.sbuf_tensor(f"Y1{call}", [128, 2, D], F32))
            for cch in range(32):
                gd0 = r.dma("gpsimd", dysort[DUMPPOS:DUMPPOS + 128, 64 * cch:64 * (cch + 1)],
                            zero2k[:])
            r.wait("gpsimd", "GD", toks[("experts", call)])
            pe_last = 0
            for j in range(4):
                r.wait("gpsimd", "PE", pe_last)
                g0 = r.op("gpsimd", lambda EE, j=j: EE.dma_gather(
                    Y0[:], dysort, posw[:, 16 * j:16 * (j + 1)],
                    num_idxs=CAP, num_idxs_reg=CAP, elem_size=D, transpose=False),
                    "GD", 16)
                g1 = r.op("gpsimd", lambda EE, j=j: EE.dma_gather(
                    Y1[:], dysort, posw[:, 64 + 16 * j:64 + 16 * (j + 1)],
                    num_idxs=CAP, num_idxs_reg=CAP, elem_size=D, transpose=False),
                    "GD", 16)
                r.wait("vector", "GD", g1)
                ve = v_tt(Y0[:], Y0[:], Y1[:], ALU.add)
                r.wait("tensor", "VE", ve)
                for c in range(2):
                    for k in range(DT):
                        b = 6 + (k % 2)
                        r.wait("tensor", "VE", ps_reader[b])
                        pe = t_tp(PS[b][:, 0:128], Y0[:, c, 128 * k:128 * (k + 1)])
                        ps_writer[b] = pe
                        r.wait("vector", "PE", pe)
                        tsl = slice(256 * j + 128 * c, 256 * j + 128 * (c + 1))
                        ve = v_stt(rT[:, k, tsl], PS[b][:, 0:128], 0.5, rT[:, k, tsl],
                                   ALU.mult, ALU.add)
                        ps_reader[b] = ve
                pe_last = pe

        def emit_ffn(call):
            with ExitStack() as FX:
                EF = FX.enter_context
                RA = alloc_routing(EF, call) if stage >= 2 else None
                with ExitStack() as AX2:
                    EA = AX2.enter_context
                    zT = EA(nc.sbuf_tensor(f"zT{call}", [128, DT, TLOC], F32))
                    meanB = EA(nc.sbuf_tensor(f"meanB{call}", [128, TLOC], F32))
                    rstdB = EA(nc.sbuf_tensor(f"rstdB{call}", [128, TLOC], F32))
                    sc = tuple(EA(nc.sbuf_tensor(f"lnrow{call}_{i}", [1, 512], F32))
                               for i in range(2))
                    emit_ln(zT, meanB, rstdB, sc)
                    if debug and stage <= 1 and call == 0:
                        ve = r.cnt["VE"]
                        r.wait("gpsimd", "VE", ve)
                        r.dma("gpsimd", dbg["z1"], zT.ap().rearrange("p a b -> p (a b)"), "ST")
                        r.barrier_now()
                        r.flush()
                        return
                    if stage >= 2:
                        emit_routing(zT, RA, call)
                    if stage >= 3:
                        emit_spill_z(zT, EA, call)
                        emit_shared(zT, EA, call)
                    r.barrier_now()
                    r.flush()
                if stage >= 4:
                    with ExitStack() as BX2:
                        emit_experts(RA, BX2.enter_context, call)
                        r.barrier_now()
                        r.flush()
                    with ExitStack() as CX2:
                        emit_return(RA, CX2.enter_context, call)
                    r.barrier_now()
                    r.flush()

        def emit_mamba():
            with ExitStack() as MX:
                EM = MX.enter_context
                deltaT = EM(nc.sbuf_tensor("deltaT", [128, DT, TLOC], F32))
                z2h = EM(nc.sbuf_tensor("z2h", [128, DT, 512], F32))
                bcT = EM(nc.sbuf_tensor("bcT", [32, TLOC], F32))
                aneg = EM(nc.sbuf_tensor("aneg", [128, DT, DST], F32))
                dv_s = EM(nc.sbuf_tensor("dv_s", [128, DT], F32))
                sd = EM(nc.sbuf_tensor("sd", [128, DT], F32))
                selb = EM(nc.sbuf_tensor("selb", [32, 2, 128], F32))
                sel_i = EM(nc.sbuf_tensor("sel_i", [32, 128], I32))
                brep = EM(nc.sbuf_tensor("brep", [128, 512], F32))
                crep = EM(nc.sbuf_tensor("crep", [128, 512], F32))
                da = EM(nc.sbuf_tensor("da", [128, 512], F32))
                bx = EM(nc.sbuf_tensor("bx", [128, 512], F32))
                hsc = EM(nc.sbuf_tensor("hsc", [128, 512], F32))
                hend = EM(nc.sbuf_tensor("hend", [128, 256], F32))
                PPack = EM(nc.sbuf_tensor("PPack", [128, 256], F32))
                hinit = EM(nc.sbuf_tensor("hinit", [128, 256], F32))
                statev = EM(nc.sbuf_tensor("statev", [128, 256], F32))
                rankB = EM(nc.sbuf_tensor("rankB", [128, 1], F32))
                mcol = EM(nc.sbuf_tensor("mcol", [128, 1], F32))

                v_memset(sd[:], 0.0)
                v_memset(hinit[:], 0.0)
                v_memset(statev[:], 0.0)
                r.op("gpsimd", lambda EE: EE.iota(sel_i[:], pattern=[[0, 128]], base=0,
                                                  channel_multiplier=1), "GE")
                ge = r.cnt["GE"]
                r.wait("gpsimd", "IN", in_cid)
                gr = r.op("gpsimd", lambda EE: EE.partition_broadcast(rankB[:], cid_s[0:1, :]), "GE")

                def build_sel(s):
                    r.wait("vector", "GE", ge)
                    v_ts(selb[:, 0, :], sel_i[:], float(s), None, ALU.is_equal)
                    return v_ts(selb[:, 1, :], sel_i[:], float(16 + s), None, ALU.is_equal)

                with ExitStack() as WX:
                    EW = WX.enter_context
                    wdbcT_s = EW(nc.sbuf_tensor("wdbcT_s", [128, DT, 160], F32))
                    wdtT_b = EW(nc.sbuf_tensor("wdtT_b", [128, 128], F32))
                    bdlo_s = EW(nc.sbuf_tensor("bdlo_s", [128, 1], F32))
                    bbc_s = EW(nc.sbuf_tensor("bbc_s", [32, 1], F32))
                    bdt_s = EW(nc.sbuf_tensor("bdt_s", [128, DT], F32))
                    meanB = EW(nc.sbuf_tensor("meanB_m", [128, TLOC], F32))
                    rstdB = EW(nc.sbuf_tensor("rstdB_m", [128, TLOC], F32))
                    sc = tuple(EW(nc.sbuf_tensor(f"lnrow_m_{i}", [1, 512], F32))
                               for i in range(2))
                    dtmp = brep
                    dtmp2 = bx

                    l1 = r.dma("sync", wdbcT_s[:], dwdbcT.rearrange("(k p) c -> p k c", p=128))
                    l3 = r.dma("sync", bdlo_s[:], dbdlo)
                    l4 = r.dma("sync", bbc_s[:], dbbc)
                    l5 = r.dma("sync", bdt_s[:], dbdt)
                    l6 = r.dma("sync", aneg[:], dalog)
                    l7 = r.dma("sync", dv_s[:], ddvec)
                    r.wait("scalar", "IN", l7)
                    se = s_act(aneg[:], aneg[:], AF.Exp)
                    r.wait("vector", "SE", se)
                    v_ts(aneg[:], aneg[:], -1.0, None, ALU.mult)
                    r.wait("vector", "IN", l7)
                    emit_ln(None, meanB, rstdB, sc, make_z=False, sq=deltaT)

                    wdt_free = 0
                    for h in range(2):
                        S = slice(512 * h, 512 * (h + 1))
                        for k in range(DT):
                            v_tt(z2h[:, k, :], rT[:, k, S], meanB[:, S], ALU.subtract)
                            v_tt(z2h[:, k, :], z2h[:, k, :], rstdB[:, S], ALU.mult)
                        ve_z2 = r.cnt["VE"]
                        r.wait("tensor", "VE", ve_z2)
                        mm_group(0, 512, [wdbcT_s[:, k, 0:128] for k in range(DT)],
                                 [z2h[:, k, :] for k in range(DT)])
                        def _dlo(dst, ps):
                            return v_ts(dst, ps, bdlo_s[:], None, ALU.add)
                        evac(0, dtmp[:], 512, _dlo)
                        r.wait("tensor", "VE", ps_reader[1])
                        for k in range(DT):
                            pe = t_mm(PS[1][0:32, 0:512], wdbcT_s[:, k, 128:160],
                                      z2h[:, k, :], k == 0, k == DT - 1,
                                      "PE" if k == DT - 1 else None)
                        ps_writer[1] = pe
                        r.wait("vector", "PE", ps_writer[1])
                        ve = r.op("vector", lambda EE, S=S: EE.tensor_scalar(
                            bcT[:, S], PS[1][0:32, 0:512], bbc_s[:], None, op0=ALU.add), "VE")
                        ps_reader[1] = ve
                        ve_dlo = r.cnt["VE"]
                        r.wait("tensor", "VE", ve_dlo)
                        for k in range(DT):
                            bank = 2 + (k % 2)
                            r.wait("sync", "PE", wdt_free)
                            wtok = r.dma("sync", wdtT_b[:], dwdtT[:, 128 * k:128 * (k + 1)], "WD")
                            r.wait("tensor", "WD", wtok)
                            r.wait("tensor", "VE", ps_reader[bank])
                            pe = t_mm(PS[bank][:, 0:512], wdtT_b[:],
                                      dtmp[:], True, True, "PE")
                            wdt_free = pe
                            ps_writer[bank] = pe
                            r.wait("vector", "PE", pe)
                            ve = v_ts(dtmp2[:], PS[bank][:, 0:512], bdt_s[:, k:k + 1], None, ALU.add)
                            ps_reader[bank] = ve
                            v_ts(hsc[:], dtmp2[:], -1.0, None, ALU.mult)
                            ve = v_tt(hsc[:], dtmp2[:], hsc[:], ALU.min)
                            r.wait("scalar", "VE", ve)
                            se = s_act(da[:], hsc[:], AF.Exp)
                            se = s_act(da[:], da[:], AF.Ln, bias=onecol[:])
                            r.wait("vector", "SE", se)
                            v_ts(dtmp2[:], dtmp2[:], 0.0, None, ALU.max)
                            v_tt(deltaT[:, k, S], dtmp2[:], da[:], ALU.add)
                            v_stt(rT[:, k, S], z2h[:, k, :], dv_s[:, k:k + 1], rT[:, k, S],
                                  ALU.mult, ALU.add)
                            v_red(mcol[:], deltaT[:, k, S], ALU.add)
                            v_tt(sd[:, k:k + 1], sd[:, k:k + 1], mcol[:], ALU.add)
                        ve_pre = r.cnt["VE"]
                        r.wait("tensor", "VE", ve_pre)
                        for s in range(DST):
                            ve_sel = build_sel(s)
                            r.wait("tensor", "VE", ve_sel)
                            for which, dst in ((0, brep), (1, crep)):
                                bank = 4 + which
                                r.wait("tensor", "VE", ps_reader[bank])
                                pe = t_mm(PS[bank][:, 0:512], selb[:, which, :], bcT[:, S],
                                          True, True, "PE")
                                ps_writer[bank] = pe
                                evac(bank, dst[:], 512)
                            for k in range(DT):
                                col = k * 16 + s
                                r.wait("scalar", "VE", r.cnt["VE"])
                                se = s_act(da[:], deltaT[:, k, S], AF.Exp,
                                           scale=aneg[:, k, s:s + 1])
                                r.wait("vector", "SE", se)
                                v_tt(bx[:], deltaT[:, k, S], z2h[:, k, :], ALU.mult)
                                v_tt(bx[:], bx[:], brep[:], ALU.mult)
                                if h == 0:
                                    v_scan(hsc[:], da[:], bx[:], 0.0)
                                else:
                                    v_scan(hsc[:], da[:], bx[:], hend[:, col:col + 1])
                                v_copy(hend[:, col:col + 1], hsc[:, 511:512])
                                v_tt(bx[:], hsc[:], crep[:], ALU.mult)
                                ve = v_tt(rT[:, k, S], rT[:, k, S], bx[:], ALU.add)
                            r.wait("scalar", "VE", r.cnt["VE"])
                    r.barrier_now()
                    if debug and stage == 5:
                        r.dma("gpsimd", dbg["z1"], deltaT.ap().rearrange("p a b -> p (a b)"), "ST")
                        r.dma("gpsimd", dbg["bc"], bcT[:], "ST")
                        r.dma("gpsimd", dbg["mam"][:, 0:256], hend[:], "ST")
                        r.barrier_now()
                    r.flush()

                with ExitStack() as CX:
                    EC = CX.enter_context
                    crepF = EC(nc.sbuf_tensor("crepF", [128, TLOC], F32))
                    ctmp = EC(nc.sbuf_tensor("ctmp", [128, TLOC], F32))
                    gath1 = EC(nc.sbuf_tensor("gath1", [128, 512], F32))

                    for k in range(DT):
                        r.wait("scalar", "VE", r.cnt["VE"])
                        se = s_act(PPack[:, 16 * k:16 * (k + 1)], aneg[:, k, :], AF.Exp,
                                   scale=sd[:, k:k + 1])
                    r.wait("gpsimd", "SE", se)
                    r.wait("gpsimd", "VE", r.cnt["VE"])
                    r.dma("gpsimd", dccin[:, 0:256], PPack[:])
                    gd = r.dma("gpsimd", dccin[:, 256:512], hend[:])
                    r.wait("gpsimd", "GD", gd)
                    cc = r.op("gpsimd", lambda EE: EE.collective_compute(
                        "AllGather", ALU.bypass, replica_groups=[list(range(NCORES))],
                        ins=[dccin.opt()], outs=[dccout.opt()]), "CC")
                    r.wait("gpsimd", "CC", cc)
                    r.wait("vector", "GE", gr)
                    r.wait("vector", "CC", cc)
                    for j in range(NCORES):
                        gd = r.dma("gpsimd", gath1[:], dccout[128 * j:128 * (j + 1), :])
                        r.wait("vector", "GD", gd)
                        v_ts(mcol[:], rankB[:], float(j), None, ALU.is_equal)
                        r.wait("vector", "VE", r.cnt["VE"])
                        v_stt(hinit[:], statev[:], mcol[:], hinit[:], ALU.mult, ALU.add)
                        v_tt(statev[:], gath1[:, 0:256], statev[:], ALU.mult)
                        ve = v_tt(statev[:], statev[:], gath1[:, 256:512], ALU.add)
                        r.wait("gpsimd", "VE", ve)
                    if debug and stage == 5:
                        r.wait("gpsimd", "VE", r.cnt["VE"])
                        r.dma("gpsimd", dbg["mam"][:, 256:512], PPack[:], "ST")
                        r.dma("gpsimd", dbg["mam"][:, 512:768], hinit[:], "ST")
                    v_memset(ctmp[:], 1.0)
                    for k in range(DT):
                        v_scan(deltaT[:, k, :], ctmp[:], deltaT[:, k, :], 0.0)
                    ve_cs = r.cnt["VE"]
                    r.wait("tensor", "VE", ve_cs)
                    for s in range(DST):
                        ve_sel = build_sel(s)
                        r.wait("tensor", "VE", ve_sel)
                        for half in range(2):
                            bank = 6 + half
                            r.wait("tensor", "VE", ps_reader[bank])
                            pe = t_mm(PS[bank][:, 0:512], selb[:, 1, :],
                                      bcT[:, 512 * half:512 * (half + 1)], True, True, "PE")
                            ps_writer[bank] = pe
                            evac(bank, crepF[:, 512 * half:512 * (half + 1)], 512)
                        for k in range(DT):
                            r.wait("scalar", "VE", r.cnt["VE"])
                            se = s_act(ctmp[:], deltaT[:, k, :], AF.Exp,
                                       scale=aneg[:, k, s:s + 1])
                            r.wait("vector", "SE", se)
                            v_stt(ctmp[:], ctmp[:], hinit[:, 16 * k + s:16 * k + s + 1],
                                  crepF[:], ALU.mult, ALU.mult)
                            ve = v_tt(rT[:, k, :], rT[:, k, :], ctmp[:], ALU.add)
                            r.wait("scalar", "VE", ve)
                    r.barrier_now()
                    r.flush()

        emit_ffn(0)
        if stage >= 5:
            emit_mamba()
        if stage >= 6:
            emit_ffn(1)

        with ExitStack() as OX:
            otm = OX.enter_context(nc.sbuf_tensor("otm", [128, D], F32))
            for a in range(8):
                for k in range(DT):
                    b = k % 4
                    r.wait("tensor", "VE", ps_reader[b])
                    pe = t_tp(PS[b][:, 0:128], rT[:, k, 128 * a:128 * (a + 1)])
                    ps_writer[b] = pe
                    ve = evac(b, otm[:, 128 * k:128 * (k + 1)], 128)
                r.wait("gpsimd", "VE", ve)
                st = r.dma("gpsimd", dout[128 * a:128 * (a + 1), :], otm[:], "ST")
                r.wait("vector", "ST", st)
            r.wait("gpsimd", "ST", r.cnt["ST"])
            r.flush()

    return nc



def _pack_lhsT(w):
    Dd, M = w.shape
    return np.ascontiguousarray(
        w.reshape(Dd // 128, 128, M // 128, 128).transpose(2, 1, 0, 3))


def prep_inputs(inp):
    x = np.asarray(inp["L0_attn_x"], dtype=np.float32).reshape(8192, 2048)
    W_dbc = np.asarray(inp["L0_attn_W_dbc"], np.float32)
    b_dbc = np.asarray(inp["L0_attn_b_dbc"], np.float32)
    W_dt = np.asarray(inp["L0_attn_W_dt"], np.float32)
    b_dt = np.asarray(inp["L0_attn_b_dt"], np.float32)
    A_log = np.asarray(inp["L0_attn_A_log"], np.float32)
    Dv = np.asarray(inp["L0_attn_D"], np.float32)
    gate_w = np.asarray(inp["L0_ffn_gate_w"], np.float32)
    w_gate_up = np.asarray(inp["L0_ffn_w_gate_up"], np.float32)
    w_down = np.asarray(inp["L0_ffn_w_down"], np.float32)
    sgu = np.asarray(inp["L0_ffn_shared_gate_up"], np.float32)
    sdn = np.asarray(inp["L0_ffn_shared_down"], np.float32)

    common = dict(
        wsg=_pack_lhsT(sgu),
        wsd=np.ascontiguousarray(sdn),
        wg=np.stack([_pack_lhsT(w_gate_up[e]) for e in range(E)]),
        wd=np.ascontiguousarray(w_down),
        gwT=np.ascontiguousarray(gate_w.T),
        wdbcT=np.ascontiguousarray(W_dbc.T),
        wdtT=np.ascontiguousarray(W_dt.T),
        bdlo=np.ascontiguousarray(b_dbc[:128].reshape(128, 1)),
        bbc=np.ascontiguousarray(b_dbc[128:].reshape(32, 1)),
        bdt=np.ascontiguousarray(b_dt.reshape(DT, 128).T),
        alog=np.ascontiguousarray(A_log.reshape(DT, 128, DST).transpose(1, 0, 2)),
        dvec=np.ascontiguousarray(Dv.reshape(DT, 128).T),
    )
    maps = []
    for c in range(NCORES):
        m = dict(common)
        m["x"] = np.ascontiguousarray(x[TLOC * c:TLOC * (c + 1)])
        m["cid"] = np.array([[float(c)]], np.float32)
        maps.append(m)
    return maps


def kernel(**inputs):
    maps = prep_inputs(inputs)
    nc = build_program(stage=6, debug=False)
    nc.compile()
    res = run_bass_kernel_spmd(nc, maps, core_ids=list(range(NCORES)))
    kernel.last_results = res
    out = np.concatenate([res.results[c]["out"] for c in range(NCORES)], axis=0)
    return out.reshape(1, 8192, 2048)


# version 02
# speedup vs baseline: 1.1702x; 1.1702x over previous
import os
import numpy as np
from contextlib import ExitStack

import concourse.bacc as bacc
import concourse.bass as bass
import concourse.mybir as mybir
from concourse.bass_utils import run_bass_kernel_spmd

F32 = mybir.dt.float32
BF16 = mybir.dt.bfloat16
I16 = mybir.dt.int16
I32 = mybir.dt.int32
U32 = mybir.dt.uint32
AF = mybir.ActivationFunctionType
ALU = mybir.AluOpType
AX = mybir.AxisListType.X

NCORES = 8
TLOC = 1024
D = 2048
DT = D // 128
E = 16
CAP = 256
II = 1408
ITL = II // 128
SI = 2816
SIT = SI // 128
DST = 16
NSORT = E * CAP + 128
DUMPPOS = E * CAP
EPS = 1e-5

ENGINES = ("sync", "tensor", "vector", "scalar", "gpsimd")


class Rec:
    def __init__(self, nc, ctx, block):
        self.nc = nc
        self.block = block
        self.ops = {e: [] for e in ENGINES}
        self.semh = {}
        self.cnt = {}
        for name in ("IN", "WD", "GD", "PE", "VE", "SE", "GE", "CC", "ST"):
            self.semh[name] = ctx.enter_context(nc.semaphore(name))
            self.cnt[name] = 0
        self.barrier = dict(self.cnt)

    VSERIAL = True

    def op(self, eng, builder, sem=None, n=1):
        if sem is None and eng == "vector" and self.VSERIAL:
            sem = "VE"
        if sem is None:
            self.ops[eng].append(builder)
            return None
        h = self.semh[sem]
        self.cnt[sem] += n

        def f(EE, b=builder, h=h, n=n):
            b(EE).then_inc(h, n)
        self.ops[eng].append(f)
        tok = self.cnt[sem]
        if eng == "vector" and sem == "VE" and self.VSERIAL:
            self.ops[eng].append(lambda EE, h=h, t=tok: EE.wait_ge(h, t))
        return tok

    def wait(self, eng, sem, tok):
        if not tok:
            return
        h = self.semh[sem]
        self.ops[eng].append(lambda EE, h=h, t=tok: EE.wait_ge(h, t))

    def dma(self, eng, out, in_, sem=None):
        if sem is None:
            sem = "GD" if eng == "gpsimd" else ("IN" if eng == "sync" else "WD")
        return self.op(eng, lambda EE, o=out, i=in_: EE.dma_start(out=o, in_=i), sem, 16)

    def dma_nc(self, eng, out, in_, sem=None):
        if sem is None:
            sem = "GD" if eng == "gpsimd" else ("IN" if eng == "sync" else "WD")
        return self.op(eng, lambda EE, o=out, i=in_: EE.dma_start(
            out=o, in_=i, allow_slow_non_contiguous=True), sem, 16)

    def flush(self):
        for eng in ENGINES:
            ops = self.ops[eng]
            if not ops:
                continue

            def body(EE, ops=list(ops)):
                for f in ops:
                    f(EE)
            getattr(self.block, eng)(body)
            self.ops[eng] = []

    def barrier_now(self):
        toks = dict(self.cnt)
        for eng in ENGINES:
            for s, t in toks.items():
                if t > self.barrier.get(s, 0):
                    self.wait(eng, s, t)
        self.barrier = toks


def build_program(stage=6, debug=False):
    nc = bacc.Bacc("TRN2", target_bir_lowering=False, debug=False)

    dx = nc.dram_tensor("x", [TLOC, D], F32, kind="ExternalInput").ap()
    dcid = nc.dram_tensor("cid", [1, 1], F32, kind="ExternalInput").ap()
    dwsg = nc.dram_tensor("wsg", [2 * SIT, 128, DT, 128], F32, kind="ExternalInput").ap()
    dwsd = nc.dram_tensor("wsd", [SI, D], F32, kind="ExternalInput").ap()
    dwg = nc.dram_tensor("wg", [E, 2 * ITL, 128, DT, 128], F32, kind="ExternalInput").ap()
    dwd = nc.dram_tensor("wd", [E, II, D], F32, kind="ExternalInput").ap()
    dgwT = nc.dram_tensor("gwT", [D, E], F32, kind="ExternalInput").ap()
    dwdbcT = nc.dram_tensor("wdbcT", [D, 160], F32, kind="ExternalInput").ap()
    dwdtT = nc.dram_tensor("wdtT", [128, D], F32, kind="ExternalInput").ap()
    dbdlo = nc.dram_tensor("bdlo", [128, 1], F32, kind="ExternalInput").ap()
    dbbc = nc.dram_tensor("bbc", [32, 1], F32, kind="ExternalInput").ap()
    dbdt = nc.dram_tensor("bdt", [128, DT], F32, kind="ExternalInput").ap()
    dalog = nc.dram_tensor("alog", [128, DT, DST], F32, kind="ExternalInput").ap()
    ddvec = nc.dram_tensor("dvec", [128, DT], F32, kind="ExternalInput").ap()

    dout = nc.dram_tensor("out", [TLOC, D], F32, kind="ExternalOutput").ap()

    dzsp = nc.dram_tensor("zsp", [TLOC, D], F32).ap()
    dysort = nc.dram_tensor("ysort", [NSORT, D], F32).ap()
    didlin = nc.dram_tensor("idlin", [NSORT, 1], F32).ap()
    dglin = nc.dram_tensor("glin", [NSORT, 1], F32).ap()
    dposlin = nc.dram_tensor("poslin", [2 * TLOC, 1], F32).ap()
    dccin = nc.dram_tensor("ccin", [128, 512], F32).ap()
    dccout = nc.dram_tensor("ccout", [128 * NCORES, 512], F32).ap()

    dbg = {}
    if debug:
        dbg["z1"] = nc.dram_tensor("dbg_z1", [128, DT * TLOC], F32, kind="ExternalOutput").ap()
        dbg["route"] = nc.dram_tensor("dbg_route", [128, 64], F32, kind="ExternalOutput").ap()
        dbg["pos"] = nc.dram_tensor("dbg_pos", [128, 16], F32, kind="ExternalOutput").ap()
        dbg["idl"] = nc.dram_tensor("dbg_idl", [128, 33], F32, kind="ExternalOutput").ap()
        dbg["gl"] = nc.dram_tensor("dbg_gl", [128, 33], F32, kind="ExternalOutput").ap()
        dbg["snap"] = nc.dram_tensor("dbg_snap", [128, 80], F32, kind="ExternalOutput").ap()
        dbg["mam"] = nc.dram_tensor("dbg_mam", [128, 768], F32, kind="ExternalOutput").ap()
        dbg["bc"] = nc.dram_tensor("dbg_bc", [32, TLOC], F32, kind="ExternalOutput").ap()

    with ExitStack() as CTX:
        EN = CTX.enter_context
        block = EN(nc.Block())
        r = Rec(nc, CTX, block)

        rT = EN(nc.sbuf_tensor("rT", [128, DT, TLOC], F32))
        ident = EN(nc.sbuf_tensor("ident", [128, 128], F32))
        triUS = EN(nc.sbuf_tensor("triUS", [128, 128], F32))
        allone = EN(nc.sbuf_tensor("allone", [128, 128], F32))
        onecol = EN(nc.sbuf_tensor("onecol", [128, 1], F32))
        iota16f = EN(nc.sbuf_tensor("iota16f", [128, 16], F32))
        baseT = EN(nc.sbuf_tensor("baseT", [128, 16], F32))
        pid_f = EN(nc.sbuf_tensor("pid_f", [128, 1], F32))
        gwT_s = EN(nc.sbuf_tensor("gwT_s", [128, DT, E], F32))
        cid_s = EN(nc.sbuf_tensor("cid_s", [1, 1], F32))
        zero2k = EN(nc.sbuf_tensor("zero2k", [128, 64], F32))

        PS = [EN(nc.psum_tensor(f"ps{b}", [128, 512], F32)) for b in range(8)]
        ps_writer = [0] * 8
        ps_reader = [0] * 8

        toks = {}

        def v_ts(out, in0, s1, s2, op0, op1=None):
            if op1 is None:
                return r.op("vector", lambda EE, o=out, i=in0, a=s1, b=s2, p=op0:
                            EE.tensor_scalar(o, i, a, b, op0=p), "VE")
            return r.op("vector", lambda EE, o=out, i=in0, a=s1, b=s2, p=op0, q=op1:
                        EE.tensor_scalar(o, i, a, b, op0=p, op1=q), "VE")

        def v_tt(out, in0, in1, opk):
            return r.op("vector", lambda EE, o=out, a=in0, b=in1, p=opk:
                        EE.tensor_tensor(o, a, b, op=p), "VE")

        def v_stt(out, in0, scal, in1, op0, op1):
            return r.op("vector", lambda EE, o=out, a=in0, s=scal, b=in1, p=op0, q=op1:
                        EE.scalar_tensor_tensor(o, a, s, b, op0=p, op1=q), "VE")

        def v_copy(out, in_):
            return r.op("vector", lambda EE, o=out, i=in_: EE.tensor_copy(o, i), "VE")

        def v_red(out, in_, opk):
            r.op("vector", lambda EE: EE.nop())
            return r.op("vector", lambda EE, o=out, i=in_, p=opk:
                        EE.tensor_reduce(o, i, axis=AX, op=p), "VE")

        def v_memset(ap_, val):
            return r.op("vector", lambda EE, o=ap_, v=val: EE.memset(o, v), "VE")

        def v_recip(out, in_):
            r.op("vector", lambda EE: EE.nop())
            return r.op("vector", lambda EE, o=out, i=in_: EE.reciprocal(o, i), "VE")

        def v_scan(out, d0, d1, init, op0=ALU.mult, op1=ALU.add):
            r.op("vector", lambda EE: EE.nop())
            return r.op("vector", lambda EE, o=out, a=d0, b=d1, i=init, p=op0, q=op1:
                        EE.tensor_tensor_scan(o, a, b, i, op0=p, op1=q), "VE")

        def s_act(out, in_, func, bias=None, scale=None):
            def f(EE, o=out, i=in_, fn=func, b=bias, s=scale):
                kw = {}
                if b is not None:
                    kw["bias"] = b
                if s is not None:
                    kw["scale"] = s
                return EE.activation(o, i, fn, **kw)
            return r.op("scalar", f, "SE")

        def t_mm(out, lhsT, rhs, start, stop, sem=None):
            return r.op("tensor", lambda EE, o=out, l=lhsT, rr=rhs, s=start, t=stop:
                        EE.matmul(o, l, rr, start=s, stop=t), sem)

        def t_tp(out, in_, sem="PE"):
            return r.op("tensor", lambda EE, o=out, i=in_, d=ident[:]:
                        EE.transpose(o, i, d), sem)

        def mm_group(bank, width, lhsT_list, rhs_list):
            r.wait("tensor", "VE", ps_reader[bank])
            n = len(lhsT_list)
            pe = None
            for i, (lt, rh) in enumerate(zip(lhsT_list, rhs_list)):
                pe = t_mm(PS[bank][:, 0:width], lt, rh, i == 0, i == n - 1,
                          "PE" if i == n - 1 else None)
            ps_writer[bank] = pe
            return pe

        def evac(bank, dst, width, fn=None):
            r.wait("vector", "PE", ps_writer[bank])
            if fn is None:
                ve = v_copy(dst, PS[bank][:, 0:width])
            else:
                ve = fn(dst, PS[bank][:, 0:width])
            ps_reader[bank] = ve
            return ve

        scr_i = EN(nc.sbuf_tensor("scr_i", [128, 128], I32))
        p0ctx = ExitStack()
        xtm = p0ctx.enter_context(nc.sbuf_tensor("xtm", [128, 8, D], F32))

        idn_i = p0ctx.enter_context(nc.sbuf_tensor("idn_i", [128, 128], I32))
        r.op("gpsimd", lambda EE: EE.iota(scr_i[:, 0:16], pattern=[[1, 16]], base=0,
                                          channel_multiplier=0), "GE")
        r.op("gpsimd", lambda EE: EE.iota(scr_i[:, 16:17], pattern=[[0, 1]], base=0,
                                          channel_multiplier=1), "GE")
        r.op("gpsimd", lambda EE: EE.iota(idn_i[:], pattern=[[1, 128]], base=0,
                                          channel_multiplier=-1), "GE")
        ge3 = r.cnt["GE"]
        r.wait("vector", "GE", ge3)
        v_copy(iota16f[:], scr_i[:, 0:16])
        v_ts(baseT[:], iota16f[:], float(CAP), None, ALU.mult)
        v_copy(pid_f[:], scr_i[:, 16:17])
        v_memset(onecol[:], 1.0)
        v_memset(allone[:], 1.0)
        v_memset(zero2k[:], 0.0)
        v_ts(ident[:], idn_i[:], 0.0, None, ALU.is_equal)
        tv = v_ts(triUS[:], idn_i[:], 0.0, None, ALU.is_gt)

        in_x = r.dma("sync", xtm[:], dx.rearrange("(a p) d -> p a d", p=128))
        in_gw = r.dma("sync", gwT_s[:], dgwT.rearrange("(k p) e -> p k e", p=128))
        in_cid = r.dma("sync", cid_s[:], dcid)

        r.wait("tensor", "IN", in_x)
        r.wait("tensor", "VE", tv)
        for a in range(8):
            for k in range(DT):
                b = (a * DT + k) % 4
                r.wait("tensor", "VE", ps_reader[b])
                pe = t_tp(PS[b][:, 0:128], xtm[:, a, 128 * k:128 * (k + 1)])
                ps_writer[b] = pe
                evac(b, rT[:, k, 128 * a:128 * (a + 1)], 128)
        r.barrier_now()
        r.flush()
        p0ctx.close()

        def emit_ln(zT, meanB, rstdB, sc, make_z=True, sq=None):
            rows, rows2 = sc
            if sq is None:
                sq = zT
            for k in range(DT):
                v_tt(sq[:, k, :], rT[:, k, :], rT[:, k, :], ALU.mult)
            ve_sq = r.cnt["VE"]
            r.wait("tensor", "VE", ve_sq)
            for half in range(2):
                sl = slice(512 * half, 512 * (half + 1))
                m01, r01 = meanB[0:1, sl], rstdB[0:1, sl]
                for b, src, dstrow in ((0, rT, rows), (1, sq, rows2)):
                    r.wait("tensor", "VE", ps_reader[b])
                    for k in range(DT):
                        pe = t_mm(PS[b][0:1, 0:512], onecol[:], src[:, k, sl],
                                  k == 0, k == DT - 1, "PE" if k == DT - 1 else None)
                    ps_writer[b] = pe
                    r.wait("vector", "PE", ps_writer[b])
                    ve = v_copy(dstrow[:], PS[b][0:1, 0:512])
                    ps_reader[b] = ve
                v_ts(m01, rows[:], 1.0 / D, None, ALU.mult)
                v_ts(rows2[:], rows2[:], 1.0 / D, None, ALU.mult)
                v_tt(rows[:], m01, m01, ALU.mult)
                v_tt(rows2[:], rows2[:], rows[:], ALU.subtract)
                v_ts(rows2[:], rows2[:], EPS, None, ALU.add)
                ve = v_recip(rows[:], rows2[:])
                r.wait("scalar", "VE", ve)
                se = s_act(rows[:], rows[:], AF.Sqrt)
                r.wait("vector", "SE", se)
                v_tt(r01, rows[:], rows[:], ALU.mult)
                v_tt(r01, r01, rows2[:], ALU.mult)
                v_ts(r01, r01, -0.5, 1.5, ALU.mult, ALU.add)
                ve = v_tt(r01, rows[:], r01, ALU.mult)
            r.wait("gpsimd", "VE", ve)
            r.op("gpsimd", lambda EE, o=meanB: EE.partition_broadcast(o[:], o[0:1, :]), "GE")
            g2 = r.op("gpsimd", lambda EE, o=rstdB: EE.partition_broadcast(o[:], o[0:1, :]), "GE")
            r.wait("vector", "GE", g2)
            if make_z:
                for k in range(DT):
                    v_tt(zT[:, k, :], rT[:, k, :], meanB[:], ALU.subtract)
                    v_tt(zT[:, k, :], zT[:, k, :], rstdB[:], ALU.mult)

        def alloc_routing(EF, call):
            d = {}
            d["lg"] = EF(nc.sbuf_tensor(f"lg{call}", [128, 8, E], F32))
            d["mx8"] = EF(nc.sbuf_tensor(f"mx8{call}", [128, 8, 8], F32))
            d["mi8"] = EF(nc.sbuf_tensor(f"mi8{call}", [128, 8, 8], F32))
            d["wk"] = EF(nc.sbuf_tensor(f"wk{call}", [128, 8, 2], F32))
            d["colv3"] = EF(nc.sbuf_tensor(f"colv{call}", [128, 8, 12], F32))
            d["lgf"] = EF(nc.sbuf_tensor(f"lgf{call}", [128, 16], F32))
            d["mxf"] = EF(nc.sbuf_tensor(f"mxf{call}", [128, 8], F32))
            d["mif"] = EF(nc.sbuf_tensor(f"mif{call}", [128, 8], U32))
            d["ohA"] = EF(nc.sbuf_tensor(f"ohA{call}", [128, 16, E], F32))
            d["Rrk"] = EF(nc.sbuf_tensor(f"Rrk{call}", [128, 16, E], F32))
            d["posg"] = EF(nc.sbuf_tensor(f"posg{call}", [128, 16, 2], F32))
            d["posI"] = EF(nc.sbuf_tensor(f"posI{call}", [128, 1], U32))
            d["posIv"] = EF(nc.sbuf_tensor(f"posIv{call}", [128, 16], U32))
            d["valv"] = EF(nc.sbuf_tensor(f"valv{call}", [128, 16], F32))
            d["poscp"] = EF(nc.sbuf_tensor(f"poscp{call}", [128, 16], F32))
            d["valf"] = EF(nc.sbuf_tensor(f"valf{call}", [128, 1], F32))
            d["scr"] = EF(nc.sbuf_tensor(f"pscr{call}", [128, E], F32))
            d["sidx_all"] = EF(nc.sbuf_tensor(f"sidx{call}", [128, NSORT // 16], I16))
            d["idw8"] = EF(nc.sbuf_tensor(f"idw8{call}", [128, NSORT // 16], F32))
            d["gtile"] = EF(nc.sbuf_tensor(f"gtile{call}", [128, NSORT // 128], F32))
            d["posw"] = EF(nc.sbuf_tensor(f"posw{call}", [128, 128], I16))
            d["poswf"] = EF(nc.sbuf_tensor(f"poswf{call}", [128, 128], F32))
            if debug:
                d["snap"] = EF(nc.sbuf_tensor(f"snap{call}", [128, 80], F32))
            return d

        def emit_routing(zT, RA, call, part=None):
            lg = RA["lg"]
            mx8, mi8, wk = RA["mx8"], RA["mi8"], RA["wk"]
            colv3, lgf, mxf, mif = RA["colv3"], RA["lgf"], RA["mxf"], RA["mif"]
            ohA, Rrk, posg, posI = RA["ohA"], RA["Rrk"], RA["posg"], RA["posI"]
            poscp, valf, scr = RA["poscp"], RA["valf"], RA["scr"]
            posIv, valv = RA["posIv"], RA["valv"]
            sidx_all, idw8, gtile = RA["sidx_all"], RA["idw8"], RA["gtile"]
            posw, poswf = RA["posw"], RA["poswf"]
            snap = RA.get("snap")

            if part != 2:
                ve_z = r.cnt["VE"]
                r.wait("tensor", "VE", ve_z)
                for tt in range(8):
                    b = tt % 2
                    mm_group(b, E, [zT[:, k, 128 * tt:128 * (tt + 1)] for k in range(DT)],
                             [gwT_s[:, k, :] for k in range(DT)])
                    evac(b, lg[:, tt, :], E)
            for tt in ([] if part == 2 else range(8)):
                colv = colv3[:, tt, :]
                v_red(colv[:, 0:1], lg[:, tt, :], ALU.max)
                r.wait("vector", "VE", r.cnt["VE"])
                v_ts(lgf[:], lg[:, tt, :], colv[:, 0:1], None, ALU.is_equal)
                v_tt(scr[:], lgf[:], iota16f[:], ALU.mult)
                v_ts(lgf[:], lgf[:], -99.0, 99.0, ALU.mult, ALU.add)
                v_tt(scr[:], scr[:], lgf[:], ALU.add)
                if debug and tt == 0:
                    v_copy(snap[:, 32:48], scr[:])
                v_red(colv[:, 1:2], scr[:], ALU.min)
                r.wait("vector", "VE", r.cnt["VE"])
                v_ts(ohA[:, tt, :], iota16f[:], colv[:, 1:2], None, ALU.is_equal)
                v_stt(lgf[:], ohA[:, tt, :], -1e30, lg[:, tt, :], ALU.mult, ALU.add)
                if debug and tt == 0:
                    v_copy(snap[:, 0:16], ohA[:, 0, :])
                    v_copy(snap[:, 16:32], lgf[:])
                    v_copy(snap[:, 64:80], lg[:, 0, :])
                v_red(colv[:, 2:3], lgf[:], ALU.max)
                r.wait("vector", "VE", r.cnt["VE"])
                v_ts(scr[:], lgf[:], colv[:, 2:3], None, ALU.is_equal)
                v_tt(mx8[:, tt, 0:1], colv[:, 0:1], colv[:, 0:1], ALU.min)
                v_copy(mx8[:, tt, 1:2], colv[:, 2:3])
                v_copy(mi8[:, tt, 0:1], colv[:, 1:2])
                v_tt(scr[:], scr[:], iota16f[:], ALU.mult)
                r.wait("vector", "VE", r.cnt["VE"])
                v_ts(lgf[:], lgf[:], colv[:, 2:3], None, ALU.is_equal)
                v_ts(lgf[:], lgf[:], -99.0, 99.0, ALU.mult, ALU.add)
                v_tt(scr[:], scr[:], lgf[:], ALU.add)
                v_red(colv[:, 3:4], scr[:], ALU.min)
                r.wait("vector", "VE", r.cnt["VE"])
                v_ts(ohA[:, 8 + tt, :], iota16f[:], colv[:, 3:4], None, ALU.is_equal)
                v_copy(mi8[:, tt, 1:2], colv[:, 3:4])
                ve = v_ts(colv[:, 4:5], colv[:, 0:1], -1.0, None, ALU.mult)
                r.wait("scalar", "VE", ve)
                se = s_act(lg[:, tt, :], lg[:, tt, :], AF.Exp, bias=colv[:, 4:5])
                r.wait("vector", "SE", se)
                if debug and tt == 0:
                    v_copy(snap[:, 48:64], lg[:, 0, :])
                v_red(colv[:, 5:6], lg[:, tt, :], ALU.add)
                v_recip(colv[:, 6:7], colv[:, 5:6])
                v_copy(wk[:, tt, 0:1], colv[:, 6:7])
                ve = v_tt(colv[:, 7:8], colv[:, 2:3], colv[:, 4:5], ALU.add)
                r.wait("scalar", "VE", ve)
                se = s_act(colv[:, 8:9], colv[:, 7:8], AF.Exp)
                r.wait("vector", "SE", se)
                v_tt(wk[:, tt, 1:2], colv[:, 8:9], colv[:, 6:7], ALU.mult)
            ve_oh = r.cnt["VE"]
            if part == 1:
                return

            r.wait("tensor", "VE", ve_oh)
            for c in range(16):
                b = 2 + (c % 2)
                lts = [allone[:]] * c + [triUS[:]]
                rhs = [ohA[:, cp, :] for cp in range(c)] + [ohA[:, c, :]]
                mm_group(b, E, lts, rhs)
                evac(b, Rrk[:, c, :], E)

            for c in range(16):
                tt, sl = c % 8, c // 8
                colv = colv3[:, tt, :]
                v_tt(scr[:], Rrk[:, c, :], ohA[:, c, :], ALU.mult)
                v_red(colv[:, 6:7], scr[:], ALU.add)
                v_tt(scr[:], baseT[:], ohA[:, c, :], ALU.mult)
                v_red(colv[:, 7:8], scr[:], ALU.add)
                v_ts(colv[:, 8:9], colv[:, 6:7], float(CAP), None, ALU.is_lt)
                v_tt(colv[:, 9:10], colv[:, 6:7], colv[:, 7:8], ALU.add)
                v_tt(colv[:, 9:10], colv[:, 9:10], colv[:, 8:9], ALU.mult)
                v_ts(colv[:, 10:11], colv[:, 8:9], -float(DUMPPOS), float(DUMPPOS),
                     ALU.mult, ALU.add)
                v_tt(posg[:, c, 0:1], colv[:, 9:10], colv[:, 10:11], ALU.add)
                v_tt(posg[:, c, 1:2], wk[:, tt, sl:sl + 1], colv[:, 8:9], ALU.mult)
            ve_pos = r.cnt["VE"]

            r.wait("gpsimd", "VE", ve_pos)
            r.dma("gpsimd", didlin.rearrange("(p a) o -> p (a o)", p=128),
                  zero2k[:, 0:NSORT // 128])
            r.dma("gpsimd", dglin.rearrange("(p a) o -> p (a o)", p=128),
                  zero2k[:, 0:NSORT // 128])
            v_copy(poscp[:], posg[:, :, 0:1].squeeze(2))
            r.wait("gpsimd", "VE", r.cnt["VE"])
            r.dma_nc("gpsimd", dposlin.rearrange("(c p) o -> p (c o)", p=128),
                  poscp[:])
            for c in range(16):
                tt = c % 8
                v_ts(valv[:, c:c + 1], pid_f[:], 1.0, float(128 * tt + 1), ALU.mult, ALU.add)
            ve = v_copy(posIv[:], poscp[:])
            gd_pre = r.cnt["GD"]
            r.wait("gpsimd", "GD", gd_pre)
            r.wait("gpsimd", "VE", ve)
            for c in range(16):
                r.op("gpsimd", lambda EE, c=c: EE.indirect_dma_start(
                    out=didlin, out_offset=bass.IndirectOffsetOnAxis(ap=posIv[:, c:c + 1], axis=0),
                    in_=valv[:, c:c + 1], in_offset=None), "GD", 16)
                r.op("gpsimd", lambda EE, c=c: EE.indirect_dma_start(
                    out=dglin, out_offset=bass.IndirectOffsetOnAxis(ap=posIv[:, c:c + 1], axis=0),
                    in_=posg[:, c, 1:2], in_offset=None), "GD", 16)
            gd_scat = r.cnt["GD"]

            r.wait("gpsimd", "GD", gd_scat)
            for g in range(8):
                r.dma_nc("gpsimd", idw8[16 * g:16 * (g + 1), :],
                      didlin.rearrange("(f p) o -> p (f o)", p=16))
            r.dma_nc("gpsimd", gtile[:], dglin.rearrange("(f p) o -> p (f o)", p=128))
            for g in range(8):
                r.dma_nc("gpsimd", poswf[16 * g:16 * (g + 1), :],
                      dposlin.rearrange("(f p) o -> p (f o)", p=16))
            toks[("route_rb", call)] = r.cnt["GD"]
            return

        def emit_routing3(RA, call):
            idw8, sidx_all = RA["idw8"], RA["sidx_all"]
            posw, poswf = RA["posw"], RA["poswf"]
            r.wait("vector", "GD", toks[("route_rb", call)])
            v_ts(idw8[:], idw8[:], -1.0, 0.0, ALU.add, ALU.max)
            v_copy(sidx_all[:], idw8[:])
            ve = v_copy(posw[:], poswf[:])
            toks[("route", call)] = r.cnt["VE"]
            if False:
                rt = RA["mx8"]
                for tt in range(8):
                    v_copy(rt[:, tt, 2:3], mi8[:, tt, 0:1])
                    v_copy(rt[:, tt, 3:4], mi8[:, tt, 1:2])
                    v_copy(rt[:, tt, 4:5], wk[:, tt, 0:1])
                    v_copy(rt[:, tt, 5:6], wk[:, tt, 1:2])
                r.wait("gpsimd", "VE", r.cnt["VE"])
                r.dma("gpsimd", dbg["route"], rt.ap().rearrange("p a b -> p (a b)"), "ST")
                r.dma("gpsimd", dbg["pos"], poscp[:], "ST")
                r.dma("gpsimd", dbg["idl"], idw8[:, 0:33], "ST")
                r.dma("gpsimd", dbg["gl"], gtile[:], "ST")
                r.dma("gpsimd", dbg["snap"], snap[:], "ST")

            return

        def emit_spill_z(zT, EF, call):
            ztm = EF(nc.sbuf_tensor(f"ztm{call}", [128, D], F32))
            ve_z = r.cnt["VE"]
            r.wait("tensor", "VE", ve_z)
            for a in range(8):
                for k in range(DT):
                    b = 4 + (k % 2)
                    r.wait("tensor", "VE", ps_reader[b])
                    pe = t_tp(PS[b][:, 0:128], zT[:, k, 128 * a:128 * (a + 1)])
                    ps_writer[b] = pe
                    ve = evac(b, ztm[:, 128 * k:128 * (k + 1)], 128)
                r.wait("gpsimd", "VE", ve)
                gd = r.dma("gpsimd", dzsp[128 * a:128 * (a + 1), :], ztm[:])
                r.wait("vector", "GD", gd)
            toks[("zspill", call)] = r.cnt["GD"]

        def emit_shared(zT, EF, call):
            TCH = 512
            wbuf = [EF(nc.sbuf_tensor(f"wsgb{call}_{i}", [128, DT, 128], F32)) for i in range(2)]
            wdbuf0 = EF(nc.sbuf_tensor(f"wsdb{call}_0", [128, D], F32))
            wdbuf = [wdbuf0.ap(), wbuf[0].ap().rearrange("p a b -> p (a b)")]
            hsb = EF(nc.sbuf_tensor(f"hs{call}", [128, SIT, TCH], F32))
            wfree = [0, 0]
            wdfree = [0, 0]

            ve_z = r.cnt["VE"]
            r.wait("tensor", "VE", ve_z)
            for tc in range(TLOC // TCH):
                sl = slice(TCH * tc, TCH * (tc + 1))
                if tc > 0:
                    r.wait("vector", "PE", r.cnt["PE"])
                    wfree[0] = max(wfree[0], wdfree[1])
                for it in range(SIT):
                    for gu in range(2):
                        jt = it + SIT * gu
                        buf = jt % 2
                        r.wait("sync", "PE", wfree[buf])
                        wtok = r.dma("sync", wbuf[buf][:], dwsg[jt], "WD")
                        bank = gu
                        r.wait("tensor", "WD", wtok)
                        pe = mm_group(bank, TCH, [wbuf[buf][:, k, :] for k in range(DT)],
                                      [zT[:, k, sl] for k in range(DT)])
                        wfree[buf] = pe
                    r.wait("scalar", "PE", ps_writer[1])
                    se = s_act(hsb[:, it, :], PS[0][:, 0:TCH], AF.Silu)
                    r.wait("vector", "SE", se)
                    ve = v_tt(hsb[:, it, :], hsb[:, it, :], PS[1][:, 0:TCH], ALU.mult)
                    ps_reader[0] = ve
                    ps_reader[1] = ve
                ve_hs = r.cnt["VE"]
                r.wait("tensor", "VE", ve_hs)
                wdfree[1] = max(wdfree[1], wfree[0])
                for dg in range(2):
                    for it in range(SIT):
                        buf = it % 2
                        r.wait("sync", "PE", wdfree[buf])
                        wtok = r.dma("sync", wdbuf[buf][:], dwsd[128 * it:128 * (it + 1), :], "WD")
                        r.wait("tensor", "WD", wtok)
                        for d8 in range(8):
                            k = 8 * dg + d8
                            bank = d8
                            if it == 0:
                                r.wait("tensor", "VE", ps_reader[bank])
                            pe = t_mm(PS[bank][:, 0:TCH], wdbuf[buf][:, 128 * k:128 * (k + 1)],
                                      hsb[:, it, :], it == 0, it == SIT - 1,
                                      "PE" if (d8 == 7 or it == SIT - 1) else None)
                            if it == SIT - 1:
                                ps_writer[bank] = pe
                        wdfree[buf] = pe
                    for d8 in range(8):
                        k = 8 * dg + d8
                        bank = d8
                        r.wait("vector", "PE", ps_writer[bank])
                        ve = v_stt(rT[:, k, sl], PS[bank][:, 0:TCH], 0.5, rT[:, k, sl],
                                   ALU.mult, ALU.add)
                        ps_reader[bank] = ve

        def emit_experts(RA, EF, call):
            sidx_all, gtile, posw = RA["sidx_all"], RA["gtile"], RA["posw"]
            zst = [EF(nc.sbuf_tensor(f"zst{call}_{i}", [128, 2, D], F32)) for i in range(2)]
            zeT = EF(nc.sbuf_tensor(f"zeT{call}", [128, DT, CAP], F32))
            wgb = [EF(nc.sbuf_tensor(f"wgb{call}_{i}", [128, DT, 128], F32)) for i in range(2)]
            hT = EF(nc.sbuf_tensor(f"hT{call}", [128, ITL, CAP], F32))
            wdb = [EF(nc.sbuf_tensor(f"wdb{call}_{i}", [128, D], F32)) for i in range(2)]
            ye = EF(nc.sbuf_tensor(f"ye{call}", [128, D], F32))
            sg = EF(nc.sbuf_tensor(f"sge{call}", [128, CAP], F32))
            zst_free = [0, 0]
            wg_free = [0, 0]
            wd_free = [0, 0]
            ye_free = 0

            r.wait("gpsimd", "VE", toks[("route", call)])
            r.wait("gpsimd", "GD", toks[("zspill", call)])
            for e in range(E):
                buf = e % 2
                if e > 0:
                    r.wait("vector", "PE", r.cnt["PE"])
                r.wait("gpsimd", "PE", zst_free[buf])
                gd = r.op("gpsimd", lambda EE, b=buf, e=e: EE.dma_gather(
                    zst[b][:], dzsp, sidx_all[:, 16 * e:16 * (e + 1)],
                    num_idxs=CAP, num_idxs_reg=CAP, elem_size=D, transpose=False),
                    "GD", 16)
                r.wait("tensor", "GD", gd)
                for c in range(2):
                    for k in range(DT):
                        b = 2 + (k % 2)
                        r.wait("tensor", "VE", ps_reader[b])
                        pe = t_tp(PS[b][:, 0:128], zst[buf][:, c, 128 * k:128 * (k + 1)])
                        ps_writer[b] = pe
                        evac(b, zeT[:, k, 128 * c:128 * (c + 1)], 128)
                zst_free[buf] = pe
                for it in range(ITL):
                    for gu in range(2):
                        jt = it + ITL * gu
                        wbuf_i = jt % 2
                        r.wait("sync", "PE", wg_free[wbuf_i])
                        wtok = r.dma("sync", wgb[wbuf_i][:], dwg[e, jt], "WD")
                        bank = 4 + gu
                        r.wait("tensor", "WD", wtok)
                        pe = mm_group(bank, CAP, [wgb[wbuf_i][:, k, :] for k in range(DT)],
                                      [zeT[:, k, :] for k in range(DT)])
                        wg_free[wbuf_i] = pe
                    r.wait("scalar", "PE", ps_writer[5])
                    se = s_act(sg[:], PS[4][:, 0:CAP], AF.Silu)
                    r.wait("vector", "SE", se)
                    ve = v_tt(hT[:, it, :], sg[:], PS[5][:, 0:CAP], ALU.mult)
                    ps_reader[4] = ve
                    ps_reader[5] = ve
                ve_h = r.cnt["VE"]
                r.wait("tensor", "VE", ve_h)
                for it in range(ITL):
                    buf_d = it % 2
                    r.wait("sync", "PE", wd_free[buf_d])
                    wtok = r.dma("sync", wdb[buf_d][:], dwd[e, 128 * it:128 * (it + 1), :], "WD")
                    r.wait("tensor", "WD", wtok)
                    for tokt in range(2):
                        for dc in range(4):
                            bank = 4 * tokt + dc
                            if it == 0:
                                r.wait("tensor", "VE", ps_reader[bank])
                            pe = t_mm(PS[bank][:, 0:512],
                                      hT[:, it, 128 * tokt:128 * (tokt + 1)],
                                      wdb[buf_d][:, 512 * dc:512 * (dc + 1)],
                                      it == 0, it == ITL - 1,
                                      "PE" if (tokt == 1 and dc == 3) or it == ITL - 1 else None)
                            if it == ITL - 1:
                                ps_writer[bank] = pe
                    wd_free[buf_d] = pe
                for tokt in range(2):
                    r.wait("vector", "GD", ye_free)
                    for dc in range(4):
                        bank = 4 * tokt + dc
                        r.wait("vector", "PE", ps_writer[bank])
                        ve = v_ts(ye[:, 512 * dc:512 * (dc + 1)], PS[bank][:, 0:512],
                                  gtile[:, 2 * e + tokt:2 * e + tokt + 1], None, ALU.mult)
                        ps_reader[bank] = ve
                    r.wait("gpsimd", "VE", ve)
                    rowbase = CAP * e + 128 * tokt
                    ye_free = r.dma("gpsimd", dysort[rowbase:rowbase + 128, :], ye[:])
            toks[("experts", call)] = r.cnt["GD"]

        def emit_return(RA, EF, call):
            posw = RA["posw"]
            Y0 = EF(nc.sbuf_tensor(f"Y0{call}", [128, 2, D], F32))
            Y1 = EF(nc.sbuf_tensor(f"Y1{call}", [128, 2, D], F32))
            for cch in range(32):
                gd0 = r.dma("gpsimd", dysort[DUMPPOS:DUMPPOS + 128, 64 * cch:64 * (cch + 1)],
                            zero2k[:])
            r.wait("gpsimd", "GD", toks[("experts", call)])
            pe_last = 0
            for j in range(4):
                r.wait("gpsimd", "PE", pe_last)
                g0 = r.op("gpsimd", lambda EE, j=j: EE.dma_gather(
                    Y0[:], dysort, posw[:, 16 * j:16 * (j + 1)],
                    num_idxs=CAP, num_idxs_reg=CAP, elem_size=D, transpose=False),
                    "GD", 16)
                g1 = r.op("gpsimd", lambda EE, j=j: EE.dma_gather(
                    Y1[:], dysort, posw[:, 64 + 16 * j:64 + 16 * (j + 1)],
                    num_idxs=CAP, num_idxs_reg=CAP, elem_size=D, transpose=False),
                    "GD", 16)
                r.wait("vector", "GD", g1)
                ve = v_tt(Y0[:], Y0[:], Y1[:], ALU.add)
                r.wait("tensor", "VE", ve)
                for c in range(2):
                    for k in range(DT):
                        b = 6 + (k % 2)
                        r.wait("tensor", "VE", ps_reader[b])
                        pe = t_tp(PS[b][:, 0:128], Y0[:, c, 128 * k:128 * (k + 1)])
                        ps_writer[b] = pe
                        r.wait("vector", "PE", pe)
                        tsl = slice(256 * j + 128 * c, 256 * j + 128 * (c + 1))
                        ve = v_stt(rT[:, k, tsl], PS[b][:, 0:128], 0.5, rT[:, k, tsl],
                                   ALU.mult, ALU.add)
                        ps_reader[b] = ve
                pe_last = pe

        def emit_ffn(call):
            with ExitStack() as FX:
                EF = FX.enter_context
                RA = alloc_routing(EF, call) if stage >= 2 else None
                with ExitStack() as AX2:
                    EA = AX2.enter_context
                    zT = EA(nc.sbuf_tensor(f"zT{call}", [128, DT, TLOC], F32))
                    with ExitStack() as LX2:
                        EL = LX2.enter_context
                        meanB = EL(nc.sbuf_tensor(f"meanB{call}", [128, TLOC], F32))
                        rstdB = EL(nc.sbuf_tensor(f"rstdB{call}", [128, TLOC], F32))
                        sc = tuple(EL(nc.sbuf_tensor(f"lnrow{call}_{i}", [1, 512], F32))
                                   for i in range(2))
                        emit_ln(zT, meanB, rstdB, sc)
                        if debug and stage <= 1 and call == 0:
                            ve = r.cnt["VE"]
                            r.wait("gpsimd", "VE", ve)
                            r.dma("gpsimd", dbg["z1"], zT.ap().rearrange("p a b -> p (a b)"), "ST")
                            r.barrier_now()
                            r.flush()
                            return
                        if stage >= 2 and stage < 3:
                            emit_routing(zT, RA, call)
                        if stage >= 3:
                            emit_routing(zT, RA, call, part=1)
                        r.barrier_now()
                    if stage >= 3:
                        with ExitStack() as SPX:
                            emit_spill_z(zT, SPX.enter_context, call)
                            emit_routing(zT, RA, call, part=2)
                            r.barrier_now()
                        emit_shared(zT, EA, call)
                        emit_routing3(RA, call)
                    r.barrier_now()
                    r.flush()
                if stage >= 4:
                    with ExitStack() as BX2:
                        emit_experts(RA, BX2.enter_context, call)
                        r.barrier_now()
                        r.flush()
                    with ExitStack() as CX2:
                        emit_return(RA, CX2.enter_context, call)
                    r.barrier_now()
                    r.flush()

        def emit_mamba():
            with ExitStack() as MX:
                EM = MX.enter_context
                deltaT = EM(nc.sbuf_tensor("deltaT", [128, DT, TLOC], F32))
                z2h = EM(nc.sbuf_tensor("z2h", [128, DT, 512], F32))
                bcT = EM(nc.sbuf_tensor("bcT", [32, TLOC], F32))
                aneg = EM(nc.sbuf_tensor("aneg", [128, DT, DST], F32))
                dv_s = EM(nc.sbuf_tensor("dv_s", [128, DT], F32))
                sd = EM(nc.sbuf_tensor("sd", [128, DT], F32))
                selb = EM(nc.sbuf_tensor("selb", [32, 2, 128], F32))
                sel_i = EM(nc.sbuf_tensor("sel_i", [32, 128], I32))
                brep = EM(nc.sbuf_tensor("brep", [128, 512], F32))
                crep = EM(nc.sbuf_tensor("crep", [128, 512], F32))
                da = EM(nc.sbuf_tensor("da", [128, 512], F32))
                bx = EM(nc.sbuf_tensor("bx", [128, 512], F32))
                hsc = EM(nc.sbuf_tensor("hsc", [128, 512], F32))
                hend = EM(nc.sbuf_tensor("hend", [128, 256], F32))
                PPack = EM(nc.sbuf_tensor("PPack", [128, 256], F32))
                hinit = EM(nc.sbuf_tensor("hinit", [128, 256], F32))
                statev = EM(nc.sbuf_tensor("statev", [128, 256], F32))
                rankB = EM(nc.sbuf_tensor("rankB", [128, 1], F32))
                mcol = EM(nc.sbuf_tensor("mcol", [128, 1], F32))

                v_memset(sd[:], 0.0)
                v_memset(hinit[:], 0.0)
                v_memset(statev[:], 0.0)
                r.op("gpsimd", lambda EE: EE.iota(sel_i[:], pattern=[[0, 128]], base=0,
                                                  channel_multiplier=1), "GE")
                ge = r.cnt["GE"]
                r.wait("gpsimd", "IN", in_cid)
                gr = r.op("gpsimd", lambda EE: EE.partition_broadcast(rankB[:], cid_s[0:1, :]), "GE")

                def build_sel(s):
                    r.wait("vector", "GE", ge)
                    v_ts(selb[:, 0, :], sel_i[:], float(s), None, ALU.is_equal)
                    return v_ts(selb[:, 1, :], sel_i[:], float(16 + s), None, ALU.is_equal)

                with ExitStack() as WX:
                    EW = WX.enter_context
                    wdbcT_s = EW(nc.sbuf_tensor("wdbcT_s", [128, DT, 160], F32))
                    wdtT_b = EW(nc.sbuf_tensor("wdtT_b", [128, 128], F32))
                    bdlo_s = EW(nc.sbuf_tensor("bdlo_s", [128, 1], F32))
                    bbc_s = EW(nc.sbuf_tensor("bbc_s", [32, 1], F32))
                    bdt_s = EW(nc.sbuf_tensor("bdt_s", [128, DT], F32))
                    meanB = EW(nc.sbuf_tensor("meanB_m", [128, TLOC], F32))
                    rstdB = EW(nc.sbuf_tensor("rstdB_m", [128, TLOC], F32))
                    sc = tuple(EW(nc.sbuf_tensor(f"lnrow_m_{i}", [1, 512], F32))
                               for i in range(2))
                    dtmp = brep
                    dtmp2 = bx

                    l1 = r.dma("sync", wdbcT_s[:], dwdbcT.rearrange("(k p) c -> p k c", p=128))
                    l3 = r.dma("sync", bdlo_s[:], dbdlo)
                    l4 = r.dma("sync", bbc_s[:], dbbc)
                    l5 = r.dma("sync", bdt_s[:], dbdt)
                    l6 = r.dma("sync", aneg[:], dalog)
                    l7 = r.dma("sync", dv_s[:], ddvec)
                    r.wait("scalar", "IN", l7)
                    se = s_act(aneg[:], aneg[:], AF.Exp)
                    r.wait("vector", "SE", se)
                    v_ts(aneg[:], aneg[:], -1.0, None, ALU.mult)
                    r.wait("vector", "IN", l7)
                    emit_ln(None, meanB, rstdB, sc, make_z=False, sq=deltaT)

                    wdt_free = 0
                    for h in range(2):
                        S = slice(512 * h, 512 * (h + 1))
                        for k in range(DT):
                            v_tt(z2h[:, k, :], rT[:, k, S], meanB[:, S], ALU.subtract)
                            v_tt(z2h[:, k, :], z2h[:, k, :], rstdB[:, S], ALU.mult)
                        ve_z2 = r.cnt["VE"]
                        r.wait("tensor", "VE", ve_z2)
                        mm_group(0, 512, [wdbcT_s[:, k, 0:128] for k in range(DT)],
                                 [z2h[:, k, :] for k in range(DT)])
                        def _dlo(dst, ps):
                            return v_ts(dst, ps, bdlo_s[:], None, ALU.add)
                        evac(0, dtmp[:], 512, _dlo)
                        r.wait("tensor", "VE", ps_reader[1])
                        for k in range(DT):
                            pe = t_mm(PS[1][0:32, 0:512], wdbcT_s[:, k, 128:160],
                                      z2h[:, k, :], k == 0, k == DT - 1,
                                      "PE" if k == DT - 1 else None)
                        ps_writer[1] = pe
                        r.wait("vector", "PE", ps_writer[1])
                        ve = r.op("vector", lambda EE, S=S: EE.tensor_scalar(
                            bcT[:, S], PS[1][0:32, 0:512], bbc_s[:], None, op0=ALU.add), "VE")
                        ps_reader[1] = ve
                        ve_dlo = r.cnt["VE"]
                        r.wait("tensor", "VE", ve_dlo)
                        for k in range(DT):
                            bank = 2 + (k % 2)
                            r.wait("sync", "PE", wdt_free)
                            wtok = r.dma("sync", wdtT_b[:], dwdtT[:, 128 * k:128 * (k + 1)], "WD")
                            r.wait("tensor", "WD", wtok)
                            r.wait("tensor", "VE", ps_reader[bank])
                            pe = t_mm(PS[bank][:, 0:512], wdtT_b[:],
                                      dtmp[:], True, True, "PE")
                            wdt_free = pe
                            ps_writer[bank] = pe
                            r.wait("vector", "PE", pe)
                            ve = v_ts(dtmp2[:], PS[bank][:, 0:512], bdt_s[:, k:k + 1], None, ALU.add)
                            ps_reader[bank] = ve
                            v_ts(hsc[:], dtmp2[:], -1.0, None, ALU.mult)
                            ve = v_tt(hsc[:], dtmp2[:], hsc[:], ALU.min)
                            r.wait("scalar", "VE", ve)
                            se = s_act(da[:], hsc[:], AF.Exp)
                            se = s_act(da[:], da[:], AF.Ln, bias=onecol[:])
                            r.wait("vector", "SE", se)
                            v_ts(dtmp2[:], dtmp2[:], 0.0, None, ALU.max)
                            v_tt(deltaT[:, k, S], dtmp2[:], da[:], ALU.add)
                            v_stt(rT[:, k, S], z2h[:, k, :], dv_s[:, k:k + 1], rT[:, k, S],
                                  ALU.mult, ALU.add)
                            v_red(mcol[:], deltaT[:, k, S], ALU.add)
                            v_tt(sd[:, k:k + 1], sd[:, k:k + 1], mcol[:], ALU.add)
                        ve_pre = r.cnt["VE"]
                        r.wait("tensor", "VE", ve_pre)
                        for s in range(DST):
                            ve_sel = build_sel(s)
                            r.wait("tensor", "VE", ve_sel)
                            for which, dst in ((0, brep), (1, crep)):
                                bank = 4 + which
                                r.wait("tensor", "VE", ps_reader[bank])
                                pe = t_mm(PS[bank][:, 0:512], selb[:, which, :], bcT[:, S],
                                          True, True, "PE")
                                ps_writer[bank] = pe
                                evac(bank, dst[:], 512)
                            for k in range(DT):
                                col = k * 16 + s
                                r.wait("scalar", "VE", r.cnt["VE"])
                                se = s_act(da[:], deltaT[:, k, S], AF.Exp,
                                           scale=aneg[:, k, s:s + 1])
                                r.wait("vector", "SE", se)
                                v_tt(bx[:], deltaT[:, k, S], z2h[:, k, :], ALU.mult)
                                v_tt(bx[:], bx[:], brep[:], ALU.mult)
                                if h == 0:
                                    v_scan(hsc[:], da[:], bx[:], 0.0)
                                else:
                                    v_scan(hsc[:], da[:], bx[:], hend[:, col:col + 1])
                                v_copy(hend[:, col:col + 1], hsc[:, 511:512])
                                v_tt(bx[:], hsc[:], crep[:], ALU.mult)
                                ve = v_tt(rT[:, k, S], rT[:, k, S], bx[:], ALU.add)
                            r.wait("scalar", "VE", r.cnt["VE"])
                    r.barrier_now()
                    if debug and stage == 5:
                        r.dma("gpsimd", dbg["z1"], deltaT.ap().rearrange("p a b -> p (a b)"), "ST")
                        r.dma("gpsimd", dbg["bc"], bcT[:], "ST")
                        r.dma("gpsimd", dbg["mam"][:, 0:256], hend[:], "ST")
                        r.barrier_now()
                    r.flush()

                with ExitStack() as CX:
                    EC = CX.enter_context
                    crepF = EC(nc.sbuf_tensor("crepF", [128, TLOC], F32))
                    ctmp = EC(nc.sbuf_tensor("ctmp", [128, TLOC], F32))
                    gath1 = EC(nc.sbuf_tensor("gath1", [128, 512], F32))

                    for k in range(DT):
                        r.wait("scalar", "VE", r.cnt["VE"])
                        se = s_act(PPack[:, 16 * k:16 * (k + 1)], aneg[:, k, :], AF.Exp,
                                   scale=sd[:, k:k + 1])
                    r.wait("gpsimd", "SE", se)
                    r.wait("gpsimd", "VE", r.cnt["VE"])
                    r.dma("gpsimd", dccin[:, 0:256], PPack[:])
                    gd = r.dma("gpsimd", dccin[:, 256:512], hend[:])
                    r.wait("gpsimd", "GD", gd)
                    cc = r.op("gpsimd", lambda EE: EE.collective_compute(
                        "AllGather", ALU.bypass, replica_groups=[list(range(NCORES))],
                        ins=[dccin.opt()], outs=[dccout.opt()]), "CC")
                    r.wait("gpsimd", "CC", cc)
                    r.wait("vector", "GE", gr)
                    r.wait("vector", "CC", cc)
                    for j in range(NCORES):
                        gd = r.dma("gpsimd", gath1[:], dccout[128 * j:128 * (j + 1), :])
                        r.wait("vector", "GD", gd)
                        v_ts(mcol[:], rankB[:], float(j), None, ALU.is_equal)
                        r.wait("vector", "VE", r.cnt["VE"])
                        v_stt(hinit[:], statev[:], mcol[:], hinit[:], ALU.mult, ALU.add)
                        v_tt(statev[:], gath1[:, 0:256], statev[:], ALU.mult)
                        ve = v_tt(statev[:], statev[:], gath1[:, 256:512], ALU.add)
                        r.wait("gpsimd", "VE", ve)
                    if debug and stage == 5:
                        r.wait("gpsimd", "VE", r.cnt["VE"])
                        r.dma("gpsimd", dbg["mam"][:, 256:512], PPack[:], "ST")
                        r.dma("gpsimd", dbg["mam"][:, 512:768], hinit[:], "ST")
                    v_memset(ctmp[:], 1.0)
                    for k in range(DT):
                        v_scan(deltaT[:, k, :], ctmp[:], deltaT[:, k, :], 0.0)
                    ve_cs = r.cnt["VE"]
                    r.wait("tensor", "VE", ve_cs)
                    for s in range(DST):
                        ve_sel = build_sel(s)
                        r.wait("tensor", "VE", ve_sel)
                        for half in range(2):
                            bank = 6 + half
                            r.wait("tensor", "VE", ps_reader[bank])
                            pe = t_mm(PS[bank][:, 0:512], selb[:, 1, :],
                                      bcT[:, 512 * half:512 * (half + 1)], True, True, "PE")
                            ps_writer[bank] = pe
                            evac(bank, crepF[:, 512 * half:512 * (half + 1)], 512)
                        for k in range(DT):
                            r.wait("scalar", "VE", r.cnt["VE"])
                            se = s_act(ctmp[:], deltaT[:, k, :], AF.Exp,
                                       scale=aneg[:, k, s:s + 1])
                            r.wait("vector", "SE", se)
                            v_stt(ctmp[:], ctmp[:], hinit[:, 16 * k + s:16 * k + s + 1],
                                  crepF[:], ALU.mult, ALU.mult)
                            ve = v_tt(rT[:, k, :], rT[:, k, :], ctmp[:], ALU.add)
                            r.wait("scalar", "VE", ve)
                    r.barrier_now()
                    r.flush()

        emit_ffn(0)
        if stage >= 5:
            emit_mamba()
        if stage >= 6:
            emit_ffn(1)

        with ExitStack() as OX:
            otm = OX.enter_context(nc.sbuf_tensor("otm", [128, D], F32))
            for a in range(8):
                for k in range(DT):
                    b = k % 4
                    r.wait("tensor", "VE", ps_reader[b])
                    pe = t_tp(PS[b][:, 0:128], rT[:, k, 128 * a:128 * (a + 1)])
                    ps_writer[b] = pe
                    ve = evac(b, otm[:, 128 * k:128 * (k + 1)], 128)
                r.wait("gpsimd", "VE", ve)
                st = r.dma("gpsimd", dout[128 * a:128 * (a + 1), :], otm[:], "ST")
                r.wait("vector", "ST", st)
            r.wait("gpsimd", "ST", r.cnt["ST"])
            r.flush()

    return nc



def _pack_lhsT(w):
    Dd, M = w.shape
    return np.ascontiguousarray(
        w.reshape(Dd // 128, 128, M // 128, 128).transpose(2, 1, 0, 3))


def prep_inputs(inp):
    x = np.asarray(inp["L0_attn_x"], dtype=np.float32).reshape(8192, 2048)
    W_dbc = np.asarray(inp["L0_attn_W_dbc"], np.float32)
    b_dbc = np.asarray(inp["L0_attn_b_dbc"], np.float32)
    W_dt = np.asarray(inp["L0_attn_W_dt"], np.float32)
    b_dt = np.asarray(inp["L0_attn_b_dt"], np.float32)
    A_log = np.asarray(inp["L0_attn_A_log"], np.float32)
    Dv = np.asarray(inp["L0_attn_D"], np.float32)
    gate_w = np.asarray(inp["L0_ffn_gate_w"], np.float32)
    w_gate_up = np.asarray(inp["L0_ffn_w_gate_up"], np.float32)
    w_down = np.asarray(inp["L0_ffn_w_down"], np.float32)
    sgu = np.asarray(inp["L0_ffn_shared_gate_up"], np.float32)
    sdn = np.asarray(inp["L0_ffn_shared_down"], np.float32)

    common = dict(
        wsg=_pack_lhsT(sgu),
        wsd=np.ascontiguousarray(sdn),
        wg=np.stack([_pack_lhsT(w_gate_up[e]) for e in range(E)]),
        wd=np.ascontiguousarray(w_down),
        gwT=np.ascontiguousarray(gate_w.T),
        wdbcT=np.ascontiguousarray(W_dbc.T),
        wdtT=np.ascontiguousarray(W_dt.T),
        bdlo=np.ascontiguousarray(b_dbc[:128].reshape(128, 1)),
        bbc=np.ascontiguousarray(b_dbc[128:].reshape(32, 1)),
        bdt=np.ascontiguousarray(b_dt.reshape(DT, 128).T),
        alog=np.ascontiguousarray(A_log.reshape(DT, 128, DST).transpose(1, 0, 2)),
        dvec=np.ascontiguousarray(Dv.reshape(DT, 128).T),
    )
    maps = []
    for c in range(NCORES):
        m = dict(common)
        m["x"] = np.ascontiguousarray(x[TLOC * c:TLOC * (c + 1)])
        m["cid"] = np.array([[float(c)]], np.float32)
        maps.append(m)
    return maps


def kernel(**inputs):
    maps = prep_inputs(inputs)
    nc = build_program(stage=6, debug=False)
    nc.compile()
    res = run_bass_kernel_spmd(nc, maps, core_ids=list(range(NCORES)))
    kernel.last_results = res
    out = np.concatenate([res.results[c]["out"] for c in range(NCORES)], axis=0)
    return out.reshape(1, 8192, 2048)


# version 03
# speedup vs baseline: 1.1945x; 1.0207x over previous
import os
import numpy as np
from contextlib import ExitStack

import concourse.bacc as bacc
import concourse.bass as bass
import concourse.mybir as mybir
from concourse.bass_utils import run_bass_kernel_spmd

F32 = mybir.dt.float32
BF16 = mybir.dt.bfloat16
I16 = mybir.dt.int16
I32 = mybir.dt.int32
U32 = mybir.dt.uint32
AF = mybir.ActivationFunctionType
ALU = mybir.AluOpType
AX = mybir.AxisListType.X

NCORES = 8
TLOC = 1024
D = 2048
DT = D // 128
E = 16
CAP = 256
II = 1408
ITL = II // 128
SI = 2816
SIT = SI // 128
DST = 16
NSORT = E * CAP + 128
DUMPPOS = E * CAP
EPS = 1e-5

ENGINES = ("sync", "tensor", "vector", "scalar", "gpsimd")


class Rec:
    def __init__(self, nc, ctx, block):
        self.nc = nc
        self.block = block
        self.ops = {e: [] for e in ENGINES}
        self.semh = {}
        self.cnt = {}
        for name in ("IN", "WD", "GD", "PE", "VE", "SE", "GE", "CC", "ST"):
            self.semh[name] = ctx.enter_context(nc.semaphore(name))
            self.cnt[name] = 0
        self.barrier = dict(self.cnt)

    VSERIAL = True

    def op(self, eng, builder, sem=None, n=1):
        if sem is None and eng == "vector" and self.VSERIAL:
            sem = "VE"
        if sem is None:
            self.ops[eng].append(builder)
            return None
        h = self.semh[sem]
        self.cnt[sem] += n

        def f(EE, b=builder, h=h, n=n):
            b(EE).then_inc(h, n)
        self.ops[eng].append(f)
        tok = self.cnt[sem]
        if eng == "vector" and sem == "VE" and self.VSERIAL:
            self.ops[eng].append(lambda EE, h=h, t=tok: EE.wait_ge(h, t))
        return tok

    def wait(self, eng, sem, tok):
        if not tok:
            return
        h = self.semh[sem]
        self.ops[eng].append(lambda EE, h=h, t=tok: EE.wait_ge(h, t))

    def dma(self, eng, out, in_, sem=None):
        if sem is None:
            sem = "GD" if eng == "gpsimd" else ("IN" if eng == "sync" else "WD")
        return self.op(eng, lambda EE, o=out, i=in_: EE.dma_start(out=o, in_=i), sem, 16)

    def dma_nc(self, eng, out, in_, sem=None):
        if sem is None:
            sem = "GD" if eng == "gpsimd" else ("IN" if eng == "sync" else "WD")
        return self.op(eng, lambda EE, o=out, i=in_: EE.dma_start(
            out=o, in_=i, allow_slow_non_contiguous=True), sem, 16)

    def flush(self):
        for eng in ENGINES:
            ops = self.ops[eng]
            if not ops:
                continue

            def body(EE, ops=list(ops)):
                for f in ops:
                    f(EE)
            getattr(self.block, eng)(body)
            self.ops[eng] = []

    def barrier_now(self):
        toks = dict(self.cnt)
        for eng in ENGINES:
            for s, t in toks.items():
                if t > self.barrier.get(s, 0):
                    self.wait(eng, s, t)
        self.barrier = toks


def build_program(stage=6, debug=False):
    nc = bacc.Bacc("TRN2", target_bir_lowering=False, debug=False)

    dx = nc.dram_tensor("x", [TLOC, D], F32, kind="ExternalInput").ap()
    dcid = nc.dram_tensor("cid", [1, 1], F32, kind="ExternalInput").ap()
    dwsg = nc.dram_tensor("wsg", [2 * SIT, 128, DT, 128], F32, kind="ExternalInput").ap()
    dwsd = nc.dram_tensor("wsd", [SI, D], F32, kind="ExternalInput").ap()
    dwg = nc.dram_tensor("wg", [E, 2 * ITL, 128, DT, 128], F32, kind="ExternalInput").ap()
    dwd = nc.dram_tensor("wd", [E, II, D], F32, kind="ExternalInput").ap()
    dgwT = nc.dram_tensor("gwT", [D, E], F32, kind="ExternalInput").ap()
    dwdbcT = nc.dram_tensor("wdbcT", [D, 160], F32, kind="ExternalInput").ap()
    dwdtT = nc.dram_tensor("wdtT", [128, D], F32, kind="ExternalInput").ap()
    dbdlo = nc.dram_tensor("bdlo", [128, 1], F32, kind="ExternalInput").ap()
    dbbc = nc.dram_tensor("bbc", [32, 1], F32, kind="ExternalInput").ap()
    dbdt = nc.dram_tensor("bdt", [128, DT], F32, kind="ExternalInput").ap()
    dalog = nc.dram_tensor("alog", [128, DT, DST], F32, kind="ExternalInput").ap()
    ddvec = nc.dram_tensor("dvec", [128, DT], F32, kind="ExternalInput").ap()

    dout = nc.dram_tensor("out", [TLOC, D], F32, kind="ExternalOutput").ap()

    dzsp = nc.dram_tensor("zsp", [TLOC, D], F32).ap()
    dysort = nc.dram_tensor("ysort", [NSORT, D], F32).ap()
    didlin = nc.dram_tensor("idlin", [NSORT, 1], F32).ap()
    dglin = nc.dram_tensor("glin", [NSORT, 1], F32).ap()
    dposlin = nc.dram_tensor("poslin", [2 * TLOC, 1], F32).ap()
    dccin = nc.dram_tensor("ccin", [128, 512], F32).ap()
    dccout = nc.dram_tensor("ccout", [128 * NCORES, 512], F32).ap()

    dbg = {}
    if debug:
        dbg["z1"] = nc.dram_tensor("dbg_z1", [128, DT * TLOC], F32, kind="ExternalOutput").ap()
        dbg["route"] = nc.dram_tensor("dbg_route", [128, 64], F32, kind="ExternalOutput").ap()
        dbg["pos"] = nc.dram_tensor("dbg_pos", [128, 16], F32, kind="ExternalOutput").ap()
        dbg["idl"] = nc.dram_tensor("dbg_idl", [128, 33], F32, kind="ExternalOutput").ap()
        dbg["gl"] = nc.dram_tensor("dbg_gl", [128, 33], F32, kind="ExternalOutput").ap()
        dbg["snap"] = nc.dram_tensor("dbg_snap", [128, 80], F32, kind="ExternalOutput").ap()
        dbg["mam"] = nc.dram_tensor("dbg_mam", [128, 768], F32, kind="ExternalOutput").ap()
        dbg["bc"] = nc.dram_tensor("dbg_bc", [32, TLOC], F32, kind="ExternalOutput").ap()

    with ExitStack() as CTX:
        EN = CTX.enter_context
        block = EN(nc.Block())
        r = Rec(nc, CTX, block)

        rT = EN(nc.sbuf_tensor("rT", [128, DT, TLOC], F32))
        ident = EN(nc.sbuf_tensor("ident", [128, 128], F32))
        triUS = EN(nc.sbuf_tensor("triUS", [128, 128], F32))
        allone = EN(nc.sbuf_tensor("allone", [128, 128], F32))
        onecol = EN(nc.sbuf_tensor("onecol", [128, 1], F32))
        iota16f = EN(nc.sbuf_tensor("iota16f", [128, 16], F32))
        baseT = EN(nc.sbuf_tensor("baseT", [128, 16], F32))
        pid_f = EN(nc.sbuf_tensor("pid_f", [128, 1], F32))
        gwT_s = EN(nc.sbuf_tensor("gwT_s", [128, DT, E], F32))
        cid_s = EN(nc.sbuf_tensor("cid_s", [1, 1], F32))
        zero2k = EN(nc.sbuf_tensor("zero2k", [128, 64], F32))

        PS = [EN(nc.psum_tensor(f"ps{b}", [128, 512], F32)) for b in range(8)]
        ps_writer = [0] * 8
        ps_reader = [0] * 8

        toks = {}

        def v_ts(out, in0, s1, s2, op0, op1=None):
            if op1 is None:
                return r.op("vector", lambda EE, o=out, i=in0, a=s1, b=s2, p=op0:
                            EE.tensor_scalar(o, i, a, b, op0=p), "VE")
            return r.op("vector", lambda EE, o=out, i=in0, a=s1, b=s2, p=op0, q=op1:
                        EE.tensor_scalar(o, i, a, b, op0=p, op1=q), "VE")

        def v_tt(out, in0, in1, opk):
            return r.op("vector", lambda EE, o=out, a=in0, b=in1, p=opk:
                        EE.tensor_tensor(o, a, b, op=p), "VE")

        def v_stt(out, in0, scal, in1, op0, op1):
            return r.op("vector", lambda EE, o=out, a=in0, s=scal, b=in1, p=op0, q=op1:
                        EE.scalar_tensor_tensor(o, a, s, b, op0=p, op1=q), "VE")

        def v_copy(out, in_):
            return r.op("vector", lambda EE, o=out, i=in_: EE.tensor_copy(o, i), "VE")

        def v_red(out, in_, opk):
            r.op("vector", lambda EE: EE.nop())
            return r.op("vector", lambda EE, o=out, i=in_, p=opk:
                        EE.tensor_reduce(o, i, axis=AX, op=p), "VE")

        def v_memset(ap_, val):
            return r.op("vector", lambda EE, o=ap_, v=val: EE.memset(o, v), "VE")

        def v_recip(out, in_):
            r.op("vector", lambda EE: EE.nop())
            return r.op("vector", lambda EE, o=out, i=in_: EE.reciprocal(o, i), "VE")

        def v_scan(out, d0, d1, init, op0=ALU.mult, op1=ALU.add):
            r.op("vector", lambda EE: EE.nop())
            return r.op("vector", lambda EE, o=out, a=d0, b=d1, i=init, p=op0, q=op1:
                        EE.tensor_tensor_scan(o, a, b, i, op0=p, op1=q), "VE")

        def s_act(out, in_, func, bias=None, scale=None):
            def f(EE, o=out, i=in_, fn=func, b=bias, s=scale):
                kw = {}
                if b is not None:
                    kw["bias"] = b
                if s is not None:
                    kw["scale"] = s
                return EE.activation(o, i, fn, **kw)
            return r.op("scalar", f, "SE")

        def t_mm(out, lhsT, rhs, start, stop, sem=None):
            return r.op("tensor", lambda EE, o=out, l=lhsT, rr=rhs, s=start, t=stop:
                        EE.matmul(o, l, rr, start=s, stop=t), sem)

        def t_tp(out, in_, sem="PE"):
            return r.op("tensor", lambda EE, o=out, i=in_, d=ident[:]:
                        EE.transpose(o, i, d), sem)

        def mm_group(bank, width, lhsT_list, rhs_list):
            r.wait("tensor", "VE", ps_reader[bank])
            n = len(lhsT_list)
            pe = None
            for i, (lt, rh) in enumerate(zip(lhsT_list, rhs_list)):
                pe = t_mm(PS[bank][:, 0:width], lt, rh, i == 0, i == n - 1,
                          "PE" if i == n - 1 else None)
            ps_writer[bank] = pe
            return pe

        def evac(bank, dst, width, fn=None):
            r.wait("vector", "PE", ps_writer[bank])
            if fn is None:
                ve = v_copy(dst, PS[bank][:, 0:width])
            else:
                ve = fn(dst, PS[bank][:, 0:width])
            ps_reader[bank] = ve
            return ve

        scr_i = EN(nc.sbuf_tensor("scr_i", [128, 128], I32))
        p0ctx = ExitStack()
        xtm = p0ctx.enter_context(nc.sbuf_tensor("xtm", [128, 8, D], F32))

        idn_i = p0ctx.enter_context(nc.sbuf_tensor("idn_i", [128, 128], I32))
        r.op("gpsimd", lambda EE: EE.iota(scr_i[:, 0:16], pattern=[[1, 16]], base=0,
                                          channel_multiplier=0), "GE")
        r.op("gpsimd", lambda EE: EE.iota(scr_i[:, 16:17], pattern=[[0, 1]], base=0,
                                          channel_multiplier=1), "GE")
        r.op("gpsimd", lambda EE: EE.iota(idn_i[:], pattern=[[1, 128]], base=0,
                                          channel_multiplier=-1), "GE")
        ge3 = r.cnt["GE"]
        r.wait("vector", "GE", ge3)
        v_copy(iota16f[:], scr_i[:, 0:16])
        v_ts(baseT[:], iota16f[:], float(CAP), None, ALU.mult)
        v_copy(pid_f[:], scr_i[:, 16:17])
        v_memset(onecol[:], 1.0)
        v_memset(allone[:], 1.0)
        v_memset(zero2k[:], 0.0)
        v_ts(ident[:], idn_i[:], 0.0, None, ALU.is_equal)
        tv = v_ts(triUS[:], idn_i[:], 0.0, None, ALU.is_gt)

        in_x = r.dma("sync", xtm[:], dx.rearrange("(a p) d -> p a d", p=128))
        in_gw = r.dma("sync", gwT_s[:], dgwT.rearrange("(k p) e -> p k e", p=128))
        in_cid = r.dma("sync", cid_s[:], dcid)

        r.wait("tensor", "IN", in_x)
        r.wait("tensor", "VE", tv)
        for a in range(8):
            for k in range(DT):
                b = (a * DT + k) % 4
                r.wait("tensor", "VE", ps_reader[b])
                pe = t_tp(PS[b][:, 0:128], xtm[:, a, 128 * k:128 * (k + 1)])
                ps_writer[b] = pe
                evac(b, rT[:, k, 128 * a:128 * (a + 1)], 128)
        r.barrier_now()
        r.flush()
        p0ctx.close()

        def emit_ln(zT, meanB, rstdB, sc, make_z=True, sq=None):
            rows, rows2 = sc
            if sq is None:
                sq = zT
            for k in range(DT):
                v_tt(sq[:, k, :], rT[:, k, :], rT[:, k, :], ALU.mult)
            ve_sq = r.cnt["VE"]
            r.wait("tensor", "VE", ve_sq)
            for half in range(2):
                sl = slice(512 * half, 512 * (half + 1))
                m01, r01 = meanB[0:1, sl], rstdB[0:1, sl]
                for b, src, dstrow in ((0, rT, rows), (1, sq, rows2)):
                    r.wait("tensor", "VE", ps_reader[b])
                    for k in range(DT):
                        pe = t_mm(PS[b][0:1, 0:512], onecol[:], src[:, k, sl],
                                  k == 0, k == DT - 1, "PE" if k == DT - 1 else None)
                    ps_writer[b] = pe
                    r.wait("vector", "PE", ps_writer[b])
                    ve = v_copy(dstrow[:], PS[b][0:1, 0:512])
                    ps_reader[b] = ve
                v_ts(m01, rows[:], 1.0 / D, None, ALU.mult)
                v_ts(rows2[:], rows2[:], 1.0 / D, None, ALU.mult)
                v_tt(rows[:], m01, m01, ALU.mult)
                v_tt(rows2[:], rows2[:], rows[:], ALU.subtract)
                v_ts(rows2[:], rows2[:], EPS, None, ALU.add)
                ve = v_recip(rows[:], rows2[:])
                r.wait("scalar", "VE", ve)
                se = s_act(rows[:], rows[:], AF.Sqrt)
                r.wait("vector", "SE", se)
                v_tt(r01, rows[:], rows[:], ALU.mult)
                v_tt(r01, r01, rows2[:], ALU.mult)
                v_ts(r01, r01, -0.5, 1.5, ALU.mult, ALU.add)
                ve = v_tt(r01, rows[:], r01, ALU.mult)
            r.wait("gpsimd", "VE", ve)
            r.op("gpsimd", lambda EE, o=meanB: EE.partition_broadcast(o[:], o[0:1, :]), "GE")
            g2 = r.op("gpsimd", lambda EE, o=rstdB: EE.partition_broadcast(o[:], o[0:1, :]), "GE")
            r.wait("vector", "GE", g2)
            if make_z:
                for k in range(DT):
                    v_tt(zT[:, k, :], rT[:, k, :], meanB[:], ALU.subtract)
                    v_tt(zT[:, k, :], zT[:, k, :], rstdB[:], ALU.mult)

        def alloc_routing(EF, call):
            d = {}
            d["lg"] = EF(nc.sbuf_tensor(f"lg{call}", [128, 8, E], F32))
            d["mx8"] = EF(nc.sbuf_tensor(f"mx8{call}", [128, 8, 8], F32))
            d["mi8"] = EF(nc.sbuf_tensor(f"mi8{call}", [128, 8, 8], F32))
            d["wk"] = EF(nc.sbuf_tensor(f"wk{call}", [128, 8, 2], F32))
            d["colv3"] = EF(nc.sbuf_tensor(f"colv{call}", [128, 8, 12], F32))
            d["lgf"] = EF(nc.sbuf_tensor(f"lgf{call}", [128, 16], F32))
            d["mxf"] = EF(nc.sbuf_tensor(f"mxf{call}", [128, 8], F32))
            d["mif"] = EF(nc.sbuf_tensor(f"mif{call}", [128, 8], U32))
            d["ohA"] = EF(nc.sbuf_tensor(f"ohA{call}", [128, 16, E], F32))
            d["Rrk"] = EF(nc.sbuf_tensor(f"Rrk{call}", [128, 16, E], F32))
            d["posg"] = EF(nc.sbuf_tensor(f"posg{call}", [128, 16, 2], F32))
            d["posI"] = EF(nc.sbuf_tensor(f"posI{call}", [128, 1], U32))
            d["posIv"] = EF(nc.sbuf_tensor(f"posIv{call}", [128, 16], U32))
            d["valv"] = EF(nc.sbuf_tensor(f"valv{call}", [128, 16], F32))
            d["poscp"] = EF(nc.sbuf_tensor(f"poscp{call}", [128, 16], F32))
            d["valf"] = EF(nc.sbuf_tensor(f"valf{call}", [128, 1], F32))
            d["scr"] = EF(nc.sbuf_tensor(f"pscr{call}", [128, E], F32))
            d["sidx_all"] = EF(nc.sbuf_tensor(f"sidx{call}", [128, NSORT // 16], I16))
            d["idw8"] = EF(nc.sbuf_tensor(f"idw8{call}", [128, NSORT // 16], F32))
            d["gtile"] = EF(nc.sbuf_tensor(f"gtile{call}", [128, NSORT // 128], F32))
            d["posw"] = EF(nc.sbuf_tensor(f"posw{call}", [128, 128], I16))
            d["poswf"] = EF(nc.sbuf_tensor(f"poswf{call}", [128, 128], F32))
            if debug:
                d["snap"] = EF(nc.sbuf_tensor(f"snap{call}", [128, 80], F32))
            return d

        def emit_routing(zT, RA, call, part=None):
            lg = RA["lg"]
            mx8, mi8, wk = RA["mx8"], RA["mi8"], RA["wk"]
            colv3, lgf, mxf, mif = RA["colv3"], RA["lgf"], RA["mxf"], RA["mif"]
            ohA, Rrk, posg, posI = RA["ohA"], RA["Rrk"], RA["posg"], RA["posI"]
            poscp, valf, scr = RA["poscp"], RA["valf"], RA["scr"]
            posIv, valv = RA["posIv"], RA["valv"]
            sidx_all, idw8, gtile = RA["sidx_all"], RA["idw8"], RA["gtile"]
            posw, poswf = RA["posw"], RA["poswf"]
            snap = RA.get("snap")

            if part != 2:
                ve_z = r.cnt["VE"]
                r.wait("tensor", "VE", ve_z)
                for tt in range(8):
                    b = tt % 2
                    mm_group(b, E, [zT[:, k, 128 * tt:128 * (tt + 1)] for k in range(DT)],
                             [gwT_s[:, k, :] for k in range(DT)])
                    evac(b, lg[:, tt, :], E)
            for tt in ([] if part == 2 else range(8)):
                colv = colv3[:, tt, :]
                v_red(colv[:, 0:1], lg[:, tt, :], ALU.max)
                r.wait("vector", "VE", r.cnt["VE"])
                v_ts(lgf[:], lg[:, tt, :], colv[:, 0:1], None, ALU.is_equal)
                v_tt(scr[:], lgf[:], iota16f[:], ALU.mult)
                v_ts(lgf[:], lgf[:], -99.0, 99.0, ALU.mult, ALU.add)
                v_tt(scr[:], scr[:], lgf[:], ALU.add)
                if debug and tt == 0:
                    v_copy(snap[:, 32:48], scr[:])
                v_red(colv[:, 1:2], scr[:], ALU.min)
                r.wait("vector", "VE", r.cnt["VE"])
                v_ts(ohA[:, tt, :], iota16f[:], colv[:, 1:2], None, ALU.is_equal)
                v_stt(lgf[:], ohA[:, tt, :], -1e30, lg[:, tt, :], ALU.mult, ALU.add)
                if debug and tt == 0:
                    v_copy(snap[:, 0:16], ohA[:, 0, :])
                    v_copy(snap[:, 16:32], lgf[:])
                    v_copy(snap[:, 64:80], lg[:, 0, :])
                v_red(colv[:, 2:3], lgf[:], ALU.max)
                r.wait("vector", "VE", r.cnt["VE"])
                v_ts(scr[:], lgf[:], colv[:, 2:3], None, ALU.is_equal)
                v_tt(mx8[:, tt, 0:1], colv[:, 0:1], colv[:, 0:1], ALU.min)
                v_copy(mx8[:, tt, 1:2], colv[:, 2:3])
                v_copy(mi8[:, tt, 0:1], colv[:, 1:2])
                v_tt(scr[:], scr[:], iota16f[:], ALU.mult)
                r.wait("vector", "VE", r.cnt["VE"])
                v_ts(lgf[:], lgf[:], colv[:, 2:3], None, ALU.is_equal)
                v_ts(lgf[:], lgf[:], -99.0, 99.0, ALU.mult, ALU.add)
                v_tt(scr[:], scr[:], lgf[:], ALU.add)
                v_red(colv[:, 3:4], scr[:], ALU.min)
                r.wait("vector", "VE", r.cnt["VE"])
                v_ts(ohA[:, 8 + tt, :], iota16f[:], colv[:, 3:4], None, ALU.is_equal)
                v_copy(mi8[:, tt, 1:2], colv[:, 3:4])
                ve = v_ts(colv[:, 4:5], colv[:, 0:1], -1.0, None, ALU.mult)
                r.wait("scalar", "VE", ve)
                se = s_act(lg[:, tt, :], lg[:, tt, :], AF.Exp, bias=colv[:, 4:5])
                r.wait("vector", "SE", se)
                if debug and tt == 0:
                    v_copy(snap[:, 48:64], lg[:, 0, :])
                v_red(colv[:, 5:6], lg[:, tt, :], ALU.add)
                v_recip(colv[:, 6:7], colv[:, 5:6])
                v_copy(wk[:, tt, 0:1], colv[:, 6:7])
                ve = v_tt(colv[:, 7:8], colv[:, 2:3], colv[:, 4:5], ALU.add)
                r.wait("scalar", "VE", ve)
                se = s_act(colv[:, 8:9], colv[:, 7:8], AF.Exp)
                r.wait("vector", "SE", se)
                v_tt(wk[:, tt, 1:2], colv[:, 8:9], colv[:, 6:7], ALU.mult)
            ve_oh = r.cnt["VE"]
            if part == 1:
                return

            r.wait("tensor", "VE", ve_oh)
            for c in range(16):
                b = 2 + (c % 2)
                lts = [allone[:]] * c + [triUS[:]]
                rhs = [ohA[:, cp, :] for cp in range(c)] + [ohA[:, c, :]]
                mm_group(b, E, lts, rhs)
                evac(b, Rrk[:, c, :], E)

            for c in range(16):
                tt, sl = c % 8, c // 8
                colv = colv3[:, tt, :]
                v_tt(scr[:], Rrk[:, c, :], ohA[:, c, :], ALU.mult)
                v_red(colv[:, 6:7], scr[:], ALU.add)
                v_tt(scr[:], baseT[:], ohA[:, c, :], ALU.mult)
                v_red(colv[:, 7:8], scr[:], ALU.add)
                v_ts(colv[:, 8:9], colv[:, 6:7], float(CAP), None, ALU.is_lt)
                v_tt(colv[:, 9:10], colv[:, 6:7], colv[:, 7:8], ALU.add)
                v_tt(colv[:, 9:10], colv[:, 9:10], colv[:, 8:9], ALU.mult)
                v_ts(colv[:, 10:11], colv[:, 8:9], -float(DUMPPOS), float(DUMPPOS),
                     ALU.mult, ALU.add)
                v_tt(posg[:, c, 0:1], colv[:, 9:10], colv[:, 10:11], ALU.add)
                v_tt(posg[:, c, 1:2], wk[:, tt, sl:sl + 1], colv[:, 8:9], ALU.mult)
            ve_pos = r.cnt["VE"]

            r.wait("gpsimd", "VE", ve_pos)
            r.dma("gpsimd", didlin.rearrange("(p a) o -> p (a o)", p=128),
                  zero2k[:, 0:NSORT // 128])
            r.dma("gpsimd", dglin.rearrange("(p a) o -> p (a o)", p=128),
                  zero2k[:, 0:NSORT // 128])
            v_copy(poscp[:], posg[:, :, 0:1].squeeze(2))
            r.wait("gpsimd", "VE", r.cnt["VE"])
            r.dma_nc("gpsimd", dposlin.rearrange("(c p) o -> p (c o)", p=128),
                  poscp[:])
            for c in range(16):
                tt = c % 8
                v_ts(valv[:, c:c + 1], pid_f[:], 1.0, float(128 * tt + 1), ALU.mult, ALU.add)
            ve = v_copy(posIv[:], poscp[:])
            gd_pre = r.cnt["GD"]
            r.wait("gpsimd", "GD", gd_pre)
            r.wait("gpsimd", "VE", ve)
            for c in range(16):
                r.op("gpsimd", lambda EE, c=c: EE.indirect_dma_start(
                    out=didlin, out_offset=bass.IndirectOffsetOnAxis(ap=posIv[:, c:c + 1], axis=0),
                    in_=valv[:, c:c + 1], in_offset=None), "GD", 16)
                r.op("gpsimd", lambda EE, c=c: EE.indirect_dma_start(
                    out=dglin, out_offset=bass.IndirectOffsetOnAxis(ap=posIv[:, c:c + 1], axis=0),
                    in_=posg[:, c, 1:2], in_offset=None), "GD", 16)
            gd_scat = r.cnt["GD"]

            r.wait("gpsimd", "GD", gd_scat)
            for g in range(8):
                r.dma_nc("gpsimd", idw8[16 * g:16 * (g + 1), :],
                      didlin.rearrange("(f p) o -> p (f o)", p=16))
            r.dma_nc("gpsimd", gtile[:], dglin.rearrange("(f p) o -> p (f o)", p=128))
            for g in range(8):
                r.dma_nc("gpsimd", poswf[16 * g:16 * (g + 1), :],
                      dposlin.rearrange("(f p) o -> p (f o)", p=16))
            toks[("route_rb", call)] = r.cnt["GD"]
            return

        def emit_routing3(RA, call):
            idw8, sidx_all = RA["idw8"], RA["sidx_all"]
            posw, poswf = RA["posw"], RA["poswf"]
            r.wait("vector", "GD", toks[("route_rb", call)])
            v_ts(idw8[:], idw8[:], -1.0, 0.0, ALU.add, ALU.max)
            v_copy(sidx_all[:], idw8[:])
            ve = v_copy(posw[:], poswf[:])
            toks[("route", call)] = r.cnt["VE"]
            if False:
                rt = RA["mx8"]
                for tt in range(8):
                    v_copy(rt[:, tt, 2:3], mi8[:, tt, 0:1])
                    v_copy(rt[:, tt, 3:4], mi8[:, tt, 1:2])
                    v_copy(rt[:, tt, 4:5], wk[:, tt, 0:1])
                    v_copy(rt[:, tt, 5:6], wk[:, tt, 1:2])
                r.wait("gpsimd", "VE", r.cnt["VE"])
                r.dma("gpsimd", dbg["route"], rt.ap().rearrange("p a b -> p (a b)"), "ST")
                r.dma("gpsimd", dbg["pos"], poscp[:], "ST")
                r.dma("gpsimd", dbg["idl"], idw8[:, 0:33], "ST")
                r.dma("gpsimd", dbg["gl"], gtile[:], "ST")
                r.dma("gpsimd", dbg["snap"], snap[:], "ST")

            return

        def emit_spill_z(zT, EF, call):
            ztm = EF(nc.sbuf_tensor(f"ztm{call}", [128, D], F32))
            ve_z = r.cnt["VE"]
            r.wait("tensor", "VE", ve_z)
            for a in range(8):
                for k in range(DT):
                    b = 4 + (k % 2)
                    r.wait("tensor", "VE", ps_reader[b])
                    pe = t_tp(PS[b][:, 0:128], zT[:, k, 128 * a:128 * (a + 1)])
                    ps_writer[b] = pe
                    ve = evac(b, ztm[:, 128 * k:128 * (k + 1)], 128)
                r.wait("gpsimd", "VE", ve)
                gd = r.dma("gpsimd", dzsp[128 * a:128 * (a + 1), :], ztm[:])
                r.wait("vector", "GD", gd)
            toks[("zspill", call)] = r.cnt["GD"]

        def emit_shared(zT, EF, call):
            TCH = 512
            wbuf = [EF(nc.sbuf_tensor(f"wsgb{call}_{i}", [128, DT, 128], F32)) for i in range(2)]
            wdbuf0 = EF(nc.sbuf_tensor(f"wsdb{call}_0", [128, D], F32))
            wdbuf = [wdbuf0.ap(), wbuf[0].ap().rearrange("p a b -> p (a b)")]
            hsb = EF(nc.sbuf_tensor(f"hs{call}", [128, SIT, TCH], F32))
            wfree = [0, 0]
            wdfree = [0, 0]

            ve_z = r.cnt["VE"]
            r.wait("tensor", "VE", ve_z)
            for tc in range(TLOC // TCH):
                sl = slice(TCH * tc, TCH * (tc + 1))
                if tc > 0:
                    r.wait("vector", "PE", r.cnt["PE"])
                    wfree[0] = max(wfree[0], wdfree[1])
                for it in range(SIT):
                    bg = 2 * (it % 2)
                    for gu in range(2):
                        jt = it + SIT * gu
                        buf = jt % 2
                        r.wait("sync", "PE", wfree[buf])
                        wtok = r.dma("sync", wbuf[buf][:], dwsg[jt], "WD")
                        bank = bg + gu
                        r.wait("tensor", "WD", wtok)
                        pe = mm_group(bank, TCH, [wbuf[buf][:, k, :] for k in range(DT)],
                                      [zT[:, k, sl] for k in range(DT)])
                        wfree[buf] = pe
                    r.wait("scalar", "PE", ps_writer[bg + 1])
                    se = s_act(hsb[:, it, :], PS[bg][:, 0:TCH], AF.Silu)
                    r.wait("vector", "SE", se)
                    ve = v_tt(hsb[:, it, :], hsb[:, it, :], PS[bg + 1][:, 0:TCH], ALU.mult)
                    ps_reader[bg] = ve
                    ps_reader[bg + 1] = ve
                ve_hs = r.cnt["VE"]
                r.wait("tensor", "VE", ve_hs)
                wdfree[1] = max(wdfree[1], wfree[0])
                for dg in range(2):
                    for it in range(SIT):
                        buf = it % 2
                        r.wait("sync", "PE", wdfree[buf])
                        wtok = r.dma("sync", wdbuf[buf][:], dwsd[128 * it:128 * (it + 1), :], "WD")
                        r.wait("tensor", "WD", wtok)
                        for d8 in range(8):
                            k = 8 * dg + d8
                            bank = d8
                            if it == 0:
                                r.wait("tensor", "VE", ps_reader[bank])
                            pe = t_mm(PS[bank][:, 0:TCH], wdbuf[buf][:, 128 * k:128 * (k + 1)],
                                      hsb[:, it, :], it == 0, it == SIT - 1,
                                      "PE" if (d8 == 7 or it == SIT - 1) else None)
                            if it == SIT - 1:
                                ps_writer[bank] = pe
                        wdfree[buf] = pe
                    for d8 in range(8):
                        k = 8 * dg + d8
                        bank = d8
                        r.wait("vector", "PE", ps_writer[bank])
                        ve = v_stt(rT[:, k, sl], PS[bank][:, 0:TCH], 0.5, rT[:, k, sl],
                                   ALU.mult, ALU.add)
                        ps_reader[bank] = ve

        def emit_experts(RA, EF, call):
            sidx_all, gtile, posw = RA["sidx_all"], RA["gtile"], RA["posw"]
            zst = [EF(nc.sbuf_tensor(f"zst{call}_{i}", [128, 2, D], F32)) for i in range(2)]
            zeT = EF(nc.sbuf_tensor(f"zeT{call}", [128, DT, CAP], F32))
            wgb = [EF(nc.sbuf_tensor(f"wgb{call}_{i}", [128, DT, 128], F32)) for i in range(2)]
            hT = EF(nc.sbuf_tensor(f"hT{call}", [128, ITL, CAP], F32))
            wdb = [EF(nc.sbuf_tensor(f"wdb{call}_{i}", [128, D], F32)) for i in range(2)]
            ye = EF(nc.sbuf_tensor(f"ye{call}", [128, D], F32))
            sg = EF(nc.sbuf_tensor(f"sge{call}", [128, CAP], F32))
            zst_free = [0, 0]
            wg_free = [0, 0]
            wd_free = [0, 0]
            ye_free = 0

            r.wait("gpsimd", "VE", toks[("route", call)])
            r.wait("gpsimd", "GD", toks[("zspill", call)])
            for e in range(E):
                buf = e % 2
                if e > 0:
                    r.wait("vector", "PE", r.cnt["PE"])
                r.wait("gpsimd", "PE", zst_free[buf])
                gd = r.op("gpsimd", lambda EE, b=buf, e=e: EE.dma_gather(
                    zst[b][:], dzsp, sidx_all[:, 16 * e:16 * (e + 1)],
                    num_idxs=CAP, num_idxs_reg=CAP, elem_size=D, transpose=False),
                    "GD", 16)
                r.wait("tensor", "GD", gd)
                for c in range(2):
                    for k in range(DT):
                        b = 2 + (k % 2)
                        r.wait("tensor", "VE", ps_reader[b])
                        pe = t_tp(PS[b][:, 0:128], zst[buf][:, c, 128 * k:128 * (k + 1)])
                        ps_writer[b] = pe
                        evac(b, zeT[:, k, 128 * c:128 * (c + 1)], 128)
                zst_free[buf] = pe
                for it in range(ITL):
                    bg = 4 + 2 * (it % 2)
                    for gu in range(2):
                        jt = it + ITL * gu
                        wbuf_i = jt % 2
                        r.wait("sync", "PE", wg_free[wbuf_i])
                        wtok = r.dma("sync", wgb[wbuf_i][:], dwg[e, jt], "WD")
                        bank = bg + gu
                        r.wait("tensor", "WD", wtok)
                        pe = mm_group(bank, CAP, [wgb[wbuf_i][:, k, :] for k in range(DT)],
                                      [zeT[:, k, :] for k in range(DT)])
                        wg_free[wbuf_i] = pe
                    r.wait("scalar", "PE", ps_writer[bg + 1])
                    se = s_act(hT[:, it, :], PS[bg][:, 0:CAP], AF.Silu)
                    r.wait("vector", "SE", se)
                    ve = v_tt(hT[:, it, :], hT[:, it, :], PS[bg + 1][:, 0:CAP], ALU.mult)
                    ps_reader[bg] = ve
                    ps_reader[bg + 1] = ve
                ve_h = r.cnt["VE"]
                r.wait("tensor", "VE", ve_h)
                for it in range(ITL):
                    buf_d = it % 2
                    r.wait("sync", "PE", wd_free[buf_d])
                    wtok = r.dma("sync", wdb[buf_d][:], dwd[e, 128 * it:128 * (it + 1), :], "WD")
                    r.wait("tensor", "WD", wtok)
                    for tokt in range(2):
                        for dc in range(4):
                            bank = 4 * tokt + dc
                            if it == 0:
                                r.wait("tensor", "VE", ps_reader[bank])
                            pe = t_mm(PS[bank][:, 0:512],
                                      hT[:, it, 128 * tokt:128 * (tokt + 1)],
                                      wdb[buf_d][:, 512 * dc:512 * (dc + 1)],
                                      it == 0, it == ITL - 1,
                                      "PE" if (tokt == 1 and dc == 3) or it == ITL - 1 else None)
                            if it == ITL - 1:
                                ps_writer[bank] = pe
                    wd_free[buf_d] = pe
                for tokt in range(2):
                    r.wait("vector", "GD", ye_free)
                    for dc in range(4):
                        bank = 4 * tokt + dc
                        r.wait("vector", "PE", ps_writer[bank])
                        ve = v_ts(ye[:, 512 * dc:512 * (dc + 1)], PS[bank][:, 0:512],
                                  gtile[:, 2 * e + tokt:2 * e + tokt + 1], None, ALU.mult)
                        ps_reader[bank] = ve
                    r.wait("gpsimd", "VE", ve)
                    rowbase = CAP * e + 128 * tokt
                    ye_free = r.dma("gpsimd", dysort[rowbase:rowbase + 128, :], ye[:])
            toks[("experts", call)] = r.cnt["GD"]

        def emit_return(RA, EF, call):
            posw = RA["posw"]
            Y0 = EF(nc.sbuf_tensor(f"Y0{call}", [128, 2, D], F32))
            Y1 = EF(nc.sbuf_tensor(f"Y1{call}", [128, 2, D], F32))
            for cch in range(32):
                gd0 = r.dma("gpsimd", dysort[DUMPPOS:DUMPPOS + 128, 64 * cch:64 * (cch + 1)],
                            zero2k[:])
            r.wait("gpsimd", "GD", toks[("experts", call)])
            pe_last = 0
            for j in range(4):
                r.wait("gpsimd", "PE", pe_last)
                g0 = r.op("gpsimd", lambda EE, j=j: EE.dma_gather(
                    Y0[:], dysort, posw[:, 16 * j:16 * (j + 1)],
                    num_idxs=CAP, num_idxs_reg=CAP, elem_size=D, transpose=False),
                    "GD", 16)
                g1 = r.op("gpsimd", lambda EE, j=j: EE.dma_gather(
                    Y1[:], dysort, posw[:, 64 + 16 * j:64 + 16 * (j + 1)],
                    num_idxs=CAP, num_idxs_reg=CAP, elem_size=D, transpose=False),
                    "GD", 16)
                r.wait("vector", "GD", g1)
                ve = v_tt(Y0[:], Y0[:], Y1[:], ALU.add)
                r.wait("tensor", "VE", ve)
                for c in range(2):
                    for k in range(DT):
                        b = 6 + (k % 2)
                        r.wait("tensor", "VE", ps_reader[b])
                        pe = t_tp(PS[b][:, 0:128], Y0[:, c, 128 * k:128 * (k + 1)])
                        ps_writer[b] = pe
                        r.wait("vector", "PE", pe)
                        tsl = slice(256 * j + 128 * c, 256 * j + 128 * (c + 1))
                        ve = v_stt(rT[:, k, tsl], PS[b][:, 0:128], 0.5, rT[:, k, tsl],
                                   ALU.mult, ALU.add)
                        ps_reader[b] = ve
                pe_last = pe

        def emit_ffn(call):
            with ExitStack() as FX:
                EF = FX.enter_context
                RA = alloc_routing(EF, call) if stage >= 2 else None
                with ExitStack() as AX2:
                    EA = AX2.enter_context
                    zT = EA(nc.sbuf_tensor(f"zT{call}", [128, DT, TLOC], F32))
                    with ExitStack() as LX2:
                        EL = LX2.enter_context
                        meanB = EL(nc.sbuf_tensor(f"meanB{call}", [128, TLOC], F32))
                        rstdB = EL(nc.sbuf_tensor(f"rstdB{call}", [128, TLOC], F32))
                        sc = tuple(EL(nc.sbuf_tensor(f"lnrow{call}_{i}", [1, 512], F32))
                                   for i in range(2))
                        emit_ln(zT, meanB, rstdB, sc)
                        if debug and stage <= 1 and call == 0:
                            ve = r.cnt["VE"]
                            r.wait("gpsimd", "VE", ve)
                            r.dma("gpsimd", dbg["z1"], zT.ap().rearrange("p a b -> p (a b)"), "ST")
                            r.barrier_now()
                            r.flush()
                            return
                        if stage >= 2 and stage < 3:
                            emit_routing(zT, RA, call)
                        if stage >= 3:
                            emit_routing(zT, RA, call, part=1)
                        r.barrier_now()
                    if stage >= 3:
                        with ExitStack() as SPX:
                            emit_spill_z(zT, SPX.enter_context, call)
                            emit_routing(zT, RA, call, part=2)
                            r.barrier_now()
                        emit_shared(zT, EA, call)
                        emit_routing3(RA, call)
                    r.barrier_now()
                    r.flush()
                if stage >= 4:
                    with ExitStack() as BX2:
                        emit_experts(RA, BX2.enter_context, call)
                        r.barrier_now()
                        r.flush()
                    with ExitStack() as CX2:
                        emit_return(RA, CX2.enter_context, call)
                    r.barrier_now()
                    r.flush()

        def emit_mamba():
            with ExitStack() as MX:
                EM = MX.enter_context
                deltaT = EM(nc.sbuf_tensor("deltaT", [128, DT, TLOC], F32))
                z2h = EM(nc.sbuf_tensor("z2h", [128, DT, 512], F32))
                bcT = EM(nc.sbuf_tensor("bcT", [32, TLOC], F32))
                aneg = EM(nc.sbuf_tensor("aneg", [128, DT, DST], F32))
                dv_s = EM(nc.sbuf_tensor("dv_s", [128, DT], F32))
                sd = EM(nc.sbuf_tensor("sd", [128, DT], F32))
                selb = EM(nc.sbuf_tensor("selb", [32, 2, 128], F32))
                sel_i = EM(nc.sbuf_tensor("sel_i", [32, 128], I32))
                brep = EM(nc.sbuf_tensor("brep", [128, 512], F32))
                crep = EM(nc.sbuf_tensor("crep", [128, 512], F32))
                da = EM(nc.sbuf_tensor("da", [128, 512], F32))
                bx = EM(nc.sbuf_tensor("bx", [128, 512], F32))
                hsc = EM(nc.sbuf_tensor("hsc", [128, 512], F32))
                hend = EM(nc.sbuf_tensor("hend", [128, 256], F32))
                PPack = EM(nc.sbuf_tensor("PPack", [128, 256], F32))
                hinit = EM(nc.sbuf_tensor("hinit", [128, 256], F32))
                statev = EM(nc.sbuf_tensor("statev", [128, 256], F32))
                rankB = EM(nc.sbuf_tensor("rankB", [128, 1], F32))
                mcol = EM(nc.sbuf_tensor("mcol", [128, 1], F32))

                v_memset(sd[:], 0.0)
                v_memset(hinit[:], 0.0)
                v_memset(statev[:], 0.0)
                r.op("gpsimd", lambda EE: EE.iota(sel_i[:], pattern=[[0, 128]], base=0,
                                                  channel_multiplier=1), "GE")
                ge = r.cnt["GE"]
                r.wait("gpsimd", "IN", in_cid)
                gr = r.op("gpsimd", lambda EE: EE.partition_broadcast(rankB[:], cid_s[0:1, :]), "GE")

                def build_sel(s):
                    r.wait("vector", "GE", ge)
                    v_ts(selb[:, 0, :], sel_i[:], float(s), None, ALU.is_equal)
                    return v_ts(selb[:, 1, :], sel_i[:], float(16 + s), None, ALU.is_equal)

                with ExitStack() as WX:
                    EW = WX.enter_context
                    wdbcT_s = EW(nc.sbuf_tensor("wdbcT_s", [128, DT, 160], F32))
                    wdtT_b = EW(nc.sbuf_tensor("wdtT_b", [128, 128], F32))
                    bdlo_s = EW(nc.sbuf_tensor("bdlo_s", [128, 1], F32))
                    bbc_s = EW(nc.sbuf_tensor("bbc_s", [32, 1], F32))
                    bdt_s = EW(nc.sbuf_tensor("bdt_s", [128, DT], F32))
                    meanB = EW(nc.sbuf_tensor("meanB_m", [128, TLOC], F32))
                    rstdB = EW(nc.sbuf_tensor("rstdB_m", [128, TLOC], F32))
                    sc = tuple(EW(nc.sbuf_tensor(f"lnrow_m_{i}", [1, 512], F32))
                               for i in range(2))
                    dtmp = brep
                    dtmp2 = bx

                    l1 = r.dma("sync", wdbcT_s[:], dwdbcT.rearrange("(k p) c -> p k c", p=128))
                    l3 = r.dma("sync", bdlo_s[:], dbdlo)
                    l4 = r.dma("sync", bbc_s[:], dbbc)
                    l5 = r.dma("sync", bdt_s[:], dbdt)
                    l6 = r.dma("sync", aneg[:], dalog)
                    l7 = r.dma("sync", dv_s[:], ddvec)
                    r.wait("scalar", "IN", l7)
                    se = s_act(aneg[:], aneg[:], AF.Exp)
                    r.wait("vector", "SE", se)
                    v_ts(aneg[:], aneg[:], -1.0, None, ALU.mult)
                    r.wait("vector", "IN", l7)
                    emit_ln(None, meanB, rstdB, sc, make_z=False, sq=deltaT)

                    wdt_free = 0
                    for h in range(2):
                        S = slice(512 * h, 512 * (h + 1))
                        for k in range(DT):
                            v_tt(z2h[:, k, :], rT[:, k, S], meanB[:, S], ALU.subtract)
                            v_tt(z2h[:, k, :], z2h[:, k, :], rstdB[:, S], ALU.mult)
                        ve_z2 = r.cnt["VE"]
                        r.wait("tensor", "VE", ve_z2)
                        mm_group(0, 512, [wdbcT_s[:, k, 0:128] for k in range(DT)],
                                 [z2h[:, k, :] for k in range(DT)])
                        def _dlo(dst, ps):
                            return v_ts(dst, ps, bdlo_s[:], None, ALU.add)
                        evac(0, dtmp[:], 512, _dlo)
                        r.wait("tensor", "VE", ps_reader[1])
                        for k in range(DT):
                            pe = t_mm(PS[1][0:32, 0:512], wdbcT_s[:, k, 128:160],
                                      z2h[:, k, :], k == 0, k == DT - 1,
                                      "PE" if k == DT - 1 else None)
                        ps_writer[1] = pe
                        r.wait("vector", "PE", ps_writer[1])
                        ve = r.op("vector", lambda EE, S=S: EE.tensor_scalar(
                            bcT[:, S], PS[1][0:32, 0:512], bbc_s[:], None, op0=ALU.add), "VE")
                        ps_reader[1] = ve
                        ve_dlo = r.cnt["VE"]
                        r.wait("tensor", "VE", ve_dlo)
                        for k in range(DT):
                            bank = 2 + (k % 2)
                            r.wait("sync", "PE", wdt_free)
                            wtok = r.dma("sync", wdtT_b[:], dwdtT[:, 128 * k:128 * (k + 1)], "WD")
                            r.wait("tensor", "WD", wtok)
                            r.wait("tensor", "VE", ps_reader[bank])
                            pe = t_mm(PS[bank][:, 0:512], wdtT_b[:],
                                      dtmp[:], True, True, "PE")
                            wdt_free = pe
                            ps_writer[bank] = pe
                            r.wait("vector", "PE", pe)
                            ve = v_ts(dtmp2[:], PS[bank][:, 0:512], bdt_s[:, k:k + 1], None, ALU.add)
                            ps_reader[bank] = ve
                            v_ts(hsc[:], dtmp2[:], -1.0, None, ALU.mult)
                            ve = v_tt(hsc[:], dtmp2[:], hsc[:], ALU.min)
                            r.wait("scalar", "VE", ve)
                            se = s_act(da[:], hsc[:], AF.Exp)
                            se = s_act(da[:], da[:], AF.Ln, bias=onecol[:])
                            r.wait("vector", "SE", se)
                            v_ts(dtmp2[:], dtmp2[:], 0.0, None, ALU.max)
                            v_tt(deltaT[:, k, S], dtmp2[:], da[:], ALU.add)
                            v_stt(rT[:, k, S], z2h[:, k, :], dv_s[:, k:k + 1], rT[:, k, S],
                                  ALU.mult, ALU.add)
                            v_red(mcol[:], deltaT[:, k, S], ALU.add)
                            v_tt(sd[:, k:k + 1], sd[:, k:k + 1], mcol[:], ALU.add)
                        for k in range(DT):
                            v_tt(z2h[:, k, :], z2h[:, k, :], deltaT[:, k, S], ALU.mult)
                        ve_pre = r.cnt["VE"]
                        r.wait("tensor", "VE", ve_pre)
                        for s in range(DST):
                            ve_sel = build_sel(s)
                            r.wait("tensor", "VE", ve_sel)
                            for which, dst in ((0, brep), (1, crep)):
                                bank = 4 + which
                                r.wait("tensor", "VE", ps_reader[bank])
                                pe = t_mm(PS[bank][:, 0:512], selb[:, which, :], bcT[:, S],
                                          True, True, "PE")
                                ps_writer[bank] = pe
                                evac(bank, dst[:], 512)
                            for k in range(DT):
                                col = k * 16 + s
                                r.wait("scalar", "VE", r.cnt["VE"])
                                se = s_act(da[:], deltaT[:, k, S], AF.Exp,
                                           scale=aneg[:, k, s:s + 1])
                                r.wait("vector", "SE", se)
                                v_tt(bx[:], z2h[:, k, :], brep[:], ALU.mult)
                                if h == 0:
                                    v_scan(hsc[:], da[:], bx[:], 0.0)
                                else:
                                    v_scan(hsc[:], da[:], bx[:], hend[:, col:col + 1])
                                v_copy(hend[:, col:col + 1], hsc[:, 511:512])
                                v_tt(bx[:], hsc[:], crep[:], ALU.mult)
                                ve = v_tt(rT[:, k, S], rT[:, k, S], bx[:], ALU.add)
                            r.wait("scalar", "VE", r.cnt["VE"])
                    r.barrier_now()
                    if debug and stage == 5:
                        r.dma("gpsimd", dbg["z1"], deltaT.ap().rearrange("p a b -> p (a b)"), "ST")
                        r.dma("gpsimd", dbg["bc"], bcT[:], "ST")
                        r.dma("gpsimd", dbg["mam"][:, 0:256], hend[:], "ST")
                        r.barrier_now()
                    r.flush()

                with ExitStack() as CX:
                    EC = CX.enter_context
                    crepF = EC(nc.sbuf_tensor("crepF", [128, TLOC], F32))
                    ctmp = EC(nc.sbuf_tensor("ctmp", [128, TLOC], F32))
                    gath1 = EC(nc.sbuf_tensor("gath1", [128, 512], F32))

                    for k in range(DT):
                        r.wait("scalar", "VE", r.cnt["VE"])
                        se = s_act(PPack[:, 16 * k:16 * (k + 1)], aneg[:, k, :], AF.Exp,
                                   scale=sd[:, k:k + 1])
                    r.wait("gpsimd", "SE", se)
                    r.wait("gpsimd", "VE", r.cnt["VE"])
                    r.dma("gpsimd", dccin[:, 0:256], PPack[:])
                    gd = r.dma("gpsimd", dccin[:, 256:512], hend[:])
                    r.wait("gpsimd", "GD", gd)
                    cc = r.op("gpsimd", lambda EE: EE.collective_compute(
                        "AllGather", ALU.bypass, replica_groups=[list(range(NCORES))],
                        ins=[dccin.opt()], outs=[dccout.opt()]), "CC")
                    r.wait("gpsimd", "CC", cc)
                    r.wait("vector", "GE", gr)
                    r.wait("vector", "CC", cc)
                    for j in range(NCORES):
                        gd = r.dma("gpsimd", gath1[:], dccout[128 * j:128 * (j + 1), :])
                        r.wait("vector", "GD", gd)
                        v_ts(mcol[:], rankB[:], float(j), None, ALU.is_equal)
                        r.wait("vector", "VE", r.cnt["VE"])
                        v_stt(hinit[:], statev[:], mcol[:], hinit[:], ALU.mult, ALU.add)
                        v_tt(statev[:], gath1[:, 0:256], statev[:], ALU.mult)
                        ve = v_tt(statev[:], statev[:], gath1[:, 256:512], ALU.add)
                        r.wait("gpsimd", "VE", ve)
                    if debug and stage == 5:
                        r.wait("gpsimd", "VE", r.cnt["VE"])
                        r.dma("gpsimd", dbg["mam"][:, 256:512], PPack[:], "ST")
                        r.dma("gpsimd", dbg["mam"][:, 512:768], hinit[:], "ST")
                    v_memset(ctmp[:], 1.0)
                    for k in range(DT):
                        v_scan(deltaT[:, k, :], ctmp[:], deltaT[:, k, :], 0.0)
                    ve_cs = r.cnt["VE"]
                    r.wait("tensor", "VE", ve_cs)
                    for s in range(DST):
                        ve_sel = build_sel(s)
                        r.wait("tensor", "VE", ve_sel)
                        for half in range(2):
                            bank = 6 + half
                            r.wait("tensor", "VE", ps_reader[bank])
                            pe = t_mm(PS[bank][:, 0:512], selb[:, 1, :],
                                      bcT[:, 512 * half:512 * (half + 1)], True, True, "PE")
                            ps_writer[bank] = pe
                            evac(bank, crepF[:, 512 * half:512 * (half + 1)], 512)
                        for k in range(DT):
                            r.wait("scalar", "VE", r.cnt["VE"])
                            se = s_act(ctmp[:], deltaT[:, k, :], AF.Exp,
                                       scale=aneg[:, k, s:s + 1])
                            r.wait("vector", "SE", se)
                            v_stt(ctmp[:], ctmp[:], hinit[:, 16 * k + s:16 * k + s + 1],
                                  crepF[:], ALU.mult, ALU.mult)
                            ve = v_tt(rT[:, k, :], rT[:, k, :], ctmp[:], ALU.add)
                            r.wait("scalar", "VE", ve)
                    r.barrier_now()
                    r.flush()

        emit_ffn(0)
        if stage >= 5:
            emit_mamba()
        if stage >= 6:
            emit_ffn(1)

        with ExitStack() as OX:
            otm = OX.enter_context(nc.sbuf_tensor("otm", [128, D], F32))
            for a in range(8):
                for k in range(DT):
                    b = k % 4
                    r.wait("tensor", "VE", ps_reader[b])
                    pe = t_tp(PS[b][:, 0:128], rT[:, k, 128 * a:128 * (a + 1)])
                    ps_writer[b] = pe
                    ve = evac(b, otm[:, 128 * k:128 * (k + 1)], 128)
                r.wait("gpsimd", "VE", ve)
                st = r.dma("gpsimd", dout[128 * a:128 * (a + 1), :], otm[:], "ST")
                r.wait("vector", "ST", st)
            r.wait("gpsimd", "ST", r.cnt["ST"])
            r.flush()

    return nc



def _pack_lhsT(w):
    Dd, M = w.shape
    return np.ascontiguousarray(
        w.reshape(Dd // 128, 128, M // 128, 128).transpose(2, 1, 0, 3))


def prep_inputs(inp):
    x = np.asarray(inp["L0_attn_x"], dtype=np.float32).reshape(8192, 2048)
    W_dbc = np.asarray(inp["L0_attn_W_dbc"], np.float32)
    b_dbc = np.asarray(inp["L0_attn_b_dbc"], np.float32)
    W_dt = np.asarray(inp["L0_attn_W_dt"], np.float32)
    b_dt = np.asarray(inp["L0_attn_b_dt"], np.float32)
    A_log = np.asarray(inp["L0_attn_A_log"], np.float32)
    Dv = np.asarray(inp["L0_attn_D"], np.float32)
    gate_w = np.asarray(inp["L0_ffn_gate_w"], np.float32)
    w_gate_up = np.asarray(inp["L0_ffn_w_gate_up"], np.float32)
    w_down = np.asarray(inp["L0_ffn_w_down"], np.float32)
    sgu = np.asarray(inp["L0_ffn_shared_gate_up"], np.float32)
    sdn = np.asarray(inp["L0_ffn_shared_down"], np.float32)

    common = dict(
        wsg=_pack_lhsT(sgu),
        wsd=np.ascontiguousarray(sdn),
        wg=np.stack([_pack_lhsT(w_gate_up[e]) for e in range(E)]),
        wd=np.ascontiguousarray(w_down),
        gwT=np.ascontiguousarray(gate_w.T),
        wdbcT=np.ascontiguousarray(W_dbc.T),
        wdtT=np.ascontiguousarray(W_dt.T),
        bdlo=np.ascontiguousarray(b_dbc[:128].reshape(128, 1)),
        bbc=np.ascontiguousarray(b_dbc[128:].reshape(32, 1)),
        bdt=np.ascontiguousarray(b_dt.reshape(DT, 128).T),
        alog=np.ascontiguousarray(A_log.reshape(DT, 128, DST).transpose(1, 0, 2)),
        dvec=np.ascontiguousarray(Dv.reshape(DT, 128).T),
    )
    maps = []
    for c in range(NCORES):
        m = dict(common)
        m["x"] = np.ascontiguousarray(x[TLOC * c:TLOC * (c + 1)])
        m["cid"] = np.array([[float(c)]], np.float32)
        maps.append(m)
    return maps


def kernel(**inputs):
    maps = prep_inputs(inputs)
    nc = build_program(stage=6, debug=False)
    nc.compile()
    res = run_bass_kernel_spmd(nc, maps, core_ids=list(range(NCORES)))
    kernel.last_results = res
    out = np.concatenate([res.results[c]["out"] for c in range(NCORES)], axis=0)
    return out.reshape(1, 8192, 2048)


# version 04 (final)
# speedup vs baseline: 1.2313x; 1.0309x over previous
import os
import numpy as np
from contextlib import ExitStack

import concourse.bacc as bacc
import concourse.bass as bass
import concourse.mybir as mybir
from concourse.bass_utils import run_bass_kernel_spmd

F32 = mybir.dt.float32
BF16 = mybir.dt.bfloat16
I16 = mybir.dt.int16
I32 = mybir.dt.int32
U32 = mybir.dt.uint32
AF = mybir.ActivationFunctionType
ALU = mybir.AluOpType
AX = mybir.AxisListType.X

NCORES = 8
TLOC = 1024
D = 2048
DT = D // 128
E = 16
CAP = 256
II = 1408
ITL = II // 128
SI = 2816
SIT = SI // 128
DST = 16
NSORT = E * CAP + 128
DUMPPOS = E * CAP
EPS = 1e-5

ENGINES = ("sync", "tensor", "vector", "scalar", "gpsimd")


class Rec:
    def __init__(self, nc, ctx, block):
        self.nc = nc
        self.block = block
        self.ops = {e: [] for e in ENGINES}
        self.semh = {}
        self.cnt = {}
        for name in ("IN", "WD", "GD", "PE", "VE", "SE", "GE", "CC", "ST"):
            self.semh[name] = ctx.enter_context(nc.semaphore(name))
            self.cnt[name] = 0
        self.barrier = dict(self.cnt)

    VSERIAL = True

    def op(self, eng, builder, sem=None, n=1):
        if sem is None and eng == "vector" and self.VSERIAL:
            sem = "VE"
        if sem is None:
            self.ops[eng].append(builder)
            return None
        h = self.semh[sem]
        self.cnt[sem] += n

        def f(EE, b=builder, h=h, n=n):
            b(EE).then_inc(h, n)
        self.ops[eng].append(f)
        tok = self.cnt[sem]
        if eng == "vector" and sem == "VE" and self.VSERIAL:
            self.ops[eng].append(lambda EE, h=h, t=tok: EE.wait_ge(h, t))
        return tok

    def wait(self, eng, sem, tok):
        if not tok:
            return
        h = self.semh[sem]
        self.ops[eng].append(lambda EE, h=h, t=tok: EE.wait_ge(h, t))

    def dma(self, eng, out, in_, sem=None):
        if sem is None:
            sem = "GD" if eng == "gpsimd" else ("IN" if eng == "sync" else "WD")
        return self.op(eng, lambda EE, o=out, i=in_: EE.dma_start(out=o, in_=i), sem, 16)

    def dma_nc(self, eng, out, in_, sem=None):
        if sem is None:
            sem = "GD" if eng == "gpsimd" else ("IN" if eng == "sync" else "WD")
        return self.op(eng, lambda EE, o=out, i=in_: EE.dma_start(
            out=o, in_=i, allow_slow_non_contiguous=True), sem, 16)

    def flush(self):
        for eng in ENGINES:
            ops = self.ops[eng]
            if not ops:
                continue

            def body(EE, ops=list(ops)):
                for f in ops:
                    f(EE)
            getattr(self.block, eng)(body)
            self.ops[eng] = []

    def barrier_now(self):
        toks = dict(self.cnt)
        for eng in ENGINES:
            for s, t in toks.items():
                if t > self.barrier.get(s, 0):
                    self.wait(eng, s, t)
        self.barrier = toks


def build_program(stage=6, debug=False):
    nc = bacc.Bacc("TRN2", target_bir_lowering=False, debug=False)

    dx = nc.dram_tensor("x", [TLOC, D], F32, kind="ExternalInput").ap()
    dcid = nc.dram_tensor("cid", [1, 1], F32, kind="ExternalInput").ap()
    dwsg = nc.dram_tensor("wsg", [2 * SIT, 128, DT, 128], F32, kind="ExternalInput").ap()
    dwsd = nc.dram_tensor("wsd", [SI, D], F32, kind="ExternalInput").ap()
    dwg = nc.dram_tensor("wg", [E, 2 * ITL, 128, DT, 128], F32, kind="ExternalInput").ap()
    dwd = nc.dram_tensor("wd", [E, II, D], F32, kind="ExternalInput").ap()
    dgwT = nc.dram_tensor("gwT", [D, E], F32, kind="ExternalInput").ap()
    dwdbcT = nc.dram_tensor("wdbcT", [D, 160], F32, kind="ExternalInput").ap()
    dwdtT = nc.dram_tensor("wdtT", [128, D], F32, kind="ExternalInput").ap()
    dbdlo = nc.dram_tensor("bdlo", [128, 1], F32, kind="ExternalInput").ap()
    dbbc = nc.dram_tensor("bbc", [32, 1], F32, kind="ExternalInput").ap()
    dbdt = nc.dram_tensor("bdt", [128, DT], F32, kind="ExternalInput").ap()
    dalog = nc.dram_tensor("alog", [128, DT, DST], F32, kind="ExternalInput").ap()
    ddvec = nc.dram_tensor("dvec", [128, DT], F32, kind="ExternalInput").ap()

    dout = nc.dram_tensor("out", [TLOC, D], F32, kind="ExternalOutput").ap()

    dzsp = nc.dram_tensor("zsp", [TLOC, D], F32).ap()
    dzsphi = nc.dram_tensor("zsphi", [TLOC, D], BF16).ap()
    dzsplo = nc.dram_tensor("zsplo", [TLOC, D], BF16).ap()
    dysort = nc.dram_tensor("ysort", [NSORT, D], F32).ap()
    didlin = nc.dram_tensor("idlin", [NSORT, 1], F32).ap()
    dglin = nc.dram_tensor("glin", [NSORT, 1], F32).ap()
    dposlin = nc.dram_tensor("poslin", [2 * TLOC, 1], F32).ap()
    dccin = nc.dram_tensor("ccin", [128, 512], F32).ap()
    dccout = nc.dram_tensor("ccout", [128 * NCORES, 512], F32).ap()

    dbg = {}
    if debug:
        dbg["z1"] = nc.dram_tensor("dbg_z1", [128, DT * TLOC], F32, kind="ExternalOutput").ap()
        dbg["route"] = nc.dram_tensor("dbg_route", [128, 64], F32, kind="ExternalOutput").ap()
        dbg["pos"] = nc.dram_tensor("dbg_pos", [128, 16], F32, kind="ExternalOutput").ap()
        dbg["idl"] = nc.dram_tensor("dbg_idl", [128, 33], F32, kind="ExternalOutput").ap()
        dbg["gl"] = nc.dram_tensor("dbg_gl", [128, 33], F32, kind="ExternalOutput").ap()
        dbg["snap"] = nc.dram_tensor("dbg_snap", [128, 80], F32, kind="ExternalOutput").ap()
        dbg["mam"] = nc.dram_tensor("dbg_mam", [128, 768], F32, kind="ExternalOutput").ap()
        dbg["bc"] = nc.dram_tensor("dbg_bc", [32, TLOC], F32, kind="ExternalOutput").ap()

    with ExitStack() as CTX:
        EN = CTX.enter_context
        block = EN(nc.Block())
        r = Rec(nc, CTX, block)

        rT = EN(nc.sbuf_tensor("rT", [128, DT, TLOC], F32))
        ident = EN(nc.sbuf_tensor("ident", [128, 128], F32))
        triUS = EN(nc.sbuf_tensor("triUS", [128, 128], F32))
        allone = EN(nc.sbuf_tensor("allone", [128, 128], F32))
        onecol = EN(nc.sbuf_tensor("onecol", [128, 1], F32))
        iota16f = EN(nc.sbuf_tensor("iota16f", [128, 16], F32))
        baseT = EN(nc.sbuf_tensor("baseT", [128, 16], F32))
        pid_f = EN(nc.sbuf_tensor("pid_f", [128, 1], F32))
        gwT_s = EN(nc.sbuf_tensor("gwT_s", [128, DT, E], F32))
        cid_s = EN(nc.sbuf_tensor("cid_s", [1, 1], F32))
        zero2k = EN(nc.sbuf_tensor("zero2k", [128, 64], F32))

        PS = [EN(nc.psum_tensor(f"ps{b}", [128, 512], F32)) for b in range(8)]
        ps_writer = [0] * 8
        ps_reader = [0] * 8

        toks = {}

        def v_ts(out, in0, s1, s2, op0, op1=None):
            if op1 is None:
                return r.op("vector", lambda EE, o=out, i=in0, a=s1, b=s2, p=op0:
                            EE.tensor_scalar(o, i, a, b, op0=p), "VE")
            return r.op("vector", lambda EE, o=out, i=in0, a=s1, b=s2, p=op0, q=op1:
                        EE.tensor_scalar(o, i, a, b, op0=p, op1=q), "VE")

        def v_tt(out, in0, in1, opk):
            return r.op("vector", lambda EE, o=out, a=in0, b=in1, p=opk:
                        EE.tensor_tensor(o, a, b, op=p), "VE")

        def v_stt(out, in0, scal, in1, op0, op1):
            return r.op("vector", lambda EE, o=out, a=in0, s=scal, b=in1, p=op0, q=op1:
                        EE.scalar_tensor_tensor(o, a, s, b, op0=p, op1=q), "VE")

        def v_copy(out, in_):
            return r.op("vector", lambda EE, o=out, i=in_: EE.tensor_copy(o, i), "VE")

        def v_red(out, in_, opk):
            r.op("vector", lambda EE: EE.nop())
            return r.op("vector", lambda EE, o=out, i=in_, p=opk:
                        EE.tensor_reduce(o, i, axis=AX, op=p), "VE")

        def v_memset(ap_, val):
            return r.op("vector", lambda EE, o=ap_, v=val: EE.memset(o, v), "VE")

        def v_recip(out, in_):
            r.op("vector", lambda EE: EE.nop())
            return r.op("vector", lambda EE, o=out, i=in_: EE.reciprocal(o, i), "VE")

        def v_scan(out, d0, d1, init, op0=ALU.mult, op1=ALU.add):
            r.op("vector", lambda EE: EE.nop())
            return r.op("vector", lambda EE, o=out, a=d0, b=d1, i=init, p=op0, q=op1:
                        EE.tensor_tensor_scan(o, a, b, i, op0=p, op1=q), "VE")

        def s_act(out, in_, func, bias=None, scale=None):
            def f(EE, o=out, i=in_, fn=func, b=bias, s=scale):
                kw = {}
                if b is not None:
                    kw["bias"] = b
                if s is not None:
                    kw["scale"] = s
                return EE.activation(o, i, fn, **kw)
            return r.op("scalar", f, "SE")

        def t_mm(out, lhsT, rhs, start, stop, sem=None):
            return r.op("tensor", lambda EE, o=out, l=lhsT, rr=rhs, s=start, t=stop:
                        EE.matmul(o, l, rr, start=s, stop=t), sem)

        def t_tp(out, in_, sem="PE"):
            return r.op("tensor", lambda EE, o=out, i=in_, d=ident[:]:
                        EE.transpose(o, i, d), sem)

        def mm_group(bank, width, lhsT_list, rhs_list):
            r.wait("tensor", "VE", ps_reader[bank])
            n = len(lhsT_list)
            pe = None
            for i, (lt, rh) in enumerate(zip(lhsT_list, rhs_list)):
                pe = t_mm(PS[bank][:, 0:width], lt, rh, i == 0, i == n - 1,
                          "PE" if i == n - 1 else None)
            ps_writer[bank] = pe
            return pe

        def evac(bank, dst, width, fn=None):
            r.wait("vector", "PE", ps_writer[bank])
            if fn is None:
                ve = v_copy(dst, PS[bank][:, 0:width])
            else:
                ve = fn(dst, PS[bank][:, 0:width])
            ps_reader[bank] = ve
            return ve

        scr_i = EN(nc.sbuf_tensor("scr_i", [128, 128], I32))
        p0ctx = ExitStack()
        xtm = p0ctx.enter_context(nc.sbuf_tensor("xtm", [128, 8, D], F32))

        idn_i = p0ctx.enter_context(nc.sbuf_tensor("idn_i", [128, 128], I32))
        r.op("gpsimd", lambda EE: EE.iota(scr_i[:, 0:16], pattern=[[1, 16]], base=0,
                                          channel_multiplier=0), "GE")
        r.op("gpsimd", lambda EE: EE.iota(scr_i[:, 16:17], pattern=[[0, 1]], base=0,
                                          channel_multiplier=1), "GE")
        r.op("gpsimd", lambda EE: EE.iota(idn_i[:], pattern=[[1, 128]], base=0,
                                          channel_multiplier=-1), "GE")
        ge3 = r.cnt["GE"]
        r.wait("vector", "GE", ge3)
        v_copy(iota16f[:], scr_i[:, 0:16])
        v_ts(baseT[:], iota16f[:], float(CAP), None, ALU.mult)
        v_copy(pid_f[:], scr_i[:, 16:17])
        v_memset(onecol[:], 1.0)
        v_memset(allone[:], 1.0)
        v_memset(zero2k[:], 0.0)
        v_ts(ident[:], idn_i[:], 0.0, None, ALU.is_equal)
        tv = v_ts(triUS[:], idn_i[:], 0.0, None, ALU.is_gt)

        in_x = r.dma("sync", xtm[:], dx.rearrange("(a p) d -> p a d", p=128))
        in_gw = r.dma("sync", gwT_s[:], dgwT.rearrange("(k p) e -> p k e", p=128))
        in_cid = r.dma("sync", cid_s[:], dcid)

        r.wait("tensor", "IN", in_x)
        r.wait("tensor", "VE", tv)
        for a in range(8):
            for k in range(DT):
                b = (a * DT + k) % 4
                r.wait("tensor", "VE", ps_reader[b])
                pe = t_tp(PS[b][:, 0:128], xtm[:, a, 128 * k:128 * (k + 1)])
                ps_writer[b] = pe
                evac(b, rT[:, k, 128 * a:128 * (a + 1)], 128)
        r.barrier_now()
        r.flush()
        p0ctx.close()

        def emit_ln(zT, meanB, rstdB, sc, make_z=True, sq=None):
            rows, rows2 = sc
            if sq is None:
                sq = zT
            for k in range(DT):
                v_tt(sq[:, k, :], rT[:, k, :], rT[:, k, :], ALU.mult)
            ve_sq = r.cnt["VE"]
            r.wait("tensor", "VE", ve_sq)
            for half in range(2):
                sl = slice(512 * half, 512 * (half + 1))
                m01, r01 = meanB[0:1, sl], rstdB[0:1, sl]
                for b, src, dstrow in ((0, rT, rows), (1, sq, rows2)):
                    r.wait("tensor", "VE", ps_reader[b])
                    for k in range(DT):
                        pe = t_mm(PS[b][0:1, 0:512], onecol[:], src[:, k, sl],
                                  k == 0, k == DT - 1, "PE" if k == DT - 1 else None)
                    ps_writer[b] = pe
                    r.wait("vector", "PE", ps_writer[b])
                    ve = v_copy(dstrow[:], PS[b][0:1, 0:512])
                    ps_reader[b] = ve
                v_ts(m01, rows[:], 1.0 / D, None, ALU.mult)
                v_ts(rows2[:], rows2[:], 1.0 / D, None, ALU.mult)
                v_tt(rows[:], m01, m01, ALU.mult)
                v_tt(rows2[:], rows2[:], rows[:], ALU.subtract)
                v_ts(rows2[:], rows2[:], EPS, None, ALU.add)
                ve = v_recip(rows[:], rows2[:])
                r.wait("scalar", "VE", ve)
                se = s_act(rows[:], rows[:], AF.Sqrt)
                r.wait("vector", "SE", se)
                v_tt(r01, rows[:], rows[:], ALU.mult)
                v_tt(r01, r01, rows2[:], ALU.mult)
                v_ts(r01, r01, -0.5, 1.5, ALU.mult, ALU.add)
                ve = v_tt(r01, rows[:], r01, ALU.mult)
            r.wait("gpsimd", "VE", ve)
            r.op("gpsimd", lambda EE, o=meanB: EE.partition_broadcast(o[:], o[0:1, :]), "GE")
            g2 = r.op("gpsimd", lambda EE, o=rstdB: EE.partition_broadcast(o[:], o[0:1, :]), "GE")
            r.wait("vector", "GE", g2)
            if make_z:
                for k in range(DT):
                    v_tt(zT[:, k, :], rT[:, k, :], meanB[:], ALU.subtract)
                    v_tt(zT[:, k, :], zT[:, k, :], rstdB[:], ALU.mult)

        def alloc_routing(EF, call):
            d = {}
            d["lg"] = EF(nc.sbuf_tensor(f"lg{call}", [128, 8, E], F32))
            d["mx8"] = EF(nc.sbuf_tensor(f"mx8{call}", [128, 8, 8], F32))
            d["mi8"] = EF(nc.sbuf_tensor(f"mi8{call}", [128, 8, 8], F32))
            d["wk"] = EF(nc.sbuf_tensor(f"wk{call}", [128, 8, 2], F32))
            d["colv3"] = EF(nc.sbuf_tensor(f"colv{call}", [128, 8, 12], F32))
            d["lgf"] = EF(nc.sbuf_tensor(f"lgf{call}", [128, 16], F32))
            d["mxf"] = EF(nc.sbuf_tensor(f"mxf{call}", [128, 8], F32))
            d["mif"] = EF(nc.sbuf_tensor(f"mif{call}", [128, 8], U32))
            d["ohA"] = EF(nc.sbuf_tensor(f"ohA{call}", [128, 16, E], F32))
            d["Rrk"] = EF(nc.sbuf_tensor(f"Rrk{call}", [128, 16, E], F32))
            d["posg"] = EF(nc.sbuf_tensor(f"posg{call}", [128, 16, 2], F32))
            d["posI"] = EF(nc.sbuf_tensor(f"posI{call}", [128, 1], U32))
            d["posIv"] = EF(nc.sbuf_tensor(f"posIv{call}", [128, 16], U32))
            d["valv"] = EF(nc.sbuf_tensor(f"valv{call}", [128, 16], F32))
            d["poscp"] = EF(nc.sbuf_tensor(f"poscp{call}", [128, 16], F32))
            d["valf"] = EF(nc.sbuf_tensor(f"valf{call}", [128, 1], F32))
            d["scr"] = EF(nc.sbuf_tensor(f"pscr{call}", [128, E], F32))
            d["sidx_all"] = EF(nc.sbuf_tensor(f"sidx{call}", [128, NSORT // 16], I16))
            d["idw8"] = EF(nc.sbuf_tensor(f"idw8{call}", [128, NSORT // 16], F32))
            d["gtile"] = EF(nc.sbuf_tensor(f"gtile{call}", [128, NSORT // 128], F32))
            d["posw"] = EF(nc.sbuf_tensor(f"posw{call}", [128, 128], I16))
            d["poswf"] = EF(nc.sbuf_tensor(f"poswf{call}", [128, 128], F32))
            if debug:
                d["snap"] = EF(nc.sbuf_tensor(f"snap{call}", [128, 80], F32))
            return d

        def emit_routing(zT, RA, call, part=None):
            lg = RA["lg"]
            mx8, mi8, wk = RA["mx8"], RA["mi8"], RA["wk"]
            colv3, lgf, mxf, mif = RA["colv3"], RA["lgf"], RA["mxf"], RA["mif"]
            ohA, Rrk, posg, posI = RA["ohA"], RA["Rrk"], RA["posg"], RA["posI"]
            poscp, valf, scr = RA["poscp"], RA["valf"], RA["scr"]
            posIv, valv = RA["posIv"], RA["valv"]
            sidx_all, idw8, gtile = RA["sidx_all"], RA["idw8"], RA["gtile"]
            posw, poswf = RA["posw"], RA["poswf"]
            snap = RA.get("snap")

            if part != 2:
                ve_z = r.cnt["VE"]
                r.wait("tensor", "VE", ve_z)
                for tt in range(8):
                    b = tt % 2
                    mm_group(b, E, [zT[:, k, 128 * tt:128 * (tt + 1)] for k in range(DT)],
                             [gwT_s[:, k, :] for k in range(DT)])
                    evac(b, lg[:, tt, :], E)
            for tt in ([] if part == 2 else range(8)):
                colv = colv3[:, tt, :]
                v_red(colv[:, 0:1], lg[:, tt, :], ALU.max)
                r.wait("vector", "VE", r.cnt["VE"])
                v_ts(lgf[:], lg[:, tt, :], colv[:, 0:1], None, ALU.is_equal)
                v_tt(scr[:], lgf[:], iota16f[:], ALU.mult)
                v_ts(lgf[:], lgf[:], -99.0, 99.0, ALU.mult, ALU.add)
                v_tt(scr[:], scr[:], lgf[:], ALU.add)
                if debug and tt == 0:
                    v_copy(snap[:, 32:48], scr[:])
                v_red(colv[:, 1:2], scr[:], ALU.min)
                r.wait("vector", "VE", r.cnt["VE"])
                v_ts(ohA[:, tt, :], iota16f[:], colv[:, 1:2], None, ALU.is_equal)
                v_stt(lgf[:], ohA[:, tt, :], -1e30, lg[:, tt, :], ALU.mult, ALU.add)
                if debug and tt == 0:
                    v_copy(snap[:, 0:16], ohA[:, 0, :])
                    v_copy(snap[:, 16:32], lgf[:])
                    v_copy(snap[:, 64:80], lg[:, 0, :])
                v_red(colv[:, 2:3], lgf[:], ALU.max)
                r.wait("vector", "VE", r.cnt["VE"])
                v_ts(scr[:], lgf[:], colv[:, 2:3], None, ALU.is_equal)
                v_tt(mx8[:, tt, 0:1], colv[:, 0:1], colv[:, 0:1], ALU.min)
                v_copy(mx8[:, tt, 1:2], colv[:, 2:3])
                v_copy(mi8[:, tt, 0:1], colv[:, 1:2])
                v_tt(scr[:], scr[:], iota16f[:], ALU.mult)
                r.wait("vector", "VE", r.cnt["VE"])
                v_ts(lgf[:], lgf[:], colv[:, 2:3], None, ALU.is_equal)
                v_ts(lgf[:], lgf[:], -99.0, 99.0, ALU.mult, ALU.add)
                v_tt(scr[:], scr[:], lgf[:], ALU.add)
                v_red(colv[:, 3:4], scr[:], ALU.min)
                r.wait("vector", "VE", r.cnt["VE"])
                v_ts(ohA[:, 8 + tt, :], iota16f[:], colv[:, 3:4], None, ALU.is_equal)
                v_copy(mi8[:, tt, 1:2], colv[:, 3:4])
                ve = v_ts(colv[:, 4:5], colv[:, 0:1], -1.0, None, ALU.mult)
                r.wait("scalar", "VE", ve)
                se = s_act(lg[:, tt, :], lg[:, tt, :], AF.Exp, bias=colv[:, 4:5])
                r.wait("vector", "SE", se)
                if debug and tt == 0:
                    v_copy(snap[:, 48:64], lg[:, 0, :])
                v_red(colv[:, 5:6], lg[:, tt, :], ALU.add)
                v_recip(colv[:, 6:7], colv[:, 5:6])
                v_copy(wk[:, tt, 0:1], colv[:, 6:7])
                ve = v_tt(colv[:, 7:8], colv[:, 2:3], colv[:, 4:5], ALU.add)
                r.wait("scalar", "VE", ve)
                se = s_act(colv[:, 8:9], colv[:, 7:8], AF.Exp)
                r.wait("vector", "SE", se)
                v_tt(wk[:, tt, 1:2], colv[:, 8:9], colv[:, 6:7], ALU.mult)
            ve_oh = r.cnt["VE"]
            if part == 1:
                return

            r.wait("tensor", "VE", ve_oh)
            for c in range(16):
                b = 2 + (c % 2)
                lts = [allone[:]] * c + [triUS[:]]
                rhs = [ohA[:, cp, :] for cp in range(c)] + [ohA[:, c, :]]
                mm_group(b, E, lts, rhs)
                evac(b, Rrk[:, c, :], E)

            for c in range(16):
                tt, sl = c % 8, c // 8
                colv = colv3[:, tt, :]
                v_tt(scr[:], Rrk[:, c, :], ohA[:, c, :], ALU.mult)
                v_red(colv[:, 6:7], scr[:], ALU.add)
                v_tt(scr[:], baseT[:], ohA[:, c, :], ALU.mult)
                v_red(colv[:, 7:8], scr[:], ALU.add)
                v_ts(colv[:, 8:9], colv[:, 6:7], float(CAP), None, ALU.is_lt)
                v_tt(colv[:, 9:10], colv[:, 6:7], colv[:, 7:8], ALU.add)
                v_tt(colv[:, 9:10], colv[:, 9:10], colv[:, 8:9], ALU.mult)
                v_ts(colv[:, 10:11], colv[:, 8:9], -float(DUMPPOS), float(DUMPPOS),
                     ALU.mult, ALU.add)
                v_tt(posg[:, c, 0:1], colv[:, 9:10], colv[:, 10:11], ALU.add)
                v_tt(posg[:, c, 1:2], wk[:, tt, sl:sl + 1], colv[:, 8:9], ALU.mult)
            ve_pos = r.cnt["VE"]

            r.wait("gpsimd", "VE", ve_pos)
            r.dma("gpsimd", didlin.rearrange("(p a) o -> p (a o)", p=128),
                  zero2k[:, 0:NSORT // 128])
            r.dma("gpsimd", dglin.rearrange("(p a) o -> p (a o)", p=128),
                  zero2k[:, 0:NSORT // 128])
            v_copy(poscp[:], posg[:, :, 0:1].squeeze(2))
            r.wait("gpsimd", "VE", r.cnt["VE"])
            r.dma_nc("gpsimd", dposlin.rearrange("(c p) o -> p (c o)", p=128),
                  poscp[:])
            for c in range(16):
                tt = c % 8
                v_ts(valv[:, c:c + 1], pid_f[:], 1.0, float(128 * tt + 1), ALU.mult, ALU.add)
            ve = v_copy(posIv[:], poscp[:])
            gd_pre = r.cnt["GD"]
            r.wait("gpsimd", "GD", gd_pre)
            r.wait("gpsimd", "VE", ve)
            for c in range(16):
                r.op("gpsimd", lambda EE, c=c: EE.indirect_dma_start(
                    out=didlin, out_offset=bass.IndirectOffsetOnAxis(ap=posIv[:, c:c + 1], axis=0),
                    in_=valv[:, c:c + 1], in_offset=None), "GD", 16)
                r.op("gpsimd", lambda EE, c=c: EE.indirect_dma_start(
                    out=dglin, out_offset=bass.IndirectOffsetOnAxis(ap=posIv[:, c:c + 1], axis=0),
                    in_=posg[:, c, 1:2], in_offset=None), "GD", 16)
            gd_scat = r.cnt["GD"]

            r.wait("gpsimd", "GD", gd_scat)
            for g in range(8):
                r.dma_nc("gpsimd", idw8[16 * g:16 * (g + 1), :],
                      didlin.rearrange("(f p) o -> p (f o)", p=16))
            r.dma_nc("gpsimd", gtile[:], dglin.rearrange("(f p) o -> p (f o)", p=128))
            for g in range(8):
                r.dma_nc("gpsimd", poswf[16 * g:16 * (g + 1), :],
                      dposlin.rearrange("(f p) o -> p (f o)", p=16))
            toks[("route_rb", call)] = r.cnt["GD"]
            return

        def emit_routing3(RA, call):
            idw8, sidx_all = RA["idw8"], RA["sidx_all"]
            posw, poswf = RA["posw"], RA["poswf"]
            r.wait("vector", "GD", toks[("route_rb", call)])
            v_ts(idw8[:], idw8[:], -1.0, 0.0, ALU.add, ALU.max)
            v_copy(sidx_all[:], idw8[:])
            ve = v_copy(posw[:], poswf[:])
            toks[("route", call)] = r.cnt["VE"]
            if False:
                rt = RA["mx8"]
                for tt in range(8):
                    v_copy(rt[:, tt, 2:3], mi8[:, tt, 0:1])
                    v_copy(rt[:, tt, 3:4], mi8[:, tt, 1:2])
                    v_copy(rt[:, tt, 4:5], wk[:, tt, 0:1])
                    v_copy(rt[:, tt, 5:6], wk[:, tt, 1:2])
                r.wait("gpsimd", "VE", r.cnt["VE"])
                r.dma("gpsimd", dbg["route"], rt.ap().rearrange("p a b -> p (a b)"), "ST")
                r.dma("gpsimd", dbg["pos"], poscp[:], "ST")
                r.dma("gpsimd", dbg["idl"], idw8[:, 0:33], "ST")
                r.dma("gpsimd", dbg["gl"], gtile[:], "ST")
                r.dma("gpsimd", dbg["snap"], snap[:], "ST")

            return

        def emit_spill_z(zT, EF, call):
            ztm = EF(nc.sbuf_tensor(f"ztm{call}", [128, D], F32))
            ztmhi = EF(nc.sbuf_tensor(f"ztmhi{call}", [128, D], BF16))
            ztmlo = EF(nc.sbuf_tensor(f"ztmlo{call}", [128, D], BF16))
            ve_z = r.cnt["VE"]
            r.wait("tensor", "VE", ve_z)
            for a in range(8):
                for k in range(DT):
                    b = 4 + (k % 2)
                    r.wait("tensor", "VE", ps_reader[b])
                    pe = t_tp(PS[b][:, 0:128], zT[:, k, 128 * a:128 * (a + 1)])
                    ps_writer[b] = pe
                    ve = evac(b, ztm[:, 128 * k:128 * (k + 1)], 128)
                r.wait("scalar", "VE", ve)
                se = r.op("scalar", lambda EE: EE.copy(ztmhi[:], ztm[:]), "SE")
                r.wait("vector", "SE", se)
                ve = v_tt(ztmlo[:], ztm[:], ztmhi[:], ALU.subtract)
                r.wait("gpsimd", "VE", ve)
                gd = r.dma("gpsimd", dzsphi[128 * a:128 * (a + 1), :], ztmhi[:])
                gd = r.dma("gpsimd", dzsplo[128 * a:128 * (a + 1), :], ztmlo[:])
                r.wait("vector", "GD", gd)
                r.wait("scalar", "GD", gd)
            toks[("zspill", call)] = r.cnt["GD"]

        def emit_shared(zT, EF, call):
            TCH = 512
            wbuf = [EF(nc.sbuf_tensor(f"wsgb{call}_{i}", [128, DT, 128], F32)) for i in range(2)]
            wdbuf0 = EF(nc.sbuf_tensor(f"wsdb{call}_0", [128, D], F32))
            wdbuf = [wdbuf0.ap(), wbuf[0].ap().rearrange("p a b -> p (a b)")]
            hsb = EF(nc.sbuf_tensor(f"hs{call}", [128, SIT, TCH], F32))
            wfree = [0, 0]
            wdfree = [0, 0]

            ve_z = r.cnt["VE"]
            r.wait("tensor", "VE", ve_z)
            for tc in range(TLOC // TCH):
                sl = slice(TCH * tc, TCH * (tc + 1))
                if tc > 0:
                    r.wait("vector", "PE", r.cnt["PE"])
                    wfree[0] = max(wfree[0], wdfree[1])
                for it in range(SIT):
                    bg = 2 * (it % 2)
                    for gu in range(2):
                        jt = it + SIT * gu
                        buf = jt % 2
                        r.wait("sync", "PE", wfree[buf])
                        wtok = r.dma("sync", wbuf[buf][:], dwsg[jt], "WD")
                        bank = bg + gu
                        r.wait("tensor", "WD", wtok)
                        pe = mm_group(bank, TCH, [wbuf[buf][:, k, :] for k in range(DT)],
                                      [zT[:, k, sl] for k in range(DT)])
                        wfree[buf] = pe
                    r.wait("scalar", "PE", ps_writer[bg + 1])
                    se = s_act(hsb[:, it, :], PS[bg][:, 0:TCH], AF.Silu)
                    r.wait("vector", "SE", se)
                    ve = v_tt(hsb[:, it, :], hsb[:, it, :], PS[bg + 1][:, 0:TCH], ALU.mult)
                    ps_reader[bg] = ve
                    ps_reader[bg + 1] = ve
                ve_hs = r.cnt["VE"]
                r.wait("tensor", "VE", ve_hs)
                wdfree[1] = max(wdfree[1], wfree[0])
                for dg in range(2):
                    for it in range(SIT):
                        buf = it % 2
                        r.wait("sync", "PE", wdfree[buf])
                        wtok = r.dma("sync", wdbuf[buf][:], dwsd[128 * it:128 * (it + 1), :], "WD")
                        r.wait("tensor", "WD", wtok)
                        for d8 in range(8):
                            k = 8 * dg + d8
                            bank = d8
                            if it == 0:
                                r.wait("tensor", "VE", ps_reader[bank])
                            pe = t_mm(PS[bank][:, 0:TCH], wdbuf[buf][:, 128 * k:128 * (k + 1)],
                                      hsb[:, it, :], it == 0, it == SIT - 1,
                                      "PE" if (d8 == 7 or it == SIT - 1) else None)
                            if it == SIT - 1:
                                ps_writer[bank] = pe
                        wdfree[buf] = pe
                    for d8 in range(8):
                        k = 8 * dg + d8
                        bank = d8
                        r.wait("vector", "PE", ps_writer[bank])
                        ve = v_stt(rT[:, k, sl], PS[bank][:, 0:TCH], 0.5, rT[:, k, sl],
                                   ALU.mult, ALU.add)
                        ps_reader[bank] = ve

        def emit_experts(RA, EF, call):
            sidx_all, gtile, posw = RA["sidx_all"], RA["gtile"], RA["posw"]
            zehi = [EF(nc.sbuf_tensor(f"zehi{call}_{i}", [128, DT, CAP], BF16)) for i in range(2)]
            zelo = [EF(nc.sbuf_tensor(f"zelo{call}_{i}", [128, DT, CAP], BF16)) for i in range(2)]
            wgb = EF(nc.sbuf_tensor(f"wgb{call}", [128, DT, 128], F32))
            wgh = [EF(nc.sbuf_tensor(f"wgh{call}_{i}", [128, DT, 128], BF16)) for i in range(2)]
            wgl = [EF(nc.sbuf_tensor(f"wgl{call}_{i}", [128, DT, 128], BF16)) for i in range(2)]
            hTt = EF(nc.sbuf_tensor(f"hTt{call}", [128, CAP], F32))
            hThi = EF(nc.sbuf_tensor(f"hThi{call}", [128, ITL, CAP], BF16))
            hTlo = EF(nc.sbuf_tensor(f"hTlo{call}", [128, ITL, CAP], BF16))
            wdb = EF(nc.sbuf_tensor(f"wdb{call}", [128, D], F32))
            wdh = [EF(nc.sbuf_tensor(f"wdh{call}_{i}", [128, D], BF16)) for i in range(2)]
            wdl = [EF(nc.sbuf_tensor(f"wdl{call}_{i}", [128, D], BF16)) for i in range(2)]
            ye = EF(nc.sbuf_tensor(f"ye{call}", [128, D], F32))
            ze_free = [0, 0]
            wg_free = [0, 0]
            wgf_free = (0, 0)
            wd_free = [0, 0]
            wdf_free = (0, 0)
            ye_free = 0

            r.wait("gpsimd", "VE", toks[("route", call)])
            r.wait("gpsimd", "GD", toks[("zspill", call)])

            def split(dst_hi, dst_lo, srcf):
                se = r.op("scalar", lambda EE, h=dst_hi, s=srcf: EE.copy(h, s), "SE")
                r.wait("vector", "SE", se)
                ve = r.op("vector", lambda EE, l=dst_lo, s=srcf, h=dst_hi:
                          EE.tensor_tensor(l, s, h, op=ALU.subtract), "VE")
                return se, ve

            def mm3(bank, width, whi, wlo, ahi, alo):
                r.wait("tensor", "VE", ps_reader[bank])
                n = len(whi)
                pe = None
                for i in range(n):
                    first = i == 0
                    t_mm(PS[bank][:, 0:width], whi[i], ahi[i], first, False)
                    t_mm(PS[bank][:, 0:width], whi[i], alo[i], False, False)
                    pe = t_mm(PS[bank][:, 0:width], wlo[i], ahi[i], False, i == n - 1,
                              "PE" if i == n - 1 else None)
                ps_writer[bank] = pe
                return pe

            for e in range(E):
                buf = e % 2
                if e > 0:
                    r.wait("vector", "PE", r.cnt["PE"])
                r.wait("gpsimd", "PE", ze_free[buf])
                r.op("gpsimd", lambda EE, b=buf, e=e: EE.dma_gather(
                    zehi[b][:], dzsphi, sidx_all[:, 16 * e:16 * (e + 1)],
                    num_idxs=CAP, num_idxs_reg=CAP, elem_size=D, transpose=True),
                    "GD", 16)
                gd = r.op("gpsimd", lambda EE, b=buf, e=e: EE.dma_gather(
                    zelo[b][:], dzsplo, sidx_all[:, 16 * e:16 * (e + 1)],
                    num_idxs=CAP, num_idxs_reg=CAP, elem_size=D, transpose=True),
                    "GD", 16)
                r.wait("tensor", "GD", gd)
                for it in range(ITL):
                    bg = 4 + 2 * (it % 2)
                    for gu in range(2):
                        jt = it + ITL * gu
                        wi = jt % 2
                        r.wait("sync", "SE", wgf_free[0])
                        r.wait("sync", "VE", wgf_free[1])
                        wtok = r.dma("sync", wgb[:], dwg[e, jt], "WD")
                        r.wait("scalar", "WD", wtok)
                        r.wait("scalar", "PE", wg_free[wi])
                        r.wait("vector", "PE", wg_free[wi])
                        se, ve = split(wgh[wi][:], wgl[wi][:], wgb[:])
                        wgf_free = (se, ve)
                        bank = bg + gu
                        r.wait("tensor", "SE", se)
                        r.wait("tensor", "VE", ve)
                        pe = mm3(bank, CAP,
                                 [wgh[wi][:, k, :] for k in range(DT)],
                                 [wgl[wi][:, k, :] for k in range(DT)],
                                 [zehi[buf][:, k, :] for k in range(DT)],
                                 [zelo[buf][:, k, :] for k in range(DT)])
                        wg_free[wi] = pe
                    r.wait("scalar", "PE", ps_writer[bg + 1])
                    r.wait("scalar", "VE", r.cnt["VE"])
                    se = s_act(hTt[:], PS[bg][:, 0:CAP], AF.Silu)
                    r.wait("vector", "SE", se)
                    ve = v_tt(hTt[:], hTt[:], PS[bg + 1][:, 0:CAP], ALU.mult)
                    ps_reader[bg] = ve
                    ps_reader[bg + 1] = ve
                    se, ve = split(hThi[:, it, :], hTlo[:, it, :], hTt[:])
                ze_free[buf] = r.cnt["PE"]
                ve_h = r.cnt["VE"]
                se_h = r.cnt["SE"]
                r.wait("tensor", "VE", ve_h)
                r.wait("tensor", "SE", se_h)
                for it in range(ITL):
                    wi = it % 2
                    r.wait("sync", "SE", wdf_free[0])
                    r.wait("sync", "VE", wdf_free[1])
                    wtok = r.dma("sync", wdb[:], dwd[e, 128 * it:128 * (it + 1), :], "WD")
                    r.wait("scalar", "WD", wtok)
                    r.wait("scalar", "PE", wd_free[wi])
                    r.wait("vector", "PE", wd_free[wi])
                    se, ve = split(wdh[wi][:], wdl[wi][:], wdb[:])
                    wdf_free = (se, ve)
                    r.wait("tensor", "SE", se)
                    r.wait("tensor", "VE", ve)
                    for tokt in range(2):
                        for dc in range(4):
                            bank = 4 * tokt + dc
                            if it == 0:
                                r.wait("tensor", "VE", ps_reader[bank])
                            t_mm(PS[bank][:, 0:512],
                                 hThi[:, it, 128 * tokt:128 * (tokt + 1)],
                                 wdh[wi][:, 512 * dc:512 * (dc + 1)],
                                 it == 0, False)
                            t_mm(PS[bank][:, 0:512],
                                 hThi[:, it, 128 * tokt:128 * (tokt + 1)],
                                 wdl[wi][:, 512 * dc:512 * (dc + 1)],
                                 False, False)
                            pe = t_mm(PS[bank][:, 0:512],
                                      hTlo[:, it, 128 * tokt:128 * (tokt + 1)],
                                      wdh[wi][:, 512 * dc:512 * (dc + 1)],
                                      False, it == ITL - 1,
                                      "PE" if (tokt == 1 and dc == 3) or it == ITL - 1 else None)
                            if it == ITL - 1:
                                ps_writer[bank] = pe
                    wd_free[wi] = pe
                for tokt in range(2):
                    r.wait("vector", "GD", ye_free)
                    for dc in range(4):
                        bank = 4 * tokt + dc
                        r.wait("vector", "PE", ps_writer[bank])
                        ve = v_ts(ye[:, 512 * dc:512 * (dc + 1)], PS[bank][:, 0:512],
                                  gtile[:, 2 * e + tokt:2 * e + tokt + 1], None, ALU.mult)
                        ps_reader[bank] = ve
                    r.wait("gpsimd", "VE", ve)
                    rowbase = CAP * e + 128 * tokt
                    ye_free = r.dma("gpsimd", dysort[rowbase:rowbase + 128, :], ye[:])
            toks[("experts", call)] = r.cnt["GD"]

        def emit_return(RA, EF, call):
            posw = RA["posw"]
            Y0 = EF(nc.sbuf_tensor(f"Y0{call}", [128, 2, D], F32))
            Y1 = EF(nc.sbuf_tensor(f"Y1{call}", [128, 2, D], F32))
            for cch in range(32):
                gd0 = r.dma("gpsimd", dysort[DUMPPOS:DUMPPOS + 128, 64 * cch:64 * (cch + 1)],
                            zero2k[:])
            r.wait("gpsimd", "GD", toks[("experts", call)])
            pe_last = 0
            for j in range(4):
                r.wait("gpsimd", "PE", pe_last)
                g0 = r.op("gpsimd", lambda EE, j=j: EE.dma_gather(
                    Y0[:], dysort, posw[:, 16 * j:16 * (j + 1)],
                    num_idxs=CAP, num_idxs_reg=CAP, elem_size=D, transpose=False),
                    "GD", 16)
                g1 = r.op("gpsimd", lambda EE, j=j: EE.dma_gather(
                    Y1[:], dysort, posw[:, 64 + 16 * j:64 + 16 * (j + 1)],
                    num_idxs=CAP, num_idxs_reg=CAP, elem_size=D, transpose=False),
                    "GD", 16)
                r.wait("vector", "GD", g1)
                ve = v_tt(Y0[:], Y0[:], Y1[:], ALU.add)
                r.wait("tensor", "VE", ve)
                for c in range(2):
                    for k in range(DT):
                        b = 6 + (k % 2)
                        r.wait("tensor", "VE", ps_reader[b])
                        pe = t_tp(PS[b][:, 0:128], Y0[:, c, 128 * k:128 * (k + 1)])
                        ps_writer[b] = pe
                        r.wait("vector", "PE", pe)
                        tsl = slice(256 * j + 128 * c, 256 * j + 128 * (c + 1))
                        ve = v_stt(rT[:, k, tsl], PS[b][:, 0:128], 0.5, rT[:, k, tsl],
                                   ALU.mult, ALU.add)
                        ps_reader[b] = ve
                pe_last = pe

        def emit_ffn(call):
            with ExitStack() as FX:
                EF = FX.enter_context
                RA = alloc_routing(EF, call) if stage >= 2 else None
                with ExitStack() as AX2:
                    EA = AX2.enter_context
                    zT = EA(nc.sbuf_tensor(f"zT{call}", [128, DT, TLOC], F32))
                    with ExitStack() as LX2:
                        EL = LX2.enter_context
                        meanB = EL(nc.sbuf_tensor(f"meanB{call}", [128, TLOC], F32))
                        rstdB = EL(nc.sbuf_tensor(f"rstdB{call}", [128, TLOC], F32))
                        sc = tuple(EL(nc.sbuf_tensor(f"lnrow{call}_{i}", [1, 512], F32))
                                   for i in range(2))
                        emit_ln(zT, meanB, rstdB, sc)
                        if debug and stage <= 1 and call == 0:
                            ve = r.cnt["VE"]
                            r.wait("gpsimd", "VE", ve)
                            r.dma("gpsimd", dbg["z1"], zT.ap().rearrange("p a b -> p (a b)"), "ST")
                            r.barrier_now()
                            r.flush()
                            return
                        if stage >= 2 and stage < 3:
                            emit_routing(zT, RA, call)
                        if stage >= 3:
                            emit_routing(zT, RA, call, part=1)
                        r.barrier_now()
                    if stage >= 3:
                        with ExitStack() as SPX:
                            emit_spill_z(zT, SPX.enter_context, call)
                            emit_routing(zT, RA, call, part=2)
                            r.barrier_now()
                        emit_shared(zT, EA, call)
                        emit_routing3(RA, call)
                    r.barrier_now()
                    r.flush()
                if stage >= 4:
                    with ExitStack() as BX2:
                        emit_experts(RA, BX2.enter_context, call)
                        r.barrier_now()
                        r.flush()
                    with ExitStack() as CX2:
                        emit_return(RA, CX2.enter_context, call)
                    r.barrier_now()
                    r.flush()

        def emit_mamba():
            with ExitStack() as MX:
                EM = MX.enter_context
                deltaT = EM(nc.sbuf_tensor("deltaT", [128, DT, TLOC], F32))
                z2h = EM(nc.sbuf_tensor("z2h", [128, DT, 512], F32))
                bcT = EM(nc.sbuf_tensor("bcT", [32, TLOC], F32))
                aneg = EM(nc.sbuf_tensor("aneg", [128, DT, DST], F32))
                dv_s = EM(nc.sbuf_tensor("dv_s", [128, DT], F32))
                sd = EM(nc.sbuf_tensor("sd", [128, DT], F32))
                selb = EM(nc.sbuf_tensor("selb", [32, 2, 128], F32))
                sel_i = EM(nc.sbuf_tensor("sel_i", [32, 128], I32))
                brep = EM(nc.sbuf_tensor("brep", [128, 512], F32))
                crep = EM(nc.sbuf_tensor("crep", [128, 512], F32))
                da = EM(nc.sbuf_tensor("da", [128, 512], F32))
                bx = EM(nc.sbuf_tensor("bx", [128, 512], F32))
                hsc = EM(nc.sbuf_tensor("hsc", [128, 512], F32))
                hend = EM(nc.sbuf_tensor("hend", [128, 256], F32))
                PPack = EM(nc.sbuf_tensor("PPack", [128, 256], F32))
                hinit = EM(nc.sbuf_tensor("hinit", [128, 256], F32))
                statev = EM(nc.sbuf_tensor("statev", [128, 256], F32))
                rankB = EM(nc.sbuf_tensor("rankB", [128, 1], F32))
                mcol = EM(nc.sbuf_tensor("mcol", [128, 1], F32))

                v_memset(sd[:], 0.0)
                v_memset(hinit[:], 0.0)
                v_memset(statev[:], 0.0)
                r.op("gpsimd", lambda EE: EE.iota(sel_i[:], pattern=[[0, 128]], base=0,
                                                  channel_multiplier=1), "GE")
                ge = r.cnt["GE"]
                r.wait("gpsimd", "IN", in_cid)
                gr = r.op("gpsimd", lambda EE: EE.partition_broadcast(rankB[:], cid_s[0:1, :]), "GE")

                def build_sel(s):
                    r.wait("vector", "GE", ge)
                    v_ts(selb[:, 0, :], sel_i[:], float(s), None, ALU.is_equal)
                    return v_ts(selb[:, 1, :], sel_i[:], float(16 + s), None, ALU.is_equal)

                with ExitStack() as WX:
                    EW = WX.enter_context
                    wdbcT_s = EW(nc.sbuf_tensor("wdbcT_s", [128, DT, 160], F32))
                    wdtT_b = EW(nc.sbuf_tensor("wdtT_b", [128, 128], F32))
                    bdlo_s = EW(nc.sbuf_tensor("bdlo_s", [128, 1], F32))
                    bbc_s = EW(nc.sbuf_tensor("bbc_s", [32, 1], F32))
                    bdt_s = EW(nc.sbuf_tensor("bdt_s", [128, DT], F32))
                    meanB = EW(nc.sbuf_tensor("meanB_m", [128, TLOC], F32))
                    rstdB = EW(nc.sbuf_tensor("rstdB_m", [128, TLOC], F32))
                    sc = tuple(EW(nc.sbuf_tensor(f"lnrow_m_{i}", [1, 512], F32))
                               for i in range(2))
                    dtmp = brep
                    dtmp2 = bx

                    l1 = r.dma("sync", wdbcT_s[:], dwdbcT.rearrange("(k p) c -> p k c", p=128))
                    l3 = r.dma("sync", bdlo_s[:], dbdlo)
                    l4 = r.dma("sync", bbc_s[:], dbbc)
                    l5 = r.dma("sync", bdt_s[:], dbdt)
                    l6 = r.dma("sync", aneg[:], dalog)
                    l7 = r.dma("sync", dv_s[:], ddvec)
                    r.wait("scalar", "IN", l7)
                    se = s_act(aneg[:], aneg[:], AF.Exp)
                    r.wait("vector", "SE", se)
                    v_ts(aneg[:], aneg[:], -1.0, None, ALU.mult)
                    r.wait("vector", "IN", l7)
                    emit_ln(None, meanB, rstdB, sc, make_z=False, sq=deltaT)

                    wdt_free = 0
                    for h in range(2):
                        S = slice(512 * h, 512 * (h + 1))
                        for k in range(DT):
                            v_tt(z2h[:, k, :], rT[:, k, S], meanB[:, S], ALU.subtract)
                            v_tt(z2h[:, k, :], z2h[:, k, :], rstdB[:, S], ALU.mult)
                        ve_z2 = r.cnt["VE"]
                        r.wait("tensor", "VE", ve_z2)
                        mm_group(0, 512, [wdbcT_s[:, k, 0:128] for k in range(DT)],
                                 [z2h[:, k, :] for k in range(DT)])
                        def _dlo(dst, ps):
                            return v_ts(dst, ps, bdlo_s[:], None, ALU.add)
                        evac(0, dtmp[:], 512, _dlo)
                        r.wait("tensor", "VE", ps_reader[1])
                        for k in range(DT):
                            pe = t_mm(PS[1][0:32, 0:512], wdbcT_s[:, k, 128:160],
                                      z2h[:, k, :], k == 0, k == DT - 1,
                                      "PE" if k == DT - 1 else None)
                        ps_writer[1] = pe
                        r.wait("vector", "PE", ps_writer[1])
                        ve = r.op("vector", lambda EE, S=S: EE.tensor_scalar(
                            bcT[:, S], PS[1][0:32, 0:512], bbc_s[:], None, op0=ALU.add), "VE")
                        ps_reader[1] = ve
                        ve_dlo = r.cnt["VE"]
                        r.wait("tensor", "VE", ve_dlo)
                        for k in range(DT):
                            bank = 2 + (k % 2)
                            r.wait("sync", "PE", wdt_free)
                            wtok = r.dma("sync", wdtT_b[:], dwdtT[:, 128 * k:128 * (k + 1)], "WD")
                            r.wait("tensor", "WD", wtok)
                            r.wait("tensor", "VE", ps_reader[bank])
                            pe = t_mm(PS[bank][:, 0:512], wdtT_b[:],
                                      dtmp[:], True, True, "PE")
                            wdt_free = pe
                            ps_writer[bank] = pe
                            r.wait("vector", "PE", pe)
                            ve = v_ts(dtmp2[:], PS[bank][:, 0:512], bdt_s[:, k:k + 1], None, ALU.add)
                            ps_reader[bank] = ve
                            v_ts(hsc[:], dtmp2[:], -1.0, None, ALU.mult)
                            ve = v_tt(hsc[:], dtmp2[:], hsc[:], ALU.min)
                            r.wait("scalar", "VE", ve)
                            se = s_act(da[:], hsc[:], AF.Exp)
                            se = s_act(da[:], da[:], AF.Ln, bias=onecol[:])
                            r.wait("vector", "SE", se)
                            v_ts(dtmp2[:], dtmp2[:], 0.0, None, ALU.max)
                            v_tt(deltaT[:, k, S], dtmp2[:], da[:], ALU.add)
                            v_stt(rT[:, k, S], z2h[:, k, :], dv_s[:, k:k + 1], rT[:, k, S],
                                  ALU.mult, ALU.add)
                            v_red(mcol[:], deltaT[:, k, S], ALU.add)
                            v_tt(sd[:, k:k + 1], sd[:, k:k + 1], mcol[:], ALU.add)
                        for k in range(DT):
                            v_tt(z2h[:, k, :], z2h[:, k, :], deltaT[:, k, S], ALU.mult)
                        ve_pre = r.cnt["VE"]
                        r.wait("tensor", "VE", ve_pre)
                        for s in range(DST):
                            ve_sel = build_sel(s)
                            r.wait("tensor", "VE", ve_sel)
                            for which, dst in ((0, brep), (1, crep)):
                                bank = 4 + which
                                r.wait("tensor", "VE", ps_reader[bank])
                                pe = t_mm(PS[bank][:, 0:512], selb[:, which, :], bcT[:, S],
                                          True, True, "PE")
                                ps_writer[bank] = pe
                                evac(bank, dst[:], 512)
                            for k in range(DT):
                                col = k * 16 + s
                                r.wait("scalar", "VE", r.cnt["VE"])
                                se = s_act(da[:], deltaT[:, k, S], AF.Exp,
                                           scale=aneg[:, k, s:s + 1])
                                r.wait("vector", "SE", se)
                                v_tt(bx[:], z2h[:, k, :], brep[:], ALU.mult)
                                if h == 0:
                                    v_scan(hsc[:], da[:], bx[:], 0.0)
                                else:
                                    v_scan(hsc[:], da[:], bx[:], hend[:, col:col + 1])
                                v_copy(hend[:, col:col + 1], hsc[:, 511:512])
                                v_tt(bx[:], hsc[:], crep[:], ALU.mult)
                                ve = v_tt(rT[:, k, S], rT[:, k, S], bx[:], ALU.add)
                            r.wait("scalar", "VE", r.cnt["VE"])
                    r.barrier_now()
                    if debug and stage == 5:
                        r.dma("gpsimd", dbg["z1"], deltaT.ap().rearrange("p a b -> p (a b)"), "ST")
                        r.dma("gpsimd", dbg["bc"], bcT[:], "ST")
                        r.dma("gpsimd", dbg["mam"][:, 0:256], hend[:], "ST")
                        r.barrier_now()
                    r.flush()

                with ExitStack() as CX:
                    EC = CX.enter_context
                    crepF = EC(nc.sbuf_tensor("crepF", [128, TLOC], F32))
                    ctmp = EC(nc.sbuf_tensor("ctmp", [128, TLOC], F32))
                    gath1 = EC(nc.sbuf_tensor("gath1", [128, 512], F32))

                    for k in range(DT):
                        r.wait("scalar", "VE", r.cnt["VE"])
                        se = s_act(PPack[:, 16 * k:16 * (k + 1)], aneg[:, k, :], AF.Exp,
                                   scale=sd[:, k:k + 1])
                    r.wait("gpsimd", "SE", se)
                    r.wait("gpsimd", "VE", r.cnt["VE"])
                    r.dma("gpsimd", dccin[:, 0:256], PPack[:])
                    gd = r.dma("gpsimd", dccin[:, 256:512], hend[:])
                    r.wait("gpsimd", "GD", gd)
                    cc = r.op("gpsimd", lambda EE: EE.collective_compute(
                        "AllGather", ALU.bypass, replica_groups=[list(range(NCORES))],
                        ins=[dccin.opt()], outs=[dccout.opt()]), "CC")
                    r.wait("gpsimd", "CC", cc)
                    r.wait("vector", "GE", gr)
                    r.wait("vector", "CC", cc)
                    for j in range(NCORES):
                        gd = r.dma("gpsimd", gath1[:], dccout[128 * j:128 * (j + 1), :])
                        r.wait("vector", "GD", gd)
                        v_ts(mcol[:], rankB[:], float(j), None, ALU.is_equal)
                        r.wait("vector", "VE", r.cnt["VE"])
                        v_stt(hinit[:], statev[:], mcol[:], hinit[:], ALU.mult, ALU.add)
                        v_tt(statev[:], gath1[:, 0:256], statev[:], ALU.mult)
                        ve = v_tt(statev[:], statev[:], gath1[:, 256:512], ALU.add)
                        r.wait("gpsimd", "VE", ve)
                    if debug and stage == 5:
                        r.wait("gpsimd", "VE", r.cnt["VE"])
                        r.dma("gpsimd", dbg["mam"][:, 256:512], PPack[:], "ST")
                        r.dma("gpsimd", dbg["mam"][:, 512:768], hinit[:], "ST")
                    v_memset(ctmp[:], 1.0)
                    for k in range(DT):
                        v_scan(deltaT[:, k, :], ctmp[:], deltaT[:, k, :], 0.0)
                    ve_cs = r.cnt["VE"]
                    r.wait("tensor", "VE", ve_cs)
                    for s in range(DST):
                        ve_sel = build_sel(s)
                        r.wait("tensor", "VE", ve_sel)
                        for half in range(2):
                            bank = 6 + half
                            r.wait("tensor", "VE", ps_reader[bank])
                            pe = t_mm(PS[bank][:, 0:512], selb[:, 1, :],
                                      bcT[:, 512 * half:512 * (half + 1)], True, True, "PE")
                            ps_writer[bank] = pe
                            evac(bank, crepF[:, 512 * half:512 * (half + 1)], 512)
                        for k in range(DT):
                            r.wait("scalar", "VE", r.cnt["VE"])
                            se = s_act(ctmp[:], deltaT[:, k, :], AF.Exp,
                                       scale=aneg[:, k, s:s + 1])
                            r.wait("vector", "SE", se)
                            v_stt(ctmp[:], ctmp[:], hinit[:, 16 * k + s:16 * k + s + 1],
                                  crepF[:], ALU.mult, ALU.mult)
                            ve = v_tt(rT[:, k, :], rT[:, k, :], ctmp[:], ALU.add)
                            r.wait("scalar", "VE", ve)
                    r.barrier_now()
                    r.flush()

        emit_ffn(0)
        if stage >= 5:
            emit_mamba()
        if stage >= 6:
            emit_ffn(1)

        with ExitStack() as OX:
            otm = OX.enter_context(nc.sbuf_tensor("otm", [128, D], F32))
            for a in range(8):
                for k in range(DT):
                    b = k % 4
                    r.wait("tensor", "VE", ps_reader[b])
                    pe = t_tp(PS[b][:, 0:128], rT[:, k, 128 * a:128 * (a + 1)])
                    ps_writer[b] = pe
                    ve = evac(b, otm[:, 128 * k:128 * (k + 1)], 128)
                r.wait("gpsimd", "VE", ve)
                st = r.dma("gpsimd", dout[128 * a:128 * (a + 1), :], otm[:], "ST")
                r.wait("vector", "ST", st)
            r.wait("gpsimd", "ST", r.cnt["ST"])
            r.flush()

    return nc



def _pack_lhsT(w):
    Dd, M = w.shape
    return np.ascontiguousarray(
        w.reshape(Dd // 128, 128, M // 128, 128).transpose(2, 1, 0, 3))


def prep_inputs(inp):
    x = np.asarray(inp["L0_attn_x"], dtype=np.float32).reshape(8192, 2048)
    W_dbc = np.asarray(inp["L0_attn_W_dbc"], np.float32)
    b_dbc = np.asarray(inp["L0_attn_b_dbc"], np.float32)
    W_dt = np.asarray(inp["L0_attn_W_dt"], np.float32)
    b_dt = np.asarray(inp["L0_attn_b_dt"], np.float32)
    A_log = np.asarray(inp["L0_attn_A_log"], np.float32)
    Dv = np.asarray(inp["L0_attn_D"], np.float32)
    gate_w = np.asarray(inp["L0_ffn_gate_w"], np.float32)
    w_gate_up = np.asarray(inp["L0_ffn_w_gate_up"], np.float32)
    w_down = np.asarray(inp["L0_ffn_w_down"], np.float32)
    sgu = np.asarray(inp["L0_ffn_shared_gate_up"], np.float32)
    sdn = np.asarray(inp["L0_ffn_shared_down"], np.float32)

    common = dict(
        wsg=_pack_lhsT(sgu),
        wsd=np.ascontiguousarray(sdn),
        wg=np.stack([_pack_lhsT(w_gate_up[e]) for e in range(E)]),
        wd=np.ascontiguousarray(w_down),
        gwT=np.ascontiguousarray(gate_w.T),
        wdbcT=np.ascontiguousarray(W_dbc.T),
        wdtT=np.ascontiguousarray(W_dt.T),
        bdlo=np.ascontiguousarray(b_dbc[:128].reshape(128, 1)),
        bbc=np.ascontiguousarray(b_dbc[128:].reshape(32, 1)),
        bdt=np.ascontiguousarray(b_dt.reshape(DT, 128).T),
        alog=np.ascontiguousarray(A_log.reshape(DT, 128, DST).transpose(1, 0, 2)),
        dvec=np.ascontiguousarray(Dv.reshape(DT, 128).T),
    )
    maps = []
    for c in range(NCORES):
        m = dict(common)
        m["x"] = np.ascontiguousarray(x[TLOC * c:TLOC * (c + 1)])
        m["cid"] = np.array([[float(c)]], np.float32)
        maps.append(m)
    return maps


def kernel(**inputs):
    maps = prep_inputs(inputs)
    nc = build_program(stage=6, debug=False)
    nc.compile()
    res = run_bass_kernel_spmd(nc, maps, core_ids=list(range(NCORES)))
    kernel.last_results = res
    out = np.concatenate([res.results[c]["out"] for c in range(NCORES)], axis=0)
    return out.reshape(1, 8192, 2048)
